# version 03
# baseline (speedup 1.0000x reference)
import numpy as np
import ml_dtypes
from contextlib import ExitStack
import concourse.bass as bass
import concourse.mybir as mybir
from concourse.bass_utils import run_bass_kernel_spmd

F32 = mybir.dt.float32
BF16 = mybir.dt.bfloat16
AF = mybir.ActivationFunctionType
ALU = mybir.AluOpType
AX = mybir.AxisListType.X
BF16NP = ml_dtypes.bfloat16

S, D, CK, NCH, FF, NH, HD, NE = 2048, 1024, 128, 16, 4096, 16, 64, 8
NDC = D // 128
LR = 0.01
ETA = 2.0 / D
LN_EPS = 1e-5
Q_EPS = 1e-5
NEG = -1e30

ENGINES = ("sync", "vector", "scalar", "tensor", "gpsimd")


class Prog:

    def __init__(self, nc, ctx):
        self.nc = nc
        self.ctx = ctx
        self.streams = {e: [] for e in ENGINES}
        self.sems = {}
        self.ctr = {}

    def sem(self, name):
        if name not in self.sems:
            self.sems[name] = self.ctx.enter_context(self.nc.semaphore(name))
            self.ctr[name] = 0
        return self.sems[name]

    def cnt(self, name):
        self.sem(name)
        return self.ctr[name]

    def op(self, engine, fn, inc=None, wait=None):
        sem_h = None
        if inc is not None:
            sem_h = self.sem(inc[0])
            self.ctr[inc[0]] += inc[1]
        waits = []
        if wait:
            for wn, wv in wait:
                waits.append((self.sem(wn), wv))

        def run(eng, fn=fn, sem_h=sem_h, inc=inc, waits=waits):
            for wsem, wval in waits:
                eng.wait_ge(wsem, wval)
            inst = fn(eng)
            if sem_h is not None:
                inst.then_inc(sem_h, inc[1])

        self.streams[engine].append(run)
        return self.ctr[inc[0]] if inc else None

    def wait(self, engine, semname, value):
        sem_h = self.sem(semname)
        self.streams[engine].append(lambda eng, s=sem_h, v=value: eng.wait_ge(s, v))

    def finalize(self):
        nc = self.nc
        with nc.Block() as block:
            @block.sync
            def _(e):
                for fn in self.streams["sync"]:
                    fn(e)

            @block.vector
            def _(e):
                for fn in self.streams["vector"]:
                    fn(e)

            @block.scalar
            def _(e):
                for fn in self.streams["scalar"]:
                    fn(e)

            @block.tensor
            def _(e):
                for fn in self.streams["tensor"]:
                    fn(e)

            @block.gpsimd
            def _(e):
                for fn in self.streams["gpsimd"]:
                    fn(e)


def host_consts():
    c = {}
    c["ident_bf"] = np.eye(128, dtype=BF16NP)
    c["ident_f"] = np.eye(128, dtype=np.float32)
    c["iota_col"] = np.arange(128, dtype=np.float32).reshape(128, 1)
    c["iota_row"] = np.arange(S, dtype=np.float32).reshape(1, S)
    c["triu_strict"] = np.triu(np.ones((128, 128), np.float32), 1)
    c["tril_strict"] = np.tril(np.ones((128, 128), np.float32), -1)
    c["triu_incl"] = np.triu(np.ones((128, 128), np.float32), 0)
    dm = np.zeros((7, 128, 128), np.float32)
    m = 1
    for lev in range(7):
        for blk in range(0, 128, 2 * m):
            dm[lev, blk:blk + m, blk + m:blk + 2 * m] = 1.0
        m *= 2
    c["dmask"] = dm
    c["dmaskT"] = np.ascontiguousarray(dm.transpose(0, 2, 1))
    c["madd_bf"] = np.where(np.arange(128)[:, None] > np.arange(128)[None, :],
                            np.float32(NEG), np.float32(0)).astype(BF16NP)
    c["revI"] = np.broadcast_to(np.arange(NE, 0, -1, dtype=np.float32), (128, NE)).copy()
    ratio = 2.0 ** (-8.0 / NH)
    slopes = np.array([ratio ** (i + 1) for i in range(NH)], np.float32)
    bc = np.zeros((128, NH * NCH), np.float32)
    for h in range(NH):
        for kt in range(NCH):
            bc[:, h * NCH + kt] = slopes[h] * (np.arange(128) + 128 * kt)
    c["biascols"] = bc
    return c



def emit_ln_to_fm(P, x_ap, xnfm_ap, ident_sb, tag, extra_load_wait=None):
    nc, ctx = P.nc, P.ctx
    xt = [ctx.enter_context(nc.sbuf_tensor(f"{tag}xt{i}", [128, D], F32)) for i in range(2)]
    sq = ctx.enter_context(nc.sbuf_tensor(f"{tag}sq", [128, D], F32))
    msum = ctx.enter_context(nc.sbuf_tensor(f"{tag}msum", [128, 1], F32))
    mean = ctx.enter_context(nc.sbuf_tensor(f"{tag}mean", [128, 1], F32))
    s2 = ctx.enter_context(nc.sbuf_tensor(f"{tag}s2", [128, 1], F32))
    var = ctx.enter_context(nc.sbuf_tensor(f"{tag}var", [128, 1], F32))
    rv = ctx.enter_context(nc.sbuf_tensor(f"{tag}rv", [128, 1], F32))
    rstd = ctx.enter_context(nc.sbuf_tensor(f"{tag}rstd", [128, 1], F32))
    negmr = ctx.enter_context(nc.sbuf_tensor(f"{tag}negmr", [128, 1], F32))
    xnb = [ctx.enter_context(nc.sbuf_tensor(f"{tag}xnb{i}", [128, D], BF16)) for i in range(2)]
    fm = ctx.enter_context(nc.sbuf_tensor(f"{tag}fm", [128, NDC * S], BF16))
    ptp = [ctx.enter_context(nc.psum_tensor(f"{tag}ptp{i}", [128, 128], BF16)) for i in range(2)]

    sLD, sV, sS, sT, sEV, sST = (f"{tag}ld", f"{tag}v", f"{tag}s", f"{tag}t", f"{tag}ev", f"{tag}st")

    for tc in range(NCH):
        p = tc % 2
        w = [(sV, 7 * (tc - 1)), (sS, 3 * (tc - 2) + 1)] if tc >= 2 else []
        if extra_load_wait is not None:
            w = list(w) + extra_load_wait(tc)
        P.op("sync", lambda e, d=xt[p], s_=x_ap[tc * 128:(tc + 1) * 128, :]:
             e.dma_start(out=d[:, :], in_=s_), inc=(sLD, 16), wait=w or None)
        w = [(sLD, 16 * (tc + 1))]
        if tc >= 1:
            w.append((sV, 7 * (tc - 1) + 2))
        P.op("scalar", lambda e, d=sq, s_=xt[p]: e.activation(d[:, :], s_[:, :], AF.Square),
             inc=(sS, 1), wait=w)
        b = 7 * tc
        P.op("vector", lambda e, s_=xt[p]: e.reduce_sum(msum[:, :], s_[:, :], AX),
             inc=(sV, 1), wait=[(sLD, 16 * (tc + 1))])
        P.op("vector", lambda e: e.reduce_sum(s2[:, :], sq[:, :], AX),
             inc=(sV, 1), wait=[(sS, 3 * tc + 1)])
        P.op("vector", lambda e: e.scalar_tensor_tensor(
            mean[:, :], msum[:, :], 1.0 / (D * D), msum[:, :], ALU.mult, ALU.mult),
            inc=(sV, 1), wait=[(sV, b + 1)])
        w4 = [(sV, b + 2)]
        if tc >= 1:
            w4.append((sS, 3 * tc - 1))
        P.op("vector", lambda e: e.tensor_scalar(
            var[:, :], s2[:, :], 1.0 / D, LN_EPS, ALU.mult, ALU.add),
            inc=(sV, 1), wait=w4)
        P.op("vector", lambda e: e.tensor_sub(var[:, :], var[:, :], mean[:, :]),
             inc=(sV, 1), wait=[(sV, b + 4)])
        P.op("scalar", lambda e: e.activation(rv[:, :], var[:, :], AF.Ln),
             inc=(sS, 1), wait=[(sV, b + 5)])
        P.op("scalar", lambda e: e.activation(rstd[:, :], rv[:, :], AF.Exp, scale=-0.5),
             inc=(sS, 1))
        P.op("vector", lambda e: e.scalar_tensor_tensor(
            negmr[:, :], msum[:, :], -1.0 / D, rstd[:, :], ALU.mult, ALU.mult),
            inc=(sV, 1), wait=[(sS, 3 * tc + 3)])
        wv = [(sV, b + 6)]
        if tc >= 2:
            wv.append((sT, 8 * (tc - 1)))
        P.op("vector", lambda e, s_=xt[p], d=xnb[p]: e.tensor_scalar(
            d[:, :], s_[:, :], rstd[:, 0:1], negmr[:, 0:1], ALU.mult, ALU.add),
            inc=(sV, 1), wait=wv)
        for dc in range(NDC):
            q = dc % 2
            wt = [(sV, 7 * (tc + 1))]
            if tc * 8 + dc >= 2:
                wt.append((sEV, 8 * tc + dc - 1))
            P.op("tensor", lambda e, q=q, src=xnb[p], dc=dc:
                 e.transpose(ptp[q][:, :], src[:, dc * 128:(dc + 1) * 128], ident_sb[:, :]),
                 inc=(sT, 1), wait=wt)
            P.op("vector", lambda e, q=q, dc=dc, tc=tc:
                 e.tensor_copy(fm[:, dc * S + tc * 128: dc * S + (tc + 1) * 128], ptp[q][:, :]),
                 inc=(sEV, 1), wait=[(sT, 8 * tc + dc + 1)])
    for dc in range(NDC):
        P.op("sync", lambda e, dc=dc: e.dma_start(
            out=xnfm_ap[dc * 128:(dc + 1) * 128, :], in_=fm[:, dc * S:(dc + 1) * S]),
            inc=(sST, 16), wait=[(sEV, NCH * NDC)])
    return fm, sST


def build_l1():
    nc = bass.Bass("TRN2", target_bir_lowering=False, debug=False)
    x = nc.dram_tensor("x", [S, D], F32, kind="ExternalInput").ap()
    identb = nc.dram_tensor("ident_bf", [128, 128], BF16, kind="ExternalInput").ap()
    xn_fm = nc.dram_tensor("xn_fm", [D, S], BF16, kind="ExternalOutput").ap()
    with ExitStack() as ctx:
        P = Prog(nc, ctx)
        ident_sb = ctx.enter_context(nc.sbuf_tensor("ident_sb", [128, 128], BF16))
        P.op("sync", lambda e: e.dma_start(out=ident_sb[:, :], in_=identb[:, :]), inc=("ild", 16))
        P.wait("tensor", "ild", 16)
        _, sST = emit_ln_to_fm(P, x, xn_fm, ident_sb, "ln0")
        P.wait("gpsimd", sST, 16 * NDC)
        P.finalize()
    return nc



def build_l2():
    nc = bass.Bass("TRN2", target_bir_lowering=False, debug=False)
    xnfm = nc.dram_tensor("xn_fm", [D, S], BF16, kind="ExternalInput").ap()
    thKT = nc.dram_tensor("thKT", [D, D], F32, kind="ExternalInput").ap()
    thQT = nc.dram_tensor("thQT", [D, D], F32, kind="ExternalInput").ap()
    thVT = nc.dram_tensor("thVT", [D, D], F32, kind="ExternalInput").ap()
    k_fm = nc.dram_tensor("k_fm", [D, S], BF16, kind="ExternalOutput").ap()
    q_fm = nc.dram_tensor("q_fm", [D, S], BF16, kind="ExternalOutput").ap()
    ve_tm = nc.dram_tensor("ve_tm", [S, D], BF16, kind="ExternalOutput").ap()
    with ExitStack() as ctx:
        P = Prog(nc, ctx)
        xn = ctx.enter_context(nc.sbuf_tensor("xn", [128, NDC * S], BF16))
        wK = ctx.enter_context(nc.sbuf_tensor("wK", [128, NDC * D], BF16))
        wQ = ctx.enter_context(nc.sbuf_tensor("wQ", [128, NDC * D], BF16))
        wV = ctx.enter_context(nc.sbuf_tensor("wV", [128, NDC * D], BF16))
        outK = ctx.enter_context(nc.sbuf_tensor("outK", [128, NDC * S], BF16))
        outQ = ctx.enter_context(nc.sbuf_tensor("outQ", [128, NDC * S], BF16))
        outV = ctx.enter_context(nc.sbuf_tensor("outV", [128, NCH * D], BF16))
        pk = ctx.enter_context(nc.psum_tensor("pk", [128, 512], F32))
        pq = ctx.enter_context(nc.psum_tensor("pq", [128, 512], F32))
        pv = ctx.enter_context(nc.psum_tensor("pv", [128, 1024], F32))
        emit_proj_ttt(P, xnfm, thKT, thQT, thVT, xn, wK, wQ, wV, outK, outQ, outV,
                      pk, pq, pv, "p2")
        nst = 0
        for dc in range(NDC):
            P.op("sync", lambda e, dc=dc: e.dma_start(
                out=k_fm[dc * 128:(dc + 1) * 128, :], in_=outK[:, dc * S:(dc + 1) * S]),
                inc=("p2st", 16), wait=[("p2evK", NDC * (S // 512))])
            P.op("sync", lambda e, dc=dc: e.dma_start(
                out=q_fm[dc * 128:(dc + 1) * 128, :], in_=outQ[:, dc * S:(dc + 1) * S]),
                inc=("p2st", 16), wait=[("p2evQ", NDC * (S // 512))])
            nst += 2
        for tc in range(NCH):
            P.op("sync", lambda e, tc=tc: e.dma_start(
                out=ve_tm[tc * 128:(tc + 1) * 128, :], in_=outV[:, tc * D:(tc + 1) * D]),
                inc=("p2st", 16), wait=[("p2evV", NCH)])
            nst += 1
        P.wait("gpsimd", "p2st", 16 * nst)
        P.finalize()
    return nc


def emit_proj_ttt(P, xnfm, thKT, thQT, thVT, xn, wK, wQ, wV, outK, outQ, outV,
                  pk, pq, pv, tag, vscale=ETA):
    nc = P.nc
    sLD, sT, sEK, sEQ, sEV = f"{tag}ld", f"{tag}t", f"{tag}evK", f"{tag}evQ", f"{tag}evV"
    nld = 0
    for dc in range(NDC):
        P.op("sync", lambda e, dc=dc: e.dma_start(
            out=xn[:, dc * S:(dc + 1) * S], in_=xnfm[dc * 128:(dc + 1) * 128, :]), inc=(sLD, 16))
        nld += 1
    for w_sb, w_ap in ((wK, thKT), (wQ, thQT), (wV, thVT)):
        for dc in range(NDC):
            P.op("gpsimd", lambda e, d=w_sb, dc=dc, a=w_ap: e.dma_start(
                out=d[:, dc * D:(dc + 1) * D], in_=a[dc * 128:(dc + 1) * 128, :]), inc=(sLD, 16))
            nld += 1
    P.wait("tensor", sLD, 16 * nld)

    for w_sb, pbank, out_sb, sev in ((wK, pk, outK, sEK), (wQ, pq, outQ, sEQ)):
        g = 0
        for ec in range(NDC):
            for tb in range(S // 512):
                if g > 0:
                    P.wait("tensor", sev, g)
                for dc in range(NDC):
                    P.op("tensor", lambda e, dc=dc, ec=ec, tb=tb, w=w_sb, pb=pbank:
                         e.matmul(pb[:, :], w[:, dc * D + ec * 128: dc * D + (ec + 1) * 128],
                                  xn[:, dc * S + tb * 512: dc * S + (tb + 1) * 512],
                                  start=(dc == 0), stop=(dc == NDC - 1)),
                         inc=(sT, 1) if dc == NDC - 1 else None)
                P.op("vector", lambda e, ec=ec, tb=tb, pb=pbank, o=out_sb:
                     e.tensor_copy(o[:, ec * S + tb * 512: ec * S + (tb + 1) * 512], pb[:, :]),
                     inc=(sev, 1), wait=[(sT, P.cnt(sT))])
                g += 1
    for tc in range(NCH):
        if tc > 0:
            P.wait("tensor", sEV, tc)
        for dc in range(NDC):
            for half in range(2):
                P.op("tensor", lambda e, dc=dc, tc=tc, half=half:
                     e.matmul(pv[:, half * 512:(half + 1) * 512],
                              xn[:, dc * S + tc * 128: dc * S + (tc + 1) * 128],
                              wV[:, dc * D + half * 512: dc * D + (half + 1) * 512],
                              start=(dc == 0), stop=(dc == NDC - 1)),
                     inc=(sT, 1) if (dc == NDC - 1 and half == 1) else None)
        P.op("vector", lambda e, tc=tc: e.tensor_scalar_mul(
            outV[:, tc * D:(tc + 1) * D], pv[:, :], vscale),
            inc=(sEV, 1), wait=[(sT, P.cnt(sT))])



def build_l3():
    nc = bass.Bass("TRN2", target_bir_lowering=False, debug=False)
    k_fm = nc.dram_tensor("k_fm", [D, S], BF16, kind="ExternalInput").ap()
    q_fm = nc.dram_tensor("q_fm", [D, S], BF16, kind="ExternalInput").ap()
    ve_tm = nc.dram_tensor("ve_tm", [S, D], BF16, kind="ExternalInput").ap()
    W0T = nc.dram_tensor("W0T", [D, D], F32, kind="ExternalInput").ap()
    b0row = nc.dram_tensor("b0row", [1, D], F32, kind="ExternalInput").ap()
    identf = nc.dram_tensor("ident_f", [128, 128], F32, kind="ExternalInput").ap()
    identb = nc.dram_tensor("ident_bf", [128, 128], BF16, kind="ExternalInput").ap()
    dmask = nc.dram_tensor("dmask", [7, 128, 128], F32, kind="ExternalInput").ap()
    dmaskT = nc.dram_tensor("dmaskT", [7, 128, 128], F32, kind="ExternalInput").ap()
    triu_incl = nc.dram_tensor("triu_incl", [128, 128], F32, kind="ExternalInput").ap()
    s_tm = nc.dram_tensor("s_tm", [S, D], F32, kind="ExternalOutput").ap()

    LE = LR * ETA
    with ExitStack() as ctx:
        P = Prog(nc, ctx)
        sb = lambda name, shape, dt: ctx.enter_context(nc.sbuf_tensor(name, shape, dt))
        ps = lambda name, shape, dt: ctx.enter_context(nc.psum_tensor(name, shape, dt))
        K = sb("K", [128, NDC * S], BF16)
        Q = sb("Q", [128, NDC * S], BF16)
        Ve = sb("Ve", [128, NCH * D], BF16)
        Wtf = sb("Wtf", [128, NDC * D], F32)
        Wtb = sb("Wtb", [128, NDC * D], BF16)
        bf_ = sb("bf_", [1, D], F32)
        bb_ = sb("bb_", [1, D], BF16)
        idf = sb("idf", [128, 128], F32)
        idb = sb("idb", [128, 128], BF16)
        dmu = sb("dmu", [128, 7 * 128], F32)
        dml = sb("dml", [128, 7 * 128], F32)
        triu = sb("triu", [128, 128], F32)
        ones1 = sb("ones1", [1, 128], BF16)
        onesc = sb("onesc", [128, 1], BF16)
        Bfull = sb("Bfull", [128, 128], F32)
        Ebf = sb("Ebf", [128, D], BF16)
        Gb = sb("Gb", [128, D], BF16)
        Pf = sb("Pf", [128, 128], F32)
        Pb = sb("Pb", [128, 128], BF16)
        Tf = sb("Tf", [128, 128], F32)
        Ttf = sb("Ttf", [128, 128], F32)
        Tb = sb("Tb", [128, 128], BF16)
        Ttb = sb("Ttb", [128, 128], BF16)
        Cub = sb("Cub", [128, 128], BF16)
        Clb = sb("Clb", [128, 128], BF16)
        Xb = sb("Xb", [128, 128], BF16)
        Zb = sb("Zb", [128, 128], BF16)
        Ktm = sb("Ktm", [128, D], BF16)
        s_sb = [sb(f"s_sb{i}", [128, D], F32) for i in range(2)]
        psBig = ps("psBig", [128, 1024], F32)
        psBig2 = ps("psBig2", [128, 1024], F32)
        psS = ps("psS", [128, 128], F32)
        psT1 = ps("psT1", [128, 128], F32)
        psT2 = ps("psT2", [128, 128], F32)
        psTP = ps("psTP", [128, 128], BF16)

        nld = 0
        for dc in range(NDC):
            P.op("sync", lambda e, dc=dc: e.dma_start(out=K[:, dc * S:(dc + 1) * S],
                 in_=k_fm[dc * 128:(dc + 1) * 128, :]), inc=("ld", 16)); nld += 1
            P.op("sync", lambda e, dc=dc: e.dma_start(out=Q[:, dc * S:(dc + 1) * S],
                 in_=q_fm[dc * 128:(dc + 1) * 128, :]), inc=("ld", 16)); nld += 1
            P.op("sync", lambda e, dc=dc: e.dma_start(out=Wtf[:, dc * D:(dc + 1) * D],
                 in_=W0T[dc * 128:(dc + 1) * 128, :]), inc=("ld", 16)); nld += 1
        for tc in range(NCH):
            P.op("sync", lambda e, tc=tc: e.dma_start(out=Ve[:, tc * D:(tc + 1) * D],
                 in_=ve_tm[tc * 128:(tc + 1) * 128, :]), inc=("ld", 16)); nld += 1
        P.op("sync", lambda e: e.dma_start(out=bf_[:, :], in_=b0row[:, :]), inc=("ld", 16)); nld += 1
        P.op("sync", lambda e: e.dma_start(out=idf[:, :], in_=identf[:, :]), inc=("ld", 16)); nld += 1
        P.op("sync", lambda e: e.dma_start(out=idb[:, :], in_=identb[:, :]), inc=("ld", 16)); nld += 1
        P.op("sync", lambda e: e.dma_start(out=triu[:, :], in_=triu_incl[:, :]), inc=("ld", 16)); nld += 1
        for lev in range(7):
            P.op("sync", lambda e, lev=lev: e.dma_start(out=dmu[:, lev * 128:(lev + 1) * 128],
                 in_=dmask[lev]), inc=("ld", 16)); nld += 1
            P.op("sync", lambda e, lev=lev: e.dma_start(out=dml[:, lev * 128:(lev + 1) * 128],
                 in_=dmaskT[lev]), inc=("ld", 16)); nld += 1
        LDALL = 16 * nld
        P.op("vector", lambda e: e.memset(ones1[:, :], 1.0), inc=("v0", 1))
        P.op("vector", lambda e: e.memset(onesc[:, :], 1.0), inc=("v0", 1))
        for dc in range(NDC):
            P.op("vector", lambda e, dc=dc: e.tensor_copy(
                Wtb[:, dc * D:(dc + 1) * D], Wtf[:, dc * D:(dc + 1) * D]),
                inc=("wb", 1), wait=[("ld", LDALL)])
        P.op("vector", lambda e: e.tensor_copy(bb_[:, :], bf_[:, :]),
             inc=("bb", 1), wait=[("ld", LDALL)])

        kt = lambda dc, n: K[:, dc * S + n * 128: dc * S + (n + 1) * 128]
        qt = lambda dc, n: Q[:, dc * S + n * 128: dc * S + (n + 1) * 128]

        for n in range(NCH):
            wE = [("wb", 8 * (n + 1)), ("bb", n + 1)]
            if n > 0:
                wE.append(("oev", n))
            for half in range(2):
                for dc in range(NDC):
                    P.op("tensor", lambda e, dc=dc, half=half, n=n: e.matmul(
                        psBig[:, half * 512:(half + 1) * 512], kt(dc, n),
                        Wtb[:, dc * D + half * 512: dc * D + (half + 1) * 512],
                        start=(dc == 0), stop=False),
                        wait=wE if (dc == 0 and half == 0) else None)
                P.op("tensor", lambda e, half=half: e.matmul(
                    psBig[:, half * 512:(half + 1) * 512], ones1[:, :],
                    bb_[0:1, half * 512:(half + 1) * 512], start=False, stop=True),
                    inc=("t", 1))
            TE = P.cnt("t")
            P.op("vector", lambda e, n=n: e.scalar_tensor_tensor(
                Ebf[:, :], psBig[:, :], ETA, Ve[:, n * D:(n + 1) * D],
                ALU.mult, ALU.subtract), inc=("eev", 1), wait=[("t", TE)])

            wS = [("cb", 14 * n)]
            if n > 0:
                wS.append(("pe", 2 * n))
            for dc in range(NDC):
                P.op("tensor", lambda e, dc=dc, n=n: e.matmul(
                    psS[:, :], kt(dc, n), kt(dc, n), start=(dc == 0), stop=(dc == NDC - 1)),
                    inc=("t", 1) if dc == NDC - 1 else None,
                    wait=wS if dc == 0 else None)
            TS = P.cnt("t")
            P.op("vector", lambda e: e.tensor_scalar(
                Bfull[:, :], psS[:, :], LE, LE, ALU.mult, ALU.add),
                inc=("bfv", 1), wait=[("t", TS)])

            wTi = [("ld", LDALL)]
            if n > 0:
                wTi.append(("t", TS))
            P.op("vector", lambda e: e.tensor_copy(Tf[:, :], idf[:, :]), inc=("ti", 1), wait=wTi)
            P.op("vector", lambda e: e.tensor_copy(Ttf[:, :], idf[:, :]), inc=("ti", 1))
            P.op("vector", lambda e: e.tensor_copy(Tb[:, :], idb[:, :]), inc=("ti", 1))
            P.op("vector", lambda e: e.tensor_copy(Ttb[:, :], idb[:, :]), inc=("ti", 1))
            for lev in range(7):
                P.op("vector", lambda e, lev=lev: e.tensor_mul(
                    Cub[:, :], Bfull[:, :], dmu[:, lev * 128:(lev + 1) * 128]),
                    inc=("cb", 1), wait=[("bfv", P.cnt("bfv"))])
                P.op("vector", lambda e, lev=lev: e.tensor_mul(
                    Clb[:, :], Bfull[:, :], dml[:, lev * 128:(lev + 1) * 128]),
                    inc=("cb", 1))
                CB = P.cnt("cb")
                wX = [("cb", CB), ("ti", 4 * (n + 1))]
                if lev > 0:
                    wX.append(("ts", 14 * n + 2 * lev))
                    wX.append(("tb", 14 * n + 2 * lev))
                P.op("tensor", lambda e: e.matmul(psT1[:, :], Clb[:, :], Tb[:, :],
                     start=True, stop=True), inc=("t", 1), wait=wX)
                P.op("tensor", lambda e: e.matmul(psT2[:, :], Cub[:, :], Ttb[:, :],
                     start=True, stop=True), inc=("t", 1))
                TXZ = P.cnt("t")
                P.op("vector", lambda e: e.tensor_copy(Xb[:, :], psT1[:, :]),
                     inc=("xb", 1), wait=[("t", TXZ)])
                P.op("vector", lambda e: e.tensor_copy(Zb[:, :], psT2[:, :]), inc=("xb", 1))
                XB = P.cnt("xb")
                P.op("tensor", lambda e: e.matmul(psT1[:, :], Ttb[:, :], Xb[:, :],
                     start=True, stop=True), inc=("t", 1), wait=[("xb", XB)])
                P.op("tensor", lambda e: e.matmul(psT2[:, :], Tb[:, :], Zb[:, :],
                     start=True, stop=True), inc=("t", 1))
                TY = P.cnt("t")
                P.op("vector", lambda e: e.tensor_sub(Tf[:, :], Tf[:, :], psT1[:, :]),
                     inc=("ts", 1), wait=[("t", TY)])
                P.op("vector", lambda e: e.tensor_sub(Ttf[:, :], Ttf[:, :], psT2[:, :]),
                     inc=("ts", 1))
                TSS = P.cnt("ts")
                P.op("vector", lambda e: e.tensor_copy(Tb[:, :], Tf[:, :]),
                     inc=("tb", 1), wait=[("ts", TSS)])
                P.op("vector", lambda e: e.tensor_copy(Ttb[:, :], Ttf[:, :]), inc=("tb", 1))

            wG = [("tb", 14 * (n + 1)), ("eev", n + 1)]
            if n > 0:
                wG.append(("wf", 8 * n))
                wG.append(("bu", 2 * n))
            for half in range(2):
                P.op("tensor", lambda e, half=half: e.matmul(
                    psBig2[:, half * 512:(half + 1) * 512], Tb[:, :],
                    Ebf[:, half * 512:(half + 1) * 512], start=True, stop=True),
                    inc=("t", 1), wait=wG if half == 0 else None)
            TG = P.cnt("t")
            P.op("vector", lambda e: e.tensor_copy(Gb[:, :], psBig2[:, :]),
                 inc=("gb", 1), wait=[("t", TG)])

            for dc in range(NDC):
                P.op("tensor", lambda e, dc=dc, n=n: e.matmul(
                    psS[:, :], kt(dc, n), qt(dc, n), start=(dc == 0), stop=(dc == NDC - 1)),
                    inc=("t", 1) if dc == NDC - 1 else None,
                    wait=[("bfv", P.cnt("bfv"))] if dc == 0 else None)
            TSQ = P.cnt("t")
            P.op("vector", lambda e: e.tensor_scalar(
                Pf[:, :], psS[:, :], -LR, -LR, ALU.mult, ALU.add),
                inc=("pe", 1), wait=[("t", TSQ)])
            P.op("vector", lambda e: e.tensor_mul(Pb[:, :], Pf[:, :], triu[:, :]),
                 inc=("pe", 1), wait=[("pe", P.cnt("pe") - 1)])

            wO = [("gb", n + 1), ("pe", 2 * (n + 1)), ("eev", n + 1)]
            for half in range(2):
                for dc in range(NDC):
                    P.op("tensor", lambda e, dc=dc, half=half, n=n: e.matmul(
                        psBig[:, half * 512:(half + 1) * 512], qt(dc, n),
                        Wtb[:, dc * D + half * 512: dc * D + (half + 1) * 512],
                        start=(dc == 0), stop=False),
                        wait=wO if (dc == 0 and half == 0) else None)
                P.op("tensor", lambda e, half=half: e.matmul(
                    psBig[:, half * 512:(half + 1) * 512], ones1[:, :],
                    bb_[0:1, half * 512:(half + 1) * 512], start=False, stop=False))
                P.op("tensor", lambda e, half=half: e.matmul(
                    psBig[:, half * 512:(half + 1) * 512], Pb[:, :],
                    Gb[:, half * 512:(half + 1) * 512], start=False, stop=True),
                    inc=("t", 1))
            TO = P.cnt("t")
            wOe = [("t", TO)]
            if n >= 2:
                wOe.append(("sst", 16 * (n - 1)))
            P.op("vector", lambda e, n=n: e.tensor_copy(s_sb[n % 2][:, :], psBig[:, :]),
                 inc=("oev", 1), wait=wOe)
            P.op("sync", lambda e, n=n: e.dma_start(
                out=s_tm[n * 128:(n + 1) * 128, :], in_=s_sb[n % 2][:, :]),
                inc=("sst", 16), wait=[("oev", n + 1)])

            for dc in range(NDC):
                wT = [("ktv", 8 * n + dc)]
                P.op("tensor", lambda e, dc=dc, n=n: e.transpose(
                    psTP[:, :], kt(dc, n), idb[:, :]), inc=("t", 1), wait=wT)
                P.op("vector", lambda e, dc=dc: e.tensor_copy(
                    Ktm[:, dc * 128:(dc + 1) * 128], psTP[:, :]),
                    inc=("ktv", 1), wait=[("t", P.cnt("t"))])

            for dc in range(NDC):
                wW = [("ktv", 8 * n + dc + 1), ("gb", n + 1)]
                if dc > 0:
                    wW.append(("wf", 8 * n + dc))
                for half in range(2):
                    P.op("tensor", lambda e, dc=dc, half=half: e.matmul(
                        psBig2[:, half * 512:(half + 1) * 512],
                        Ktm[:, dc * 128:(dc + 1) * 128],
                        Gb[:, half * 512:(half + 1) * 512], start=True, stop=True),
                        inc=("t", 1) if half == 1 else None,
                        wait=wW if half == 0 else None)
                TW = P.cnt("t")
                P.op("vector", lambda e, dc=dc: e.scalar_tensor_tensor(
                    Wtf[:, dc * D:(dc + 1) * D], psBig2[:, :], -LR,
                    Wtf[:, dc * D:(dc + 1) * D], ALU.mult, ALU.add),
                    inc=("wf", 1), wait=[("t", TW)])
                P.op("vector", lambda e, dc=dc: e.tensor_copy(
                    Wtb[:, dc * D:(dc + 1) * D], Wtf[:, dc * D:(dc + 1) * D]),
                    inc=("wb", 1), wait=[("wf", P.cnt("wf"))])

            for half in range(2):
                wB = [("wf", 8 * (n + 1))]
                if half == 1:
                    wB = [("bu", 2 * n + 1)]
                P.op("tensor", lambda e, half=half: e.matmul(
                    psBig2[0:1, half * 512:(half + 1) * 512], onesc[:, :],
                    Gb[:, half * 512:(half + 1) * 512], start=True, stop=True),
                    inc=("t", 1), wait=wB)
                P.op("vector", lambda e, half=half: e.scalar_tensor_tensor(
                    bf_[0:1, half * 512:(half + 1) * 512],
                    psBig2[0:1, half * 512:(half + 1) * 512], -LR,
                    bf_[0:1, half * 512:(half + 1) * 512], ALU.mult, ALU.add),
                    inc=("bu", 1), wait=[("t", P.cnt("t"))])
            P.op("vector", lambda e: e.tensor_copy(bb_[:, :], bf_[:, :]),
                 inc=("bb", 1), wait=[("bu", P.cnt("bu"))])

        P.wait("gpsimd", "sst", 16 * NCH)
        P.finalize()
    return nc



def emit_quantize(P, w_ap, wq_sb, pieces, pw, pcols, acc, part, wst, ones_cf,
                  ones_rf, thr_col, nthr_col, psG, psB, nelem, tag, prev=None):
    nc = P.nc
    sLD, sV, sQ = f"{tag}ld", f"{tag}v", f"{tag}q"
    for i, (r0, c0, _) in enumerate(pieces):
        p = i % 2
        w = [(sLD, 16 * i)]
        if i >= 2:
            w.append((sV, 2 * (i - 1)))
        elif prev is not None:
            w.append(prev)
        P.op("sync", lambda e, p=p, r0=r0, c0=c0: e.dma_start(
            out=wst[p][:, 0:pw], in_=w_ap[r0:r0 + 128, c0:c0 + pw]), inc=(sLD, 16), wait=w)
        P.op("vector", lambda e, p=p: e.reduce_sum(
            part[:, 0:1], wst[p][:, 0:pw], AX, apply_absolute_value=True),
            inc=(sV, 1), wait=[(sLD, 16 * (i + 1)), (sV, 2 * i)])
        P.op("vector", lambda e: e.tensor_add(acc[:, :], acc[:, :], part[:, 0:1]),
             inc=(sV, 1), wait=[(sV, 2 * i + 1)])
    NV = P.cnt(sV)
    P.op("tensor", lambda e: e.matmul(psG[:, :], acc[:, :], ones_cf[:, :],
         start=True, stop=True), inc=(f"{tag}t", 1), wait=[(sV, NV)])
    P.op("vector", lambda e: e.tensor_scalar(
        part[0:1, 0:1], psG[0:1, 0:1], 0.5 / nelem, 0.5 * Q_EPS, ALU.mult, ALU.add),
        inc=(sV, 1), wait=[(f"{tag}t", 1)])
    P.op("tensor", lambda e: e.matmul(psB[:, :], ones_rf[:, :], part[0:1, 0:1],
         start=True, stop=True), inc=(f"{tag}t", 1), wait=[(sV, NV + 1)])
    P.op("vector", lambda e: e.tensor_copy(thr_col[:, :], psB[:, :]),
         inc=(sV, 1), wait=[(f"{tag}t", 2)])
    P.op("vector", lambda e: e.tensor_scalar_mul(nthr_col[:, :], thr_col[:, :], -1.0),
         inc=(sV, 1), wait=[(sV, NV + 2)])
    NV2 = P.cnt(sV)
    for i, (r0, c0, off) in enumerate(pieces):
        p = i % 2
        w = [(sV, NV2 if i < 2 else NV2 + 2 * (i - 1))]
        P.op("sync", lambda e, p=p, r0=r0, c0=c0: e.dma_start(
            out=wst[p][:, 0:pw], in_=w_ap[r0:r0 + 128, c0:c0 + pw]),
            inc=(sLD, 16), wait=w)
        P.op("vector", lambda e, p=p: e.tensor_single_scalar(
            P.A[:, 0:pw], wst[p][:, 0:pw], nthr_col[:, 0:1], ALU.is_lt),
            inc=(sV, 1), wait=[(sLD, 16 * (len(pieces) + i + 1)), (sV, NV2)])
        P.op("vector", lambda e, p=p, off=off: e.scalar_tensor_tensor(
            wq_sb[:, off:off + pw], wst[p][:, 0:pw], thr_col[:, 0:1], P.A[:, 0:pw],
            ALU.is_gt, ALU.subtract),
            inc=(sV, 1), wait=[(sV, NV2 + 2 * i + 1)])
    return P.cnt(sV)


L4_DBG = False


def build_l4():
    nc = bass.Bass("TRN2", target_bir_lowering=False, debug=False)
    xnfm = nc.dram_tensor("xn_fm", [D, S], BF16, kind="ExternalInput").ap()
    W1T = nc.dram_tensor("W1T", [D, FF], F32, kind="ExternalInput").ap()
    W2T = nc.dram_tensor("W2T", [FF, D], F32, kind="ExternalInput").ap()
    b1r = nc.dram_tensor("b1r", [128, FF // 128], F32, kind="ExternalInput").ap()
    b2row = nc.dram_tensor("b2row", [1, D], F32, kind="ExternalInput").ap()
    g_tm = nc.dram_tensor("g_tm", [S, D], F32, kind="ExternalOutput").ap()
    if L4_DBG:
        thr_dbg = nc.dram_tensor("thr_dbg", [2, 128, 1], F32, kind="ExternalOutput").ap()
        w1q_dbg = nc.dram_tensor("w1q_dbg", [128, FF], F32, kind="ExternalOutput").ap()
        h_dbg = nc.dram_tensor("h_dbg", [128, NDC * 512], F32, kind="ExternalOutput").ap()
    NFC = FF // 128
    TB = 256
    NTB = S // TB
    with ExitStack() as ctx:
        P = Prog(nc, ctx)
        sb = lambda name, shape, dt: ctx.enter_context(nc.sbuf_tensor(name, shape, dt))
        ps = lambda name, shape, dt: ctx.enter_context(nc.psum_tensor(name, shape, dt))
        xn = sb("xn", [128, NDC * S], BF16)
        W1q = sb("W1q", [128, NDC * FF], BF16)
        W2q = sb("W2q", [128, NFC * D], BF16)
        h = sb("h", [128, NFC * TB], BF16)
        wst = [sb(f"wst{i}", [128, 2048], F32) for i in range(2)]
        A = sb("A", [128, 2048], BF16)
        P.A = A[:, :]
        acc1 = sb("acc1", [128, 1], F32)
        acc2 = sb("acc2", [128, 1], F32)
        part = sb("part", [128, 2], F32)
        ones_cf = sb("ones_cf", [128, 1], F32)
        ones_rf = sb("ones_rf", [1, 128], F32)
        ones1b = sb("ones1b", [1, 128], BF16)
        thr1 = sb("thr1", [128, 1], F32)
        nthr1 = sb("nthr1", [128, 1], F32)
        thr2 = sb("thr2", [128, 1], F32)
        nthr2 = sb("nthr2", [128, 1], F32)
        b1c = sb("b1c", [128, NFC], F32)
        b2b = sb("b2b", [1, D], BF16)
        g_sb = [sb(f"g_sb{i}", [128, D], F32) for i in range(2)]
        psG = ps("psG", [1, 1], F32)
        psB = ps("psB", [128, 1], F32)
        psH = [ps(f"psH{i}", [128, TB], F32) for i in range(2)]
        psO = [ps(f"psO{i}", [128, 512], F32) for i in range(2)]

        P.op("vector", lambda e: e.memset(acc1[:, :], 0.0), inc=("init", 1))
        P.op("vector", lambda e: e.memset(acc2[:, :], 0.0), inc=("init", 1))
        P.op("vector", lambda e: e.memset(ones_cf[:, :], 1.0), inc=("init", 1))
        P.op("vector", lambda e: e.memset(ones_rf[:, :], 1.0), inc=("init", 1))
        P.op("vector", lambda e: e.memset(ones1b[:, :], 1.0), inc=("init", 1))
        for dc in range(NDC):
            P.op("sync", lambda e, dc=dc: e.dma_start(out=xn[:, dc * S:(dc + 1) * S],
                 in_=xnfm[dc * 128:(dc + 1) * 128, :]), inc=("xld", 16))
        P.op("sync", lambda e: e.dma_start(out=b1c[:, :], in_=b1r[:, :]), inc=("xld", 16))
        P.op("gpsimd", lambda e: e.dma_start(out=b2b[:, :], in_=b2row[:, :]), inc=("xld", 16))
        P.wait("vector", "init", 5)

        pieces1 = [(dc * 128, hf * 2048, dc * FF + hf * 2048)
                   for dc in range(NDC) for hf in range(2)]
        q1_done = emit_quantize(P, W1T, W1q, pieces1, 2048, FF, acc1, part, wst,
                                ones_cf, ones_rf, thr1, nthr1, psG, psB, D * FF, "q1")
        pieces2 = [(fc * 128, 0, fc * D) for fc in range(NFC)]
        q2_done = emit_quantize(P, W2T, W2q, pieces2, 1024, D, acc2, part, wst,
                                ones_cf, ones_rf, thr2, nthr2, psG, psB, FF * D, "q2",
                                prev=("q1v", q1_done))

        if L4_DBG:
            P.op("sync", lambda e: e.dma_start(out=thr_dbg[0], in_=thr1[:, :]),
                 inc=("dbg", 16), wait=[("q1v", q1_done), ("q2v", q2_done)])
            P.op("sync", lambda e: e.dma_start(out=thr_dbg[1], in_=thr2[:, :]), inc=("dbg", 16))
            P.op("gpsimd", lambda e: e.dma_start(out=w1q_dbg[:, :], in_=W1q[:, 0:FF]), inc=("dbg", 16))
            P.wait("gpsimd", "dbg", 48)
        XLD = 16 * (NDC + 2)
        for tb in range(NTB):
            for fc in range(NFC):
                pq = fc % 2
                gidx = tb * NFC + fc
                if gidx == 0:
                    w = [("q1v", q1_done), ("xld", XLD)]
                else:
                    w = [("hev", gidx - 1)]
                for dc in range(NDC):
                    P.op("tensor", lambda e, fc=fc, dc=dc, tb=tb, pq=pq: e.matmul(
                        psH[pq][:, :], W1q[:, dc * FF + fc * 128: dc * FF + (fc + 1) * 128],
                        xn[:, dc * S + tb * TB: dc * S + (tb + 1) * TB],
                        start=(dc == 0), stop=(dc == NDC - 1)),
                        inc=("t", 1) if dc == NDC - 1 else None,
                        wait=w if dc == 0 else None)
                P.op("scalar", lambda e, fc=fc, pq=pq: e.activation(
                    h[:, fc * TB:(fc + 1) * TB], psH[pq][:, :], AF.Relu,
                    bias=b1c[:, fc:fc + 1]),
                    inc=("hev", 1), wait=[("t", P.cnt("t"))])
            HEV = P.cnt("hev")
            if L4_DBG and tb == 0:
                for dcq in range(NDC):
                    P.op("gpsimd", lambda e, dcq=dcq: e.dma_start(
                        out=h_dbg[:, dcq * 512:(dcq + 1) * 512],
                        in_=h[:, dcq * TB: dcq * TB + 512]), inc=("dbg2", 16),
                        wait=[("hev", HEV)])
                P.wait("gpsimd", "dbg2", 16 * NDC)
            for ts_ in range(TB // 128):
                t128 = tb * (TB // 128) + ts_
                po = t128 % 2
                for half in range(2):
                    w2 = [("hev", HEV)]
                    if t128 == 0 and half == 0:
                        w2.append(("q2v", q2_done))
                    if half == 1:
                        w2 = [("gevh", 2 * t128 + 1)]
                    elif t128 >= 2:
                        w2.append(("gevh", 2 * (t128 - 1)))
                    for fc in range(NFC):
                        P.op("tensor", lambda e, fc=fc, half=half, ts_=ts_, po=po: e.matmul(
                            psO[po][:, :],
                            h[:, fc * TB + ts_ * 128: fc * TB + ts_ * 128 + 128],
                            W2q[:, fc * D + half * 512: fc * D + (half + 1) * 512],
                            start=(fc == 0), stop=False),
                            wait=w2 if fc == 0 else None)
                    P.op("tensor", lambda e, half=half, po=po: e.matmul(
                        psO[po][:, :], ones1b[:, :],
                        b2b[0:1, half * 512:(half + 1) * 512], start=False, stop=True),
                        inc=("t", 1))
                    TO2 = P.cnt("t")
                    wg = [("t", TO2)]
                    if t128 >= 2 and half == 0:
                        wg.append(("gst", 16 * (t128 - 1)))
                    P.op("scalar", lambda e, half=half, po=po: e.activation(
                        g_sb[po][:, half * 512:(half + 1) * 512], psO[po][:, :], AF.Silu),
                        inc=("gevh", 1), wait=wg)
                P.op("sync", lambda e, t128=t128, po=po: e.dma_start(
                    out=g_tm[t128 * 128:(t128 + 1) * 128, :], in_=g_sb[po][:, :]),
                    inc=("gst", 16), wait=[("gevh", P.cnt("gevh"))])
        P.wait("gpsimd", "gst", 16 * NCH)
        P.finalize()
    return nc



def build_l5():
    nc = bass.Bass("TRN2", target_bir_lowering=False, debug=False)
    x = nc.dram_tensor("x", [S, D], F32, kind="ExternalInput").ap()
    s_tm = nc.dram_tensor("s_tm", [S, D], F32, kind="ExternalInput").ap()
    g_tm = nc.dram_tensor("g_tm", [S, D], F32, kind="ExternalInput").ap()
    identb = nc.dram_tensor("ident_bf", [128, 128], BF16, kind="ExternalInput").ap()
    xp_tm = nc.dram_tensor("xp_tm", [S, D], F32, kind="ExternalOutput").ap()
    xn1_fm = nc.dram_tensor("xn1_fm", [D, S], BF16, kind="ExternalOutput").ap()
    with ExitStack() as ctx:
        P = Prog(nc, ctx)
        sbf = lambda name, shape, dt: ctx.enter_context(nc.sbuf_tensor(name, shape, dt))
        ident_sb = sbf("ident_sb", [128, 128], BF16)
        xb = [sbf(f"xb{i}", [128, D], F32) for i in range(2)]
        sb_ = [sbf(f"sb_{i}", [128, D], F32) for i in range(2)]
        gb = [sbf(f"gb{i}", [128, D], F32) for i in range(2)]
        xp = [sbf(f"xp{i}", [128, D], F32) for i in range(2)]
        P.op("sync", lambda e: e.dma_start(out=ident_sb[:, :], in_=identb[:, :]), inc=("ild", 16))
        P.wait("tensor", "ild", 16)
        for tc in range(NCH):
            p = tc % 2
            w = [("cmb", 2 * (tc - 1))] if tc >= 2 else None
            P.op("sync", lambda e, p=p, tc=tc: e.dma_start(
                out=xb[p][:, :], in_=x[tc * 128:(tc + 1) * 128, :]), inc=("cld", 16), wait=w)
            P.op("sync", lambda e, p=p, tc=tc: e.dma_start(
                out=sb_[p][:, :], in_=s_tm[tc * 128:(tc + 1) * 128, :]), inc=("cld", 16))
            P.op("sync", lambda e, p=p, tc=tc: e.dma_start(
                out=gb[p][:, :], in_=g_tm[tc * 128:(tc + 1) * 128, :]), inc=("cld", 16))
            w = [("cld", 48 * (tc + 1))]
            if tc >= 2:
                w.append(("cst", 16 * (tc - 1)))
            P.op("vector", lambda e, p=p: e.tensor_mul(sb_[p][:, :], sb_[p][:, :], gb[p][:, :]),
                 inc=("cmb", 1), wait=w)
            P.op("vector", lambda e, p=p: e.tensor_add(xp[p][:, :], xb[p][:, :], sb_[p][:, :]),
                 inc=("cmb", 1), wait=[("cmb", 2 * tc + 1)])
            P.op("sync", lambda e, p=p, tc=tc: e.dma_start(
                out=xp_tm[tc * 128:(tc + 1) * 128, :], in_=xp[p][:, :]),
                inc=("cst", 16), wait=[("cmb", 2 * tc + 2)])
        _, sST = emit_ln_to_fm(P, xp_tm, xn1_fm, ident_sb, "ln1",
                               extra_load_wait=lambda tc: [("cst", 16 * (tc + 1))])
        P.wait("gpsimd", sST, 16 * NDC)
        P.wait("gpsimd", "cst", 16 * NCH)
        P.finalize()
    return nc



def build_l6():
    nc = bass.Bass("TRN2", target_bir_lowering=False, debug=False)
    xnfm = nc.dram_tensor("xn1_fm", [D, S], BF16, kind="ExternalInput").ap()
    qw = nc.dram_tensor("qw", [D, D], F32, kind="ExternalInput").ap()
    kw = nc.dram_tensor("kw", [D, D], F32, kind="ExternalInput").ap()
    vw = nc.dram_tensor("vw", [D, D], F32, kind="ExternalInput").ap()
    k_fm = nc.dram_tensor("k1_fm", [D, S], BF16, kind="ExternalOutput").ap()
    q_fm = nc.dram_tensor("q1_fm", [D, S], BF16, kind="ExternalOutput").ap()
    v_tm = nc.dram_tensor("v1_tm", [S, D], BF16, kind="ExternalOutput").ap()
    with ExitStack() as ctx:
        P = Prog(nc, ctx)
        sbf = lambda name, shape, dt: ctx.enter_context(nc.sbuf_tensor(name, shape, dt))
        xn = sbf("xn", [128, NDC * S], BF16)
        wK = sbf("wK", [128, NDC * D], BF16)
        wQ = sbf("wQ", [128, NDC * D], BF16)
        wV = sbf("wV", [128, NDC * D], BF16)
        outK = sbf("outK", [128, NDC * S], BF16)
        outQ = sbf("outQ", [128, NDC * S], BF16)
        outV = sbf("outV", [128, NCH * D], BF16)
        pk = ctx.enter_context(nc.psum_tensor("pk", [128, 512], F32))
        pq = ctx.enter_context(nc.psum_tensor("pq", [128, 512], F32))
        pv = ctx.enter_context(nc.psum_tensor("pv", [128, 1024], F32))
        emit_proj_ttt(P, xnfm, kw, qw, vw, xn, wK, wQ, wV, outK, outQ, outV,
                      pk, pq, pv, "p6", vscale=1.0)
        nst = 0
        for dc in range(NDC):
            P.op("sync", lambda e, dc=dc: e.dma_start(
                out=k_fm[dc * 128:(dc + 1) * 128, :], in_=outK[:, dc * S:(dc + 1) * S]),
                inc=("p6st", 16), wait=[("p6evK", NDC * (S // 512))])
            P.op("sync", lambda e, dc=dc: e.dma_start(
                out=q_fm[dc * 128:(dc + 1) * 128, :], in_=outQ[:, dc * S:(dc + 1) * S]),
                inc=("p6st", 16), wait=[("p6evQ", NDC * (S // 512))])
            nst += 2
        for tc in range(NCH):
            P.op("sync", lambda e, tc=tc: e.dma_start(
                out=v_tm[tc * 128:(tc + 1) * 128, :], in_=outV[:, tc * D:(tc + 1) * D]),
                inc=("p6st", 16), wait=[("p6evV", NCH)])
            nst += 1
        P.wait("gpsimd", "p6st", 16 * nst)
        P.finalize()
    return nc



DBG_CROW = False
DBG_ATTN = False


def build_l7(nh_a=NH, nh_b=NH, do_c=True, nqb_b=4, bparts=7):
    nc = bass.Bass("TRN2", target_bir_lowering=False, debug=False)
    q_fm = nc.dram_tensor("q1_fm", [D, S], BF16, kind="ExternalInput").ap()
    k_fm = nc.dram_tensor("k1_fm", [D, S], BF16, kind="ExternalInput").ap()
    v_tm = nc.dram_tensor("v1_tm", [S, D], BF16, kind="ExternalInput").ap()
    ow = nc.dram_tensor("ow", [D, D], F32, kind="ExternalInput").ap()
    iota_row = nc.dram_tensor("iota_row", [1, S], F32, kind="ExternalInput").ap()
    biascols = nc.dram_tensor("biascols", [128, NH * NCH], F32, kind="ExternalInput").ap()
    madd = nc.dram_tensor("madd_bf", [128, 128], BF16, kind="ExternalInput").ap()
    identb = nc.dram_tensor("ident_bf", [128, 128], BF16, kind="ExternalInput").ap()
    crow_dram = nc.dram_tensor("crow_scr", [NH, S], F32, kind="Internal" if not DBG_CROW else "ExternalOutput").ap()
    s1_tm = nc.dram_tensor("s1_tm", [S, D], F32, kind="ExternalOutput").ap()
    attn_dbg = nc.dram_tensor("attn_dbg", [D, S], BF16, kind="ExternalOutput").ap() if DBG_ATTN else None

    ratio = 2.0 ** (-8.0 / NH)
    slopes = [ratio ** (i + 1) for i in range(NH)]
    NQB = S // 512
    with ExitStack() as ctx:
        P = Prog(nc, ctx)
        sbf = lambda name, shape, dt: ctx.enter_context(nc.sbuf_tensor(name, shape, dt))
        psf = lambda name, shape, dt: ctx.enter_context(nc.psum_tensor(name, shape, dt))
        ka = [sbf(f"ka{i}", [64, S], BF16) for i in range(2)]
        qa = [sbf(f"qa{i}", [64, S], BF16) for i in range(2)]
        kaug = [sbf(f"kaug{i}", [65, S], BF16) for i in range(2)]
        qaug = [sbf(f"qaug{i}", [65, S], BF16) for i in range(2)]
        vt = sbf("vt", [128, NCH * D], BF16)
        owb = sbf("owb", [128, NDC * D], BF16)
        attn = sbf("attn", [128, NDC * S], BF16)
        iot = sbf("iot", [1, S], F32)
        bcols = sbf("bcols", [128, NH * NCH], F32)
        madd_sb = sbf("madd_sb", [128, 128], BF16)
        idb = sbf("idb", [128, 128], BF16)
        Q2 = sbf("Q2", [64, S], BF16)
        onesc64 = sbf("onesc64", [64, 1], BF16)
        onescol = sbf("onescol", [128, 1], BF16)
        ones64f = sbf("ones64f", [1, 64], F32)
        mx = sbf("mx", [1, 4], F32)
        qm2 = sbf("qm2", [1, 1], F32)
        km2 = sbf("km2", [1, 1], F32)
        c8 = sbf("c8", [1, 1], F32)
        crow = sbf("crow", [1, S], F32)
        lrow = sbf("lrow", [1, 512], F32)
        rlrow = sbf("rlrow", [1, 512], F32)
        pt = [sbf(f"pt{i}", [128, 512], BF16) for i in range(2)]
        Osb = sbf("Osb", [64, 512], BF16)
        s1sb = [sbf(f"s1sb{i}", [128, D], F32) for i in range(2)]
        psST = [psf(f"psST{i}", [128, 512], F32) for i in range(2)]
        psO = psf("psO", [64, 512], F32)
        psL = psf("psL", [1, 512], F32)
        psRL = psf("psRL", [64, 512], F32)

        nld = 0
        for tc in range(NCH):
            P.op("sync", lambda e, tc=tc: e.dma_start(
                out=vt[:, tc * D:(tc + 1) * D], in_=v_tm[tc * 128:(tc + 1) * 128, :]),
                inc=("ld", 16)); nld += 1
        for dc in range(NDC):
            P.op("gpsimd", lambda e, dc=dc: e.dma_start(
                out=owb[:, dc * D:(dc + 1) * D], in_=ow[dc * 128:(dc + 1) * 128, :]),
                inc=("ld", 16)); nld += 1
        for name, dst, src in (("iot", iot, iota_row), ("bcols", bcols, biascols),
                               ("madd", madd_sb, madd), ("idb", idb, identb)):
            P.op("sync", lambda e, d=dst, s_=src: e.dma_start(out=d[:, :], in_=s_[:, :]),
                 inc=("ld", 16)); nld += 1
        LD = 16 * nld
        P.op("vector", lambda e: e.memset(kaug[0][64:65, :], 1.0), inc=("ms", 1))
        P.op("vector", lambda e: e.memset(kaug[1][64:65, :], 1.0), inc=("ms", 1))
        P.op("vector", lambda e: e.memset(onesc64[:, :], 1.0), inc=("ms", 1))
        P.op("vector", lambda e: e.memset(onescol[:, :], 1.0), inc=("ms", 1))
        P.op("vector", lambda e: e.memset(ones64f[:, :], 1.0), inc=("ms", 1))
        P.wait("vector", "ld", LD)

        ntA = []
        for h in range(nh_a):
            p = h % 2
            wl = [("nt", ntA[h - 2])] if h >= 2 else None
            P.op("sync", lambda e, h=h, p=p: e.dma_start(
                out=qa[p][:, :], in_=q_fm[h * 64:(h + 1) * 64, :]), inc=("lda", 16), wait=wl)
            P.op("sync", lambda e, h=h, p=p: e.dma_start(
                out=ka[p][:, :], in_=k_fm[h * 64:(h + 1) * 64, :]), inc=("lda", 16))
            for side, src_sb, dstm in ((0, qa[p], qm2), (1, ka[p], km2)):
                NT0 = P.cnt("nt")
                P.op("scalar", lambda e, s_=src_sb: e.activation(
                    Q2[:, :], s_[:, :], AF.Square),
                    inc=("ns", 1), wait=[("nt", NT0), ("lda", 32 * (h + 1)), ("ld", LD)])
                for blk in range(4):
                    wmm = [("ns", P.cnt("ns")), ("ms", 5), ("nv", P.cnt("nv"))]
                    P.op("tensor", lambda e, blk=blk: e.matmul(
                        psL[:, :], onesc64[:, :], Q2[:, blk * 512:(blk + 1) * 512],
                        start=True, stop=True), inc=("nt", 1), wait=wmm)
                    P.op("vector", lambda e, blk=blk: e.reduce_max(
                        mx[0:1, blk:blk + 1], psL[:, :], AX),
                        inc=("nv", 1), wait=[("nt", P.cnt("nt"))])
                P.op("vector", lambda e, d=dstm: e.reduce_max(d[:, :], mx[0:1, 0:4], AX),
                     inc=("nv", 1), wait=[("nv", P.cnt("nv"))])
            ntA.append(P.cnt("nt"))
            P.op("vector", lambda e: e.tensor_mul(c8[:, :], qm2[:, :], km2[:, :]),
                 inc=("nv", 1), wait=[("nv", P.cnt("nv"))])
            P.op("scalar", lambda e: e.activation(c8[:, :], c8[:, :], AF.Sqrt),
                 inc=("ns", 1), wait=[("nv", P.cnt("nv"))])
            wc = [("ns", P.cnt("ns"))]
            if h >= 1:
                wc.append(("crst", 16 * h))
            P.op("vector", lambda e, h=h: e.tensor_scalar(
                crow[:, :], iot[:, :], -8.0 * slopes[h], c8[0:1, 0:1],
                ALU.mult, ALU.subtract),
                inc=("nv", 1), wait=wc)
            P.op("sync", lambda e, h=h: e.dma_start(
                out=crow_dram[h:h + 1, :], in_=crow[:, :]),
                inc=("crst", 16), wait=[("nv", P.cnt("nv"))])

        stmH = {}
        tile = 0
        for h in range(nh_b):
            p = h % 2
            wl = [("stm", stmH[h - 2])] if h >= 2 else None
            P.op("sync", lambda e, h=h, p=p: e.dma_start(
                out=qaug[p][0:64, :], in_=q_fm[h * 64:(h + 1) * 64, :]),
                inc=("ldb", 16), wait=wl)
            wcr = [("crst", 16 * (h + 1))]
            if h >= 2:
                wcr.append(("stm", stmH[h - 2]))
            P.op("gpsimd", lambda e, h=h, p=p: e.dma_start(
                out=qaug[p][64:65, :], in_=crow_dram[h:h + 1, :]),
                inc=("ldb", 16), wait=wcr)
            P.op("sync", lambda e, h=h, p=p: e.dma_start(
                out=kaug[p][0:64, :], in_=k_fm[h * 64:(h + 1) * 64, :]),
                inc=("ldb", 16))
            LDB = P.cnt("ldb")
            for qb in range(nqb_b):
                nkt = 4 * qb + 4
                for idx in range(nkt):
                    kt = idx
                    diag = kt >= 4 * qb
                    off = 128 * (kt - 4 * qb) if diag else 0
                    w = 512 - off
                    pq = tile % 2
                    ptb = tile % 2
                    wst_ = [("ldb", LDB), ("ms", 5)]
                    if tile == 0:
                        wst_ += [("ld", LD), ("lda", P.cnt("lda")), ("crst", P.cnt("crst")),
                                 ("nv", P.cnt("nv")), ("ns", P.cnt("ns")), ("nt", P.cnt("nt"))]
                    if tile >= 2:
                        wst_.append(("expv", tile - 1))
                    P.op("tensor", lambda e, h=h, qb=qb, kt=kt, off=off, w=w, pq=pq, p=p:
                         e.matmul(psST[pq][:, 0:w],
                                  kaug[p][:, kt * 128:(kt + 1) * 128],
                                  qaug[p][:, qb * 512 + off:(qb + 1) * 512],
                                  start=True, stop=not diag),
                         inc=("stm", 1) if not diag else None, wait=wst_)
                    if diag:
                        if bparts & 1:
                            P.op("tensor", lambda e, pq=pq: e.matmul(
                                psST[pq][:, 0:128], idb[:, :], madd_sb[:, :],
                                start=False, stop=True), inc=("stm", 1))
                        else:
                            P.op("tensor", lambda e, pq=pq: e.matmul(
                                psST[pq][:, 0:128], idb[:, :], idb[:, :],
                                start=False, stop=True), inc=("stm", 1))
                    wexp = [("stm", P.cnt("stm"))]
                    if tile >= 2:
                        wexp.append(("ol", tile - 1))
                    P.op("scalar", lambda e, h=h, kt=kt, w=w, pq=pq, ptb=ptb:
                         e.activation(pt[ptb][:, 0:w], psST[pq][:, 0:w], AF.Exp,
                                      bias=bcols[:, h * NCH + kt: h * NCH + kt + 1],
                                      scale=0.125),
                         inc=("expv", 1), wait=wexp)
                    wo = [("expv", P.cnt("expv"))]
                    if idx == 0:
                        wo.append(("osb", P.cnt("osb")))
                        wo.append(("lns", P.cnt("lns")))
                    if bparts & 2:
                        P.op("tensor", lambda e, h=h, kt=kt, off=off, w=w, ptb=ptb, idx=idx,
                             nkt=nkt: e.matmul(
                                 psO[0:64, off:512], vt[:, kt * D + h * 64: kt * D + h * 64 + 64],
                                 pt[ptb][:, 0:w], start=(idx == 0), stop=(idx == nkt - 1)),
                             wait=wo)
                        P.op("tensor", lambda e, off=off, w=w, ptb=ptb, idx=idx, nkt=nkt:
                             e.matmul(psL[0:1, off:512], onescol[:, :], pt[ptb][:, 0:w],
                                      start=(idx == 0), stop=(idx == nkt - 1)),
                             inc=("ol", 1))
                    else:
                        P.op("vector", lambda e, ptb=ptb: e.tensor_copy(Osb[:, 0:4], pt[ptb][:, 0:4][0:64, :]) if False else e.engine_nop(), inc=("ol", 1), wait=wo)
                    tile += 1
                if not (bparts & 4):
                    OL = P.cnt("ol")
                    P.op("vector", lambda e: e.engine_nop(), inc=("atn", 1), wait=[("ol", OL)])
                    P.op("vector", lambda e: e.engine_nop(), inc=("osb", 1))
                    P.op("vector", lambda e: e.engine_nop(), inc=("lns", 1))
                    P.op("vector", lambda e: e.engine_nop(), inc=("lns", 1))
                    continue
                OL = P.cnt("ol")
                P.op("scalar", lambda e: e.activation(lrow[:, :], psL[0:1, :], AF.Ln),
                     inc=("lns", 1), wait=[("ol", OL)])
                P.op("scalar", lambda e: e.activation(rlrow[:, :], lrow[:, :], AF.Exp,
                     scale=-1.0), inc=("lns", 1))
                wb_ = [("lns", P.cnt("lns")), ("atn", P.cnt("atn"))]
                P.op("tensor", lambda e: e.matmul(psRL[:, :], ones64f[:, :], rlrow[:, :],
                     start=True, stop=True), inc=("ntb", 1), wait=wb_)
                P.op("vector", lambda e: e.tensor_copy(Osb[:, :], psO[0:64, :]),
                     inc=("osb", 1), wait=[("ol", OL)])
                p0 = 64 * (h % 2)
                fo = (h // 2) * S + qb * 512
                if bparts & 8:
                    P.op("vector", lambda e, p0=p0, fo=fo: e.tensor_copy(
                        attn[p0:p0 + 64, fo:fo + 512], Osb[:, :]),
                        inc=("atn", 1), wait=[("osb", P.cnt("osb")), ("ntb", P.cnt("ntb"))])
                else:
                    P.op("vector", lambda e, p0=p0, fo=fo: e.tensor_mul(
                        attn[p0:p0 + 64, fo:fo + 512], Osb[:, :], psRL[:, :]),
                        inc=("atn", 1), wait=[("osb", P.cnt("osb")), ("ntb", P.cnt("ntb"))])
            stmH[h] = P.cnt("stm")

        if DBG_ATTN:
            for dc in range(NDC):
                P.op("sync", lambda e, dc=dc: e.dma_start(
                    out=attn_dbg[dc * 128:(dc + 1) * 128, :], in_=attn[:, dc * S:(dc + 1) * S]),
                    inc=("adb", 16), wait=[("atn", P.cnt("atn"))])
            P.wait("gpsimd", "adb", 16 * NDC)
        ATN = P.cnt("atn")
        EXPALL = P.cnt("expv")
        for t128 in range(NCH if do_c else 0):
            po = t128 % 2
            for half in range(2):
                wc = [("atn", ATN), ("expv", EXPALL)]
                if t128 >= 2:
                    wc.append(("s1e", 2 * (t128 - 1)))
                for dc in range(NDC):
                    P.op("tensor", lambda e, dc=dc, t128=t128, half=half: e.matmul(
                        psST[half][:, :], attn[:, dc * S + t128 * 128: dc * S + (t128 + 1) * 128],
                        owb[:, dc * D + half * 512: dc * D + (half + 1) * 512],
                        start=(dc == 0), stop=(dc == NDC - 1)),
                        inc=("ct", 1) if dc == NDC - 1 else None,
                        wait=wc if dc == 0 else None)
                we = [("ct", P.cnt("ct"))]
                if t128 >= 2 and half == 0:
                    we.append(("s1st", 16 * (t128 - 1)))
                P.op("vector", lambda e, po=po, half=half: e.tensor_copy(
                    s1sb[po][:, half * 512:(half + 1) * 512], psST[half][:, :]),
                    inc=("s1e", 1), wait=we)
            P.op("sync", lambda e, t128=t128, po=po: e.dma_start(
                out=s1_tm[t128 * 128:(t128 + 1) * 128, :], in_=s1sb[po][:, :]),
                inc=("s1st", 16), wait=[("s1e", P.cnt("s1e"))])
        if do_c:
            P.wait("gpsimd", "s1st", 16 * NCH)
        elif nh_b > 0:
            P.wait("gpsimd", "atn", P.cnt("atn"))
        else:
            P.wait("gpsimd", "crst", 16 * nh_a)
        P.finalize()
    return nc



L8_SERIAL = False


def build_l8(nexp=NE):
    nc = bass.Bass("TRN2", target_bir_lowering=False, debug=False)
    xnfm = nc.dram_tensor("xn1_fm", [D, S], BF16, kind="ExternalInput").ap()
    WgT = nc.dram_tensor("WgT", [D, NE], F32, kind="ExternalInput").ap()
    W1 = nc.dram_tensor("W1", [NE, D, FF], F32, kind="ExternalInput").ap()
    b1r = nc.dram_tensor("b1r", [NE, 128, FF // 128], F32, kind="ExternalInput").ap()
    W2 = nc.dram_tensor("W2", [NE, FF, D], F32, kind="ExternalInput").ap()
    b2 = nc.dram_tensor("b2", [NE, D], F32, kind="ExternalInput").ap()
    revI = nc.dram_tensor("revI", [128, NE], F32, kind="ExternalInput").ap()
    identb = nc.dram_tensor("ident_bf", [128, 128], BF16, kind="ExternalInput").ap()
    y_tm = nc.dram_tensor("y_tm", [S, D], F32, kind="ExternalOutput").ap()
    IH = FF // 2
    NIC = IH // 128
    with ExitStack() as ctx:
        P = Prog(nc, ctx)
        sbf = lambda name, shape, dt: ctx.enter_context(nc.sbuf_tensor(name, shape, dt))
        psf = lambda name, shape, dt: ctx.enter_context(nc.psum_tensor(name, shape, dt))
        xn = sbf("xn", [128, NDC * S], BF16)
        y = sbf("y", [128, NCH * D], F32)
        W1h = sbf("W1h", [128, NDC * IH], BF16)
        W2h = sbf("W2h", [128, NIC * D], BF16)
        hsb = sbf("hsb", [128, NIC * 512], BF16)
        wg = sbf("wg", [128, NDC * NE], BF16)
        b2b = sbf("b2b", [NE, D], BF16)
        b1all = sbf("b1all", [128, NE * (FF // 128)], F32)
        revI_sb = sbf("revI_sb", [128, NE], F32)
        idb = sbf("idb", [128, 128], BF16)
        gates = sbf("gates", [128, NCH * NE], F32)
        gatesb = sbf("gatesb", [128, NE], BF16)
        gatesT = sbf("gatesT", [NE, S], BF16)
        lg = sbf("lg", [128, NE], F32)
        m1 = sbf("m1", [128, 1], F32)
        eq = sbf("eq", [128, NE], F32)
        s1_ = sbf("s1_", [128, NE], F32)
        rm = sbf("rm", [128, 1], F32)
        is1 = sbf("is1", [128, NE], F32)
        is2 = sbf("is2", [128, NE], F32)
        msk = sbf("msk", [128, NE], F32)
        m2 = sbf("m2", [128, 1], F32)
        tt = sbf("tt", [128, 1], F32)
        den = sbf("den", [128, 1], F32)
        w1c = sbf("w1c", [128, 1], F32)
        w2c = sbf("w2c", [128, 1], F32)
        psGt = psf("psGt", [128, NE], F32)
        psTr = psf("psTr", [NE, 128], BF16)
        psH = [psf(f"psH{i}", [128, 512], F32) for i in range(2)]
        psO2 = [psf(f"psO2{i}", [128, 1024], F32) for i in range(2)]

        nld = 0
        for dc in range(NDC):
            P.op("sync", lambda e, dc=dc: e.dma_start(out=xn[:, dc * S:(dc + 1) * S],
                 in_=xnfm[dc * 128:(dc + 1) * 128, :]), inc=("ld", 16)); nld += 1
            P.op("gpsimd", lambda e, dc=dc: e.dma_start(out=wg[:, dc * NE:(dc + 1) * NE],
                 in_=WgT[dc * 128:(dc + 1) * 128, :]), inc=("ld", 16)); nld += 1
        P.op("gpsimd", lambda e: e.dma_start(out=b2b[:, :], in_=b2[:, :]), inc=("ld", 16)); nld += 1
        for e_ in range(NE):
            P.op("sync", lambda e, e_=e_: e.dma_start(
                out=b1all[:, e_ * 32:(e_ + 1) * 32], in_=b1r[e_]), inc=("ld", 16)); nld += 1
        P.op("sync", lambda e: e.dma_start(out=revI_sb[:, :], in_=revI[:, :]), inc=("ld", 16)); nld += 1
        P.op("sync", lambda e: e.dma_start(out=idb[:, :], in_=identb[:, :]), inc=("ld", 16)); nld += 1
        LD = 16 * nld

        for tc in range(NCH):
            wg0 = [("ld", LD)]
            if tc > 0:
                wg0.append(("gv", P.cnt("gv")))
            for dc in range(NDC):
                P.op("tensor", lambda e, dc=dc, tc=tc: e.matmul(
                    psGt[:, :], xn[:, dc * S + tc * 128: dc * S + (tc + 1) * 128],
                    wg[:, dc * NE:(dc + 1) * NE], start=(dc == 0), stop=(dc == NDC - 1)),
                    inc=("gt", 1) if dc == NDC - 1 else None, wait=wg0 if dc == 0 else None)
            def chain(fn, eng="vector"):
                P.op(eng, fn, inc=("gv", 1),
                     wait=[("gv", P.cnt("gv")), ("gs", P.cnt("gs")), ("gt", P.cnt("gt"))])
            chain(lambda e: e.tensor_copy(lg[:, :], psGt[:, :]))
            chain(lambda e: e.reduce_max(m1[:, :], lg[:, :], AX))
            chain(lambda e: e.tensor_single_scalar(eq[:, :], lg[:, :], m1[:, 0:1], ALU.is_ge))
            chain(lambda e: e.tensor_mul(s1_[:, :], eq[:, :], revI_sb[:, :]))
            chain(lambda e: e.reduce_max(rm[:, :], s1_[:, :], AX))
            chain(lambda e: e.tensor_single_scalar(is1[:, :], s1_[:, :], rm[:, 0:1], ALU.is_ge))
            chain(lambda e: e.scalar_tensor_tensor(msk[:, :], is1[:, :], NEG, lg[:, :],
                                                   ALU.mult, ALU.add))
            chain(lambda e: e.reduce_max(m2[:, :], msk[:, :], AX))
            chain(lambda e: e.tensor_single_scalar(eq[:, :], msk[:, :], m2[:, 0:1], ALU.is_ge))
            chain(lambda e: e.tensor_mul(s1_[:, :], eq[:, :], revI_sb[:, :]))
            chain(lambda e: e.reduce_max(rm[:, :], s1_[:, :], AX))
            chain(lambda e: e.tensor_single_scalar(is2[:, :], s1_[:, :], rm[:, 0:1], ALU.is_ge))
            chain(lambda e: e.tensor_sub(m2[:, :], m2[:, :], m1[:, :]))
            chain(lambda e: e.activation(tt[:, :], m2[:, :], AF.Exp), eng="scalar")
            chain(lambda e: e.tensor_scalar_add(den[:, :], tt[:, :], 1.0))
            chain(lambda e: e.activation(den[:, :], den[:, :], AF.Ln), eng="scalar")
            chain(lambda e: e.activation(w1c[:, :], den[:, :], AF.Exp, scale=-1.0), eng="scalar")
            chain(lambda e: e.tensor_mul(w2c[:, :], tt[:, :], w1c[:, :]))
            chain(lambda e, tc=tc: e.tensor_scalar(
                gates[:, tc * NE:(tc + 1) * NE], is1[:, :], w1c[:, 0:1], None, ALU.mult))
            chain(lambda e, tc=tc: e.scalar_tensor_tensor(
                gates[:, tc * NE:(tc + 1) * NE], is2[:, :], w2c[:, 0:1],
                gates[:, tc * NE:(tc + 1) * NE], ALU.mult, ALU.add))
            chain(lambda e, tc=tc: e.tensor_copy(gatesb[:, :], gates[:, tc * NE:(tc + 1) * NE]))
            P.op("tensor", lambda e, tc=tc: e.transpose(psTr[:, :], gatesb[:, :], idb[:, :]),
                 inc=("gt", 1), wait=[("gv", P.cnt("gv")), ("gtr", P.cnt("gtr"))])
            P.op("vector", lambda e, tc=tc: e.tensor_copy(
                gatesT[:, tc * 128:(tc + 1) * 128], psTr[0:NE, 0:128]),
                inc=("gtr", 1), wait=[("gt", P.cnt("gt"))])
        GTR = P.cnt("gtr")
        GV = P.cnt("gv")

        for tc in range(NCH):
            wy = [("gtr", GTR)]
            if tc >= 2:
                wy.append(("yi", tc - 1))
            for half in range(2):
                P.op("tensor", lambda e, tc=tc, half=half: e.matmul(
                    psO2[tc % 2][:, half * 512:(half + 1) * 512],
                    gatesT[:, tc * 128:(tc + 1) * 128],
                    b2b[:, half * 512:(half + 1) * 512], start=True, stop=True),
                    inc=("gt", 1) if half == 1 else None, wait=wy if half == 0 else None)
            P.op("vector", lambda e, tc=tc: e.tensor_copy(
                y[:, tc * D:(tc + 1) * D], psO2[tc % 2][:, :]),
                inc=("yi", 1), wait=[("gt", P.cnt("gt"))])
        YI = P.cnt("yi")

        rnd = 0
        for e_ in range(nexp):
            for ih in range(2):
                wld = []
                if rnd > 0:
                    wld = [("t8", P.cnt("t8"))]
                for dc in range(NDC):
                    P.op("gpsimd", lambda e, e_=e_, ih=ih, dc=dc: e.dma_start(
                        out=W1h[:, dc * IH:(dc + 1) * IH],
                        in_=W1[e_, dc * 128:(dc + 1) * 128, ih * IH:(ih + 1) * IH]),
                        inc=("wld", 16), wait=wld if dc == 0 else None)
                for ic in range(NIC):
                    P.op("gpsimd", lambda e, e_=e_, ih=ih, ic=ic: e.dma_start(
                        out=W2h[:, ic * D:(ic + 1) * D],
                        in_=W2[e_, ih * IH + ic * 128: ih * IH + (ic + 1) * 128, :]),
                        inc=("wld", 16))
                WLD = P.cnt("wld")
                for tb in range(4):
                    if L8_SERIAL:
                        allw = [(s, P.cnt(s)) for s in ("t8", "hev", "yst", "yi", "wld")]
                        for s, v in allw:
                            P.wait("tensor", s, v); P.wait("vector", s, v); P.wait("scalar", s, v)
                    for ic in range(NIC):
                        pq = ic % 2
                        gidx = rnd * 64 + tb * NIC + ic
                        w1_ = [("wld", WLD), ("ld", LD)]
                        if P.cnt("hev") >= 1:
                            w1_.append(("hev", P.cnt("hev") - 1))
                        for dc in range(NDC):
                            P.op("tensor", lambda e, dc=dc, ic=ic, tb=tb, pq=pq: e.matmul(
                                psH[pq][:, :],
                                W1h[:, dc * IH + ic * 128: dc * IH + (ic + 1) * 128],
                                xn[:, dc * S + tb * 512: dc * S + (tb + 1) * 512],
                                start=(dc == 0), stop=(dc == NDC - 1)),
                                inc=("t8", 1) if dc == NDC - 1 else None,
                                wait=w1_ if dc == 0 else None)
                        P.op("scalar", lambda e, ic=ic, pq=pq, e_=e_, ih=ih: e.activation(
                            hsb[:, ic * 512:(ic + 1) * 512], psH[pq][:, :], AF.Gelu,
                            bias=b1all[:, e_ * 32 + ih * 16 + ic: e_ * 32 + ih * 16 + ic + 1]),
                            inc=("hev", 1), wait=[("t8", P.cnt("t8"))])
                    HEV = P.cnt("hev")
                    for ts_ in range(4):
                        if L8_SERIAL:
                            for s in ("t8", "hev", "yst", "yi"):
                                v = P.cnt(s)
                                P.wait("tensor", s, v); P.wait("vector", s, v)
                        t128 = tb * 4 + ts_
                        po = t128 % 2
                        w2_ = [("hev", HEV), ("yi", YI)]
                        if P.cnt("yst") >= 1:
                            w2_.append(("yst", P.cnt("yst") - 1))
                        for half in range(2):
                            for ic in range(NIC):
                                P.op("tensor", lambda e, ic=ic, half=half, ts_=ts_, po=po,
                                     tb=tb: e.matmul(
                                        psO2[po][:, half * 512:(half + 1) * 512],
                                        hsb[:, ic * 512 + ts_ * 128: ic * 512 + ts_ * 128 + 128],
                                        W2h[:, ic * D + half * 512: ic * D + (half + 1) * 512],
                                        start=(ic == 0), stop=(ic == NIC - 1)),
                                    inc=("t8", 1) if ic == NIC - 1 else None,
                                    wait=w2_ if (half == 0 and ic == 0) else None)
                        prev_same = (rnd - 1) * 16 + (tb * 4 + ts_) + 1
                        wy2 = [("t8", P.cnt("t8")), ("yi", YI)]
                        if rnd > 0:
                            wy2.append(("yst", prev_same))
                        P.op("vector", lambda e, t128=t128, po=po, e_=e_: e.scalar_tensor_tensor(
                            y[:, t128 * D:(t128 + 1) * D], psO2[po][:, :],
                            gates[:, t128 * NE + e_: t128 * NE + e_ + 1],
                            y[:, t128 * D:(t128 + 1) * D], ALU.mult, ALU.add),
                            inc=("yst", 1), wait=wy2)
                rnd += 1
        YST = P.cnt("yst") if nexp > 0 else 0
        for tc in range(NCH):
            P.op("sync", lambda e, tc=tc: e.dma_start(
                out=y_tm[tc * 128:(tc + 1) * 128, :], in_=y[:, tc * D:(tc + 1) * D]),
                inc=("yout", 16), wait=[("yst", YST), ("yi", YI)])
        P.wait("gpsimd", "yout", 16 * NCH)
        P.finalize()
    return nc



def build_l9():
    nc = bass.Bass("TRN2", target_bir_lowering=False, debug=False)
    xp = nc.dram_tensor("xp_tm", [S, D], F32, kind="ExternalInput").ap()
    s1 = nc.dram_tensor("s1_tm", [S, D], F32, kind="ExternalInput").ap()
    yy = nc.dram_tensor("y_tm", [S, D], F32, kind="ExternalInput").ap()
    out = nc.dram_tensor("out_tm", [S, D], F32, kind="ExternalOutput").ap()
    with ExitStack() as ctx:
        P = Prog(nc, ctx)
        sbf = lambda name, shape, dt: ctx.enter_context(nc.sbuf_tensor(name, shape, dt))
        xb = [sbf(f"xb{i}", [128, D], F32) for i in range(2)]
        sb_ = [sbf(f"sb_{i}", [128, D], F32) for i in range(2)]
        yb = [sbf(f"yb{i}", [128, D], F32) for i in range(2)]
        ob = [sbf(f"ob{i}", [128, D], F32) for i in range(2)]
        for tc in range(NCH):
            p = tc % 2
            w = [("cmb", 2 * (tc - 1)), ("gy", tc - 1)] if tc >= 2 else None
            P.op("sync", lambda e, p=p, tc=tc: e.dma_start(
                out=xb[p][:, :], in_=xp[tc * 128:(tc + 1) * 128, :]), inc=("cld", 16), wait=w)
            P.op("sync", lambda e, p=p, tc=tc: e.dma_start(
                out=sb_[p][:, :], in_=s1[tc * 128:(tc + 1) * 128, :]), inc=("cld", 16))
            P.op("sync", lambda e, p=p, tc=tc: e.dma_start(
                out=yb[p][:, :], in_=yy[tc * 128:(tc + 1) * 128, :]), inc=("cld", 16))
            P.op("scalar", lambda e, p=p: e.activation(yb[p][:, :], yb[p][:, :], AF.Silu),
                 inc=("gy", 1), wait=[("cld", 48 * (tc + 1))])
            P.op("vector", lambda e, p=p: e.tensor_mul(sb_[p][:, :], sb_[p][:, :], yb[p][:, :]),
                 inc=("cmb", 1), wait=[("gy", P.cnt("gy"))])
            wadd = [("cmb", 2 * tc + 1)]
            if tc >= 2:
                wadd.append(("cst", 16 * (tc - 1)))
            P.op("vector", lambda e, p=p: e.tensor_add(ob[p][:, :], xb[p][:, :], sb_[p][:, :]),
                 inc=("cmb", 1), wait=wadd)
            P.op("sync", lambda e, p=p, tc=tc: e.dma_start(
                out=out[tc * 128:(tc + 1) * 128, :], in_=ob[p][:, :]),
                inc=("cst", 16), wait=[("cmb", 2 * tc + 2)])
        P.wait("gpsimd", "cst", 16 * NCH)
        P.finalize()
    return nc



def _run(nc, in_maps):
    res = run_bass_kernel_spmd(nc, in_maps, core_ids=list(range(8)))
    return res.results


def _prep(inp):
    w = {}
    w["x"] = [np.ascontiguousarray(np.asarray(inp["L0_attn_in_seq"][i], dtype=np.float32))
              for i in range(8)]
    w["thKT"] = np.ascontiguousarray(np.asarray(inp["L0_attn_theta_K"]).T)
    w["thQT"] = np.ascontiguousarray(np.asarray(inp["L0_attn_theta_Q"]).T)
    w["thVT"] = np.ascontiguousarray(np.asarray(inp["L0_attn_theta_V"]).T)
    w["W0T"] = np.ascontiguousarray(np.asarray(inp["L0_attn_W0"]).T)
    w["b0row"] = np.ascontiguousarray(np.asarray(inp["L0_attn_b0"]).reshape(1, D))
    w["W1T"] = np.ascontiguousarray(np.asarray(inp["L0_ffn_W1"]).T)
    w["W2T"] = np.ascontiguousarray(np.asarray(inp["L0_ffn_W2"]).T)
    w["b1r"] = np.ascontiguousarray(np.asarray(inp["L0_ffn_b1"]).reshape(FF // 128, 128).T)
    w["b2row"] = np.ascontiguousarray(np.asarray(inp["L0_ffn_b2"]).reshape(1, D))
    w["qw"] = np.ascontiguousarray(np.asarray(inp["L1_attn_q_w"]))
    w["kw"] = np.ascontiguousarray(np.asarray(inp["L1_attn_k_w"]))
    w["vw"] = np.ascontiguousarray(np.asarray(inp["L1_attn_v_w"]))
    w["ow"] = np.ascontiguousarray(np.asarray(inp["L1_attn_o_w"]))
    w["WgT"] = np.ascontiguousarray(np.asarray(inp["L1_ffn_Wg"]).T)
    w["mW1"] = np.ascontiguousarray(np.asarray(inp["L1_ffn_W1"]))
    w["mb1r"] = np.ascontiguousarray(
        np.asarray(inp["L1_ffn_b1"]).reshape(NE, FF // 128, 128).transpose(0, 2, 1))
    w["mW2"] = np.ascontiguousarray(np.asarray(inp["L1_ffn_W2"]))
    w["mb2"] = np.ascontiguousarray(np.asarray(inp["L1_ffn_b2"]))
    return w


def kernel(**inp):
    w = _prep(inp)
    c = host_consts()

    nc = build_l1()
    r = _run(nc, [{"x": w["x"][i], "ident_bf": c["ident_bf"]} for i in range(8)])
    xn_fm = [r[i]["xn_fm"] for i in range(8)]

    nc = build_l2()
    r = _run(nc, [{"xn_fm": xn_fm[i], "thKT": w["thKT"], "thQT": w["thQT"],
                   "thVT": w["thVT"]} for i in range(8)])
    k_fm = [r[i]["k_fm"] for i in range(8)]
    q_fm = [r[i]["q_fm"] for i in range(8)]
    ve_tm = [r[i]["ve_tm"] for i in range(8)]

    nc = build_l3()
    r = _run(nc, [{"k_fm": k_fm[i], "q_fm": q_fm[i], "ve_tm": ve_tm[i],
                   "W0T": w["W0T"], "b0row": w["b0row"], "ident_f": c["ident_f"],
                   "ident_bf": c["ident_bf"], "dmask": c["dmask"], "dmaskT": c["dmaskT"],
                   "triu_incl": c["triu_incl"]} for i in range(8)])
    s_tm = [r[i]["s_tm"] for i in range(8)]

    nc = build_l4()
    r = _run(nc, [{"xn_fm": xn_fm[i], "W1T": w["W1T"], "W2T": w["W2T"],
                   "b1r": w["b1r"], "b2row": w["b2row"]} for i in range(8)])
    g_tm = [r[i]["g_tm"] for i in range(8)]

    nc = build_l5()
    r = _run(nc, [{"x": w["x"][i], "s_tm": s_tm[i], "g_tm": g_tm[i],
                   "ident_bf": c["ident_bf"]} for i in range(8)])
    xp_tm = [r[i]["xp_tm"] for i in range(8)]
    xn1_fm = [r[i]["xn1_fm"] for i in range(8)]

    nc = build_l6()
    r = _run(nc, [{"xn1_fm": xn1_fm[i], "qw": w["qw"], "kw": w["kw"], "vw": w["vw"]}
                  for i in range(8)])
    k1_fm = [r[i]["k1_fm"] for i in range(8)]
    q1_fm = [r[i]["q1_fm"] for i in range(8)]
    v1_tm = [r[i]["v1_tm"] for i in range(8)]

    nc = build_l7()
    r = _run(nc, [{"q1_fm": q1_fm[i], "k1_fm": k1_fm[i], "v1_tm": v1_tm[i],
                   "ow": w["ow"], "iota_row": c["iota_row"], "biascols": c["biascols"],
                   "madd_bf": c["madd_bf"], "ident_bf": c["ident_bf"]} for i in range(8)])
    s1_tm = [r[i]["s1_tm"] for i in range(8)]

    nc = build_l8()
    r = _run(nc, [{"xn1_fm": xn1_fm[i], "WgT": w["WgT"], "W1": w["mW1"], "b1r": w["mb1r"],
                   "W2": w["mW2"], "b2": w["mb2"], "revI": c["revI"],
                   "ident_bf": c["ident_bf"]} for i in range(8)])
    y_tm = [r[i]["y_tm"] for i in range(8)]

    nc = build_l9()
    r = _run(nc, [{"xp_tm": xp_tm[i], "s1_tm": s1_tm[i], "y_tm": y_tm[i]}
                  for i in range(8)])
    return np.stack([r[i]["out_tm"] for i in range(8)], axis=0)


# version 08
# speedup vs baseline: 1.0203x; 1.0203x over previous
import numpy as np
import ml_dtypes
from contextlib import ExitStack
import concourse.bass as bass
import concourse.mybir as mybir
from concourse.bass_utils import run_bass_kernel_spmd

F32 = mybir.dt.float32
BF16 = mybir.dt.bfloat16
AF = mybir.ActivationFunctionType
ALU = mybir.AluOpType
AX = mybir.AxisListType.X
BF16NP = ml_dtypes.bfloat16

S, D, CK, NCH, FF, NH, HD, NE = 2048, 1024, 128, 16, 4096, 16, 64, 8
NDC = D // 128
LR = 0.01
ETA = 2.0 / D
LN_EPS = 1e-5
Q_EPS = 1e-5
NEG = -1e30

ENGINES = ("sync", "vector", "scalar", "tensor", "gpsimd")


class Prog:

    def __init__(self, nc, ctx):
        self.nc = nc
        self.ctx = ctx
        self.tag = ""
        self.streams = {e: [] for e in ENGINES}
        self.sems = {}
        self.ctr = {}

    def sem(self, name):
        if name not in self.sems:
            self.sems[name] = self.ctx.enter_context(self.nc.semaphore(name))
            self.ctr[name] = 0
        return self.sems[name]

    def cnt(self, name):
        self.sem(name)
        return self.ctr[name]

    def op(self, engine, fn, inc=None, wait=None):
        sem_h = None
        if inc is not None:
            sem_h = self.sem(inc[0])
            self.ctr[inc[0]] += inc[1]
            if not hasattr(self, "last_inc"):
                self.last_inc = {}
            self.last_inc[engine] = (inc[0], self.ctr[inc[0]])
        waits = []
        if wait:
            if not hasattr(self, "guar"):
                self.guar = {e: {} for e in ENGINES}
            g = self.guar[engine]
            for wn, wv in wait:
                if g.get(wn, -1) >= wv:
                    continue
                g[wn] = wv
                waits.append((self.sem(wn), wv))

        def run(eng, fn=fn, sem_h=sem_h, inc=inc, waits=waits):
            for wsem, wval in waits:
                eng.wait_ge(wsem, wval)
            inst = fn(eng)
            if sem_h is not None:
                inst.then_inc(sem_h, inc[1])

        self.streams[engine].append(run)
        return self.ctr[inc[0]] if inc else None

    def wait(self, engine, semname, value):
        if not hasattr(self, "guar"):
            self.guar = {e: {} for e in ENGINES}
        g = self.guar[engine]
        if g.get(semname, -1) >= value:
            return
        g[semname] = value
        sem_h = self.sem(semname)
        self.streams[engine].append(lambda eng, s=sem_h, v=value: eng.wait_ge(s, v))

    def finalize(self):
        nc = self.nc
        with nc.Block() as block:
            @block.sync
            def _(e):
                for fn in self.streams["sync"]:
                    fn(e)

            @block.vector
            def _(e):
                for fn in self.streams["vector"]:
                    fn(e)

            @block.scalar
            def _(e):
                for fn in self.streams["scalar"]:
                    fn(e)

            @block.tensor
            def _(e):
                for fn in self.streams["tensor"]:
                    fn(e)

            @block.gpsimd
            def _(e):
                for fn in self.streams["gpsimd"]:
                    fn(e)

    def barrier(self):
        incs = dict(getattr(self, "last_inc", {}))
        for eng in ENGINES:
            for semname, v in incs.values():
                self.wait(eng, semname, v)


class NSProg:

    def __init__(self, base, tag, ctx):
        self.base = base
        self.tag = tag
        self.ctx = ctx
        self.nc = base.nc
        self.A = None

    def _n(self, name):
        return self.tag + name

    def sem(self, name):
        return self.base.sem(self._n(name))

    def cnt(self, name):
        return self.base.cnt(self._n(name))

    @property
    def ctr(self):
        raise AttributeError("use cnt()")

    def op(self, engine, fn, inc=None, wait=None):
        inc2 = (self._n(inc[0]), inc[1]) if inc else None
        wait2 = [(self._n(w), v) for w, v in wait] if wait else None
        return self.base.op(engine, fn, inc2, wait2)

    def wait(self, engine, semname, value):
        self.base.wait(engine, self._n(semname), value)

def host_consts():
    c = {}
    c["ident_bf"] = np.eye(128, dtype=BF16NP)
    c["ident_f"] = np.eye(128, dtype=np.float32)
    c["iota_col"] = np.arange(128, dtype=np.float32).reshape(128, 1)
    c["iota_row"] = np.arange(S, dtype=np.float32).reshape(1, S)
    c["triu_strict"] = np.triu(np.ones((128, 128), np.float32), 1)
    c["tril_strict"] = np.tril(np.ones((128, 128), np.float32), -1)
    c["triu_incl"] = np.triu(np.ones((128, 128), np.float32), 0)
    dm = np.zeros((7, 128, 128), np.float32)
    m = 1
    for lev in range(7):
        for blk in range(0, 128, 2 * m):
            dm[lev, blk:blk + m, blk + m:blk + 2 * m] = 1.0
        m *= 2
    c["dmask"] = dm
    c["dmaskT"] = np.ascontiguousarray(dm.transpose(0, 2, 1))
    c["madd_bf"] = np.where(np.arange(128)[:, None] > np.arange(128)[None, :],
                            np.float32(NEG), np.float32(0)).astype(BF16NP)
    c["revI"] = np.broadcast_to(np.arange(NE, 0, -1, dtype=np.float32), (128, NE)).copy()
    ratio = 2.0 ** (-8.0 / NH)
    slopes = np.array([ratio ** (i + 1) for i in range(NH)], np.float32)
    bc = np.zeros((128, NH * NCH), np.float32)
    for h in range(NH):
        for kt in range(NCH):
            bc[:, h * NCH + kt] = slopes[h] * (np.arange(128) + 128 * kt)
    c["biascols"] = bc
    c["tokid"] = (np.arange(128)[:, None] + 128 * np.arange(NCH)[None, :]).astype(np.float32)
    c["eidx"] = np.broadcast_to(np.arange(NE, dtype=np.float32), (128, NE)).copy()
    return c



def emit_ln_to_fm(P, x_ap, xnfm_ap, ident_sb, tag, extra_load_wait=None, tm_out=None):
    nc, ctx = P.nc, P.ctx
    xt = [ctx.enter_context(nc.sbuf_tensor(P.tag + f"{tag}xt{i}", [128, D], F32)) for i in range(2)]
    sq = ctx.enter_context(nc.sbuf_tensor(P.tag + f"{tag}sq", [128, D], F32))
    msum = ctx.enter_context(nc.sbuf_tensor(P.tag + f"{tag}msum", [128, 1], F32))
    mean = ctx.enter_context(nc.sbuf_tensor(P.tag + f"{tag}mean", [128, 1], F32))
    s2 = ctx.enter_context(nc.sbuf_tensor(P.tag + f"{tag}s2", [128, 1], F32))
    var = ctx.enter_context(nc.sbuf_tensor(P.tag + f"{tag}var", [128, 1], F32))
    rv = ctx.enter_context(nc.sbuf_tensor(P.tag + f"{tag}rv", [128, 1], F32))
    rstd = ctx.enter_context(nc.sbuf_tensor(P.tag + f"{tag}rstd", [128, 1], F32))
    negmr = ctx.enter_context(nc.sbuf_tensor(P.tag + f"{tag}negmr", [128, 1], F32))
    xnb = [ctx.enter_context(nc.sbuf_tensor(P.tag + f"{tag}xnb{i}", [128, D], BF16)) for i in range(2)]
    fm = ctx.enter_context(nc.sbuf_tensor(P.tag + f"{tag}fm", [128, NDC * S], BF16))
    ptp = [ctx.enter_context(nc.psum_tensor(P.tag + f"{tag}ptp{i}", [128, 128], BF16)) for i in range(2)]

    sLD, sV, sS, sT, sEV, sST = (f"{tag}ld", f"{tag}v", f"{tag}s", f"{tag}t", f"{tag}ev", f"{tag}st")

    for tc in range(NCH):
        p = tc % 2
        w = [(sV, 7 * (tc - 1)), (sS, 3 * (tc - 2) + 1)] if tc >= 2 else []
        if extra_load_wait is not None:
            w = list(w) + extra_load_wait(tc)
        P.op("sync", lambda e, d=xt[p], s_=x_ap[tc * 128:(tc + 1) * 128, :]:
             e.dma_start(out=d[:, :], in_=s_), inc=(sLD, 16), wait=w or None)
        w = [(sLD, 16 * (tc + 1))]
        if tc >= 1:
            w.append((sV, 7 * (tc - 1) + 2))
        P.op("scalar", lambda e, d=sq, s_=xt[p]: e.activation(d[:, :], s_[:, :], AF.Square),
             inc=(sS, 1), wait=w)
        b = 7 * tc
        P.op("vector", lambda e, s_=xt[p]: e.reduce_sum(msum[:, :], s_[:, :], AX),
             inc=(sV, 1), wait=[(sLD, 16 * (tc + 1))])
        P.op("vector", lambda e: e.reduce_sum(s2[:, :], sq[:, :], AX),
             inc=(sV, 1), wait=[(sS, 3 * tc + 1)])
        P.op("vector", lambda e: e.scalar_tensor_tensor(
            mean[:, :], msum[:, :], 1.0 / (D * D), msum[:, :], ALU.mult, ALU.mult),
            inc=(sV, 1), wait=[(sV, b + 1)])
        w4 = [(sV, b + 2)]
        if tc >= 1:
            w4.append((sS, 3 * tc - 1))
        P.op("vector", lambda e: e.tensor_scalar(
            var[:, :], s2[:, :], 1.0 / D, LN_EPS, ALU.mult, ALU.add),
            inc=(sV, 1), wait=w4)
        P.op("vector", lambda e: e.tensor_sub(var[:, :], var[:, :], mean[:, :]),
             inc=(sV, 1), wait=[(sV, b + 4)])
        P.op("scalar", lambda e: e.activation(rv[:, :], var[:, :], AF.Ln),
             inc=(sS, 1), wait=[(sV, b + 5)])
        P.op("scalar", lambda e: e.activation(rstd[:, :], rv[:, :], AF.Exp, scale=-0.5),
             inc=(sS, 1))
        P.op("vector", lambda e: e.scalar_tensor_tensor(
            negmr[:, :], msum[:, :], -1.0 / D, rstd[:, :], ALU.mult, ALU.mult),
            inc=(sV, 1), wait=[(sS, 3 * tc + 3)])
        wv = [(sV, b + 6)]
        if tc >= 2:
            wv.append((sT, 8 * (tc - 1)))
            if tm_out is not None:
                wv.append((f"{tag}tmo", 16 * (tc - 1)))
        P.op("vector", lambda e, s_=xt[p], d=xnb[p]: e.tensor_scalar(
            d[:, :], s_[:, :], rstd[:, 0:1], negmr[:, 0:1], ALU.mult, ALU.add),
            inc=(sV, 1), wait=wv)
        if tm_out is not None:
            P.op("sync", lambda e, p=p, tc=tc: e.dma_start(
                out=tm_out[tc * 128:(tc + 1) * 128, :], in_=xnb[p][:, :]),
                inc=(f"{tag}tmo", 16), wait=[(sV, 7 * (tc + 1))])
        for dc in range(NDC):
            q = dc % 2
            wt = [(sV, 7 * (tc + 1))]
            if tc * 8 + dc >= 2:
                wt.append((sEV, 8 * tc + dc - 1))
            P.op("tensor", lambda e, q=q, src=xnb[p], dc=dc:
                 e.transpose(ptp[q][:, :], src[:, dc * 128:(dc + 1) * 128], ident_sb[:, :]),
                 inc=(sT, 1), wait=wt)
            P.op("vector", lambda e, q=q, dc=dc, tc=tc:
                 e.tensor_copy(fm[:, dc * S + tc * 128: dc * S + (tc + 1) * 128], ptp[q][:, :]),
                 inc=(sEV, 1), wait=[(sT, 8 * tc + dc + 1)])
    for dc in range(NDC):
        P.op("sync", lambda e, dc=dc: e.dma_start(
            out=xnfm_ap[dc * 128:(dc + 1) * 128, :], in_=fm[:, dc * S:(dc + 1) * S]),
            inc=(sST, 16), wait=[(sEV, NCH * NDC)])
    return fm, sST


def phase_l1(P, dd):
    nc, ctx, tg = P.nc, P.ctx, P.tag
    x = dd["x"]
    identb = dd["ident_bf"]
    xn_fm = dd["xn_fm"]
    ident_sb = ctx.enter_context(nc.sbuf_tensor(tg + "ident_sb", [128, 128], BF16))
    P.op("sync", lambda e: e.dma_start(out=ident_sb[:, :], in_=identb[:, :]), inc=("ild", 16))
    P.wait("tensor", "ild", 16)
    _, sST = emit_ln_to_fm(P, x, xn_fm, ident_sb, "ln0")
    P.wait("gpsimd", sST, 16 * NDC)


def build_l1():
    nc = bass.Bass("TRN2", target_bir_lowering=False, debug=False)
    dd = {}
    dd["x"] = nc.dram_tensor("x", [S, D], F32, kind="ExternalInput").ap()
    dd["ident_bf"] = nc.dram_tensor("ident_bf", [128, 128], BF16, kind="ExternalInput").ap()
    dd["xn_fm"] = nc.dram_tensor("xn_fm", [D, S], BF16, kind="ExternalOutput").ap()
    with ExitStack() as ctx:
        P = Prog(nc, ctx)
        phase_l1(P, dd)
        P.finalize()
    return nc



def phase_l2(P, dd):
    nc, ctx, tg = P.nc, P.ctx, P.tag
    xnfm = dd["xn_fm"]
    thKT = dd["thKT"]
    thQT = dd["thQT"]
    thVT = dd["thVT"]
    k_fm = dd["k_fm"]
    q_fm = dd["q_fm"]
    ve_tm = dd["ve_tm"]
    xn = ctx.enter_context(nc.sbuf_tensor(tg + "xn", [128, NDC * S], BF16))
    wK = ctx.enter_context(nc.sbuf_tensor(tg + "wK", [128, NDC * D], BF16))
    wQ = ctx.enter_context(nc.sbuf_tensor(tg + "wQ", [128, NDC * D], BF16))
    wV = ctx.enter_context(nc.sbuf_tensor(tg + "wV", [128, NDC * D], BF16))
    outK = ctx.enter_context(nc.sbuf_tensor(tg + "outK", [128, NDC * S], BF16))
    outQ = ctx.enter_context(nc.sbuf_tensor(tg + "outQ", [128, NDC * S], BF16))
    outV = ctx.enter_context(nc.sbuf_tensor(tg + "outV", [128, NCH * D], BF16))
    pk = ctx.enter_context(nc.psum_tensor(tg + "pk", [128, 512], F32))
    pq = ctx.enter_context(nc.psum_tensor(tg + "pq", [128, 512], F32))
    pv = ctx.enter_context(nc.psum_tensor(tg + "pv", [128, 1024], F32))
    emit_proj_ttt(P, xnfm, thKT, thQT, thVT, xn, wK, wQ, wV, outK, outQ, outV,
                  pk, pq, pv, "p2")
    nst = 0
    for dc in range(NDC):
        P.op("sync", lambda e, dc=dc: e.dma_start(
            out=k_fm[dc * 128:(dc + 1) * 128, :], in_=outK[:, dc * S:(dc + 1) * S]),
            inc=("p2st", 16), wait=[("p2evK", NDC * (S // 512))])
        P.op("sync", lambda e, dc=dc: e.dma_start(
            out=q_fm[dc * 128:(dc + 1) * 128, :], in_=outQ[:, dc * S:(dc + 1) * S]),
            inc=("p2st", 16), wait=[("p2evQ", NDC * (S // 512))])
        nst += 2
    for tc in range(NCH):
        P.op("sync", lambda e, tc=tc: e.dma_start(
            out=ve_tm[tc * 128:(tc + 1) * 128, :], in_=outV[:, tc * D:(tc + 1) * D]),
            inc=("p2st", 16), wait=[("p2evV", NCH)])
        nst += 1
    P.wait("gpsimd", "p2st", 16 * nst)


def build_l2():
    nc = bass.Bass("TRN2", target_bir_lowering=False, debug=False)
    dd = {}
    dd["xn_fm"] = nc.dram_tensor("xn_fm", [D, S], BF16, kind="ExternalInput").ap()
    dd["thKT"] = nc.dram_tensor("thKT", [D, D], F32, kind="ExternalInput").ap()
    dd["thQT"] = nc.dram_tensor("thQT", [D, D], F32, kind="ExternalInput").ap()
    dd["thVT"] = nc.dram_tensor("thVT", [D, D], F32, kind="ExternalInput").ap()
    dd["k_fm"] = nc.dram_tensor("k_fm", [D, S], BF16, kind="ExternalOutput").ap()
    dd["q_fm"] = nc.dram_tensor("q_fm", [D, S], BF16, kind="ExternalOutput").ap()
    dd["ve_tm"] = nc.dram_tensor("ve_tm", [S, D], BF16, kind="ExternalOutput").ap()
    with ExitStack() as ctx:
        P = Prog(nc, ctx)
        phase_l2(P, dd)
        P.finalize()
    return nc


def emit_proj_ttt(P, xnfm, thKT, thQT, thVT, xn, wK, wQ, wV, outK, outQ, outV,
                  pk, pq, pv, tag, vscale=ETA):
    nc = P.nc
    sLD, sT, sEK, sEQ, sEV = f"{tag}ld", f"{tag}t", f"{tag}evK", f"{tag}evQ", f"{tag}evV"
    nld = 0
    for dc in range(NDC):
        P.op("sync", lambda e, dc=dc: e.dma_start(
            out=xn[:, dc * S:(dc + 1) * S], in_=xnfm[dc * 128:(dc + 1) * 128, :]), inc=(sLD, 16))
        nld += 1
    for w_sb, w_ap in ((wK, thKT), (wQ, thQT), (wV, thVT)):
        for dc in range(NDC):
            P.op("gpsimd", lambda e, d=w_sb, dc=dc, a=w_ap: e.dma_start(
                out=d[:, dc * D:(dc + 1) * D], in_=a[dc * 128:(dc + 1) * 128, :]), inc=(sLD, 16))
            nld += 1
    P.wait("tensor", sLD, 16 * nld)

    for w_sb, pbank, out_sb, sev in ((wK, pk, outK, sEK), (wQ, pq, outQ, sEQ)):
        g = 0
        for ec in range(NDC):
            for tb in range(S // 512):
                if g > 0:
                    P.wait("tensor", sev, g)
                for dc in range(NDC):
                    P.op("tensor", lambda e, dc=dc, ec=ec, tb=tb, w=w_sb, pb=pbank:
                         e.matmul(pb[:, :], w[:, dc * D + ec * 128: dc * D + (ec + 1) * 128],
                                  xn[:, dc * S + tb * 512: dc * S + (tb + 1) * 512],
                                  start=(dc == 0), stop=(dc == NDC - 1)),
                         inc=(sT, 1) if dc == NDC - 1 else None)
                P.op("vector", lambda e, ec=ec, tb=tb, pb=pbank, o=out_sb:
                     e.tensor_copy(o[:, ec * S + tb * 512: ec * S + (tb + 1) * 512], pb[:, :]),
                     inc=(sev, 1), wait=[(sT, P.cnt(sT))])
                g += 1
    for tc in range(NCH):
        if tc > 0:
            P.wait("tensor", sEV, tc)
        for dc in range(NDC):
            for half in range(2):
                P.op("tensor", lambda e, dc=dc, tc=tc, half=half:
                     e.matmul(pv[:, half * 512:(half + 1) * 512],
                              xn[:, dc * S + tc * 128: dc * S + (tc + 1) * 128],
                              wV[:, dc * D + half * 512: dc * D + (half + 1) * 512],
                              start=(dc == 0), stop=(dc == NDC - 1)),
                     inc=(sT, 1) if (dc == NDC - 1 and half == 1) else None)
        P.op("vector", lambda e, tc=tc: e.tensor_scalar_mul(
            outV[:, tc * D:(tc + 1) * D], pv[:, :], vscale),
            inc=(sEV, 1), wait=[(sT, P.cnt(sT))])



def phase_l3(P, dd):
    nc, ctx, tg = P.nc, P.ctx, P.tag
    LE = LR * ETA
    k_fm = dd["k_fm"]
    q_fm = dd["q_fm"]
    ve_tm = dd["ve_tm"]
    W0T = dd["W0T"]
    b0row = dd["b0row"]
    identf = dd["ident_f"]
    identb = dd["ident_bf"]
    dmask = dd["dmask"]
    dmaskT = dd["dmaskT"]
    triu_incl = dd["triu_incl"]
    s_tm = dd["s_tm"]
    sb = lambda name, shape, dt: ctx.enter_context(nc.sbuf_tensor(tg + name, shape, dt))
    ps = lambda name, shape, dt: ctx.enter_context(nc.psum_tensor(tg + name, shape, dt))
    K = sb("K", [128, NDC * S], BF16)
    Q = sb("Q", [128, NDC * S], BF16)
    Ve = sb("Ve", [128, NCH * D], BF16)
    Wtf = sb("Wtf", [128, NDC * D], F32)
    Wtb = sb("Wtb", [128, NDC * D], BF16)
    bf_ = sb("bf_", [1, D], F32)
    bb_ = sb("bb_", [1, D], BF16)
    idf = sb("idf", [128, 128], F32)
    idb = sb("idb", [128, 128], BF16)
    dmu = sb("dmu", [128, 7 * 128], F32)
    dml = sb("dml", [128, 7 * 128], F32)
    triu = sb("triu", [128, 128], F32)
    ones1 = sb("ones1", [1, 128], BF16)
    onesc = sb("onesc", [128, 1], BF16)
    Bfull = sb("Bfull", [128, 128], F32)
    Ebf = sb("Ebf", [128, D], BF16)
    Gb = sb("Gb", [128, D], BF16)
    Pf = sb("Pf", [128, 128], F32)
    Pb = sb("Pb", [128, 128], BF16)
    Tf = sb("Tf", [128, 128], F32)
    Ttf = sb("Ttf", [128, 128], F32)
    Tb = sb("Tb", [128, 128], BF16)
    Ttb = sb("Ttb", [128, 128], BF16)
    Cub = sb("Cub", [128, 128], BF16)
    Clb = sb("Clb", [128, 128], BF16)
    Xb = sb("Xb", [128, 128], BF16)
    Zb = sb("Zb", [128, 128], BF16)
    Ktm = sb("Ktm", [128, D], BF16)
    s_sb = [sb(f"s_sb{i}", [128, D], F32) for i in range(2)]
    psBig = ps("psBig", [128, 1024], F32)
    psBig2 = ps("psBig2", [128, 1024], F32)
    psS = ps("psS", [128, 128], F32)
    psT1 = ps("psT1", [128, 128], F32)
    psT2 = ps("psT2", [128, 128], F32)
    psTP = ps("psTP", [128, 128], BF16)

    nld = 0
    for dc in range(NDC):
        P.op("sync", lambda e, dc=dc: e.dma_start(out=K[:, dc * S:(dc + 1) * S],
             in_=k_fm[dc * 128:(dc + 1) * 128, :]), inc=("ld", 16)); nld += 1
        P.op("sync", lambda e, dc=dc: e.dma_start(out=Q[:, dc * S:(dc + 1) * S],
             in_=q_fm[dc * 128:(dc + 1) * 128, :]), inc=("ld", 16)); nld += 1
        P.op("sync", lambda e, dc=dc: e.dma_start(out=Wtf[:, dc * D:(dc + 1) * D],
             in_=W0T[dc * 128:(dc + 1) * 128, :]), inc=("ld", 16)); nld += 1
    for tc in range(NCH):
        P.op("sync", lambda e, tc=tc: e.dma_start(out=Ve[:, tc * D:(tc + 1) * D],
             in_=ve_tm[tc * 128:(tc + 1) * 128, :]), inc=("ld", 16)); nld += 1
    P.op("sync", lambda e: e.dma_start(out=bf_[:, :], in_=b0row[:, :]), inc=("ld", 16)); nld += 1
    P.op("sync", lambda e: e.dma_start(out=idf[:, :], in_=identf[:, :]), inc=("ld", 16)); nld += 1
    P.op("sync", lambda e: e.dma_start(out=idb[:, :], in_=identb[:, :]), inc=("ld", 16)); nld += 1
    P.op("sync", lambda e: e.dma_start(out=triu[:, :], in_=triu_incl[:, :]), inc=("ld", 16)); nld += 1
    for lev in range(7):
        P.op("sync", lambda e, lev=lev: e.dma_start(out=dmu[:, lev * 128:(lev + 1) * 128],
             in_=dmask[lev]), inc=("ld", 16)); nld += 1
        P.op("sync", lambda e, lev=lev: e.dma_start(out=dml[:, lev * 128:(lev + 1) * 128],
             in_=dmaskT[lev]), inc=("ld", 16)); nld += 1
    LDALL = 16 * nld
    P.op("vector", lambda e: e.memset(ones1[:, :], 1.0), inc=("v0", 1))
    P.op("vector", lambda e: e.memset(onesc[:, :], 1.0), inc=("v0", 1))
    for dc in range(NDC):
        P.op("vector", lambda e, dc=dc: e.tensor_copy(
            Wtb[:, dc * D:(dc + 1) * D], Wtf[:, dc * D:(dc + 1) * D]),
            inc=("wb", 1), wait=[("ld", LDALL)])
    P.op("vector", lambda e: e.tensor_copy(bb_[:, :], bf_[:, :]),
         inc=("bb", 1), wait=[("ld", LDALL)])

    kt = lambda dc, n: K[:, dc * S + n * 128: dc * S + (n + 1) * 128]
    qt = lambda dc, n: Q[:, dc * S + n * 128: dc * S + (n + 1) * 128]

    for n in range(NCH):
        wE = [("wb", 8 * (n + 1)), ("bb", n + 1)]
        if n > 0:
            wE.append(("oev", n))
        for half in range(2):
            for dc in range(NDC):
                P.op("tensor", lambda e, dc=dc, half=half, n=n: e.matmul(
                    psBig[:, half * 512:(half + 1) * 512], kt(dc, n),
                    Wtb[:, dc * D + half * 512: dc * D + (half + 1) * 512],
                    start=(dc == 0), stop=False),
                    wait=wE if (dc == 0 and half == 0) else None)
            P.op("tensor", lambda e, half=half: e.matmul(
                psBig[:, half * 512:(half + 1) * 512], ones1[:, :],
                bb_[0:1, half * 512:(half + 1) * 512], start=False, stop=True),
                inc=("t", 1))
        TE = P.cnt("t")
        P.op("vector", lambda e, n=n: e.scalar_tensor_tensor(
            Ebf[:, :], psBig[:, :], ETA, Ve[:, n * D:(n + 1) * D],
            ALU.mult, ALU.subtract), inc=("eev", 1), wait=[("t", TE)])

        wS = [("cb", 14 * n)]
        if n > 0:
            wS.append(("pe", 2 * n))
        for dc in range(NDC):
            P.op("tensor", lambda e, dc=dc, n=n: e.matmul(
                psS[:, :], kt(dc, n), kt(dc, n), start=(dc == 0), stop=(dc == NDC - 1)),
                inc=("t", 1) if dc == NDC - 1 else None,
                wait=wS if dc == 0 else None)
        TS = P.cnt("t")
        P.op("vector", lambda e: e.tensor_scalar(
            Bfull[:, :], psS[:, :], LE, LE, ALU.mult, ALU.add),
            inc=("bfv", 1), wait=[("t", TS)])

        wTi = [("ld", LDALL)]
        if n > 0:
            wTi.append(("t", TS))
        P.op("vector", lambda e: e.tensor_copy(Tf[:, :], idf[:, :]), inc=("ti", 1), wait=wTi)
        P.op("vector", lambda e: e.tensor_copy(Ttf[:, :], idf[:, :]), inc=("ti", 1))
        P.op("vector", lambda e: e.tensor_copy(Tb[:, :], idb[:, :]), inc=("ti", 1))
        P.op("vector", lambda e: e.tensor_copy(Ttb[:, :], idb[:, :]), inc=("ti", 1))
        for lev in range(7):
            P.op("vector", lambda e, lev=lev: e.tensor_mul(
                Cub[:, :], Bfull[:, :], dmu[:, lev * 128:(lev + 1) * 128]),
                inc=("cb", 1), wait=[("bfv", P.cnt("bfv"))])
            P.op("vector", lambda e, lev=lev: e.tensor_mul(
                Clb[:, :], Bfull[:, :], dml[:, lev * 128:(lev + 1) * 128]),
                inc=("cb", 1))
            CB = P.cnt("cb")
            wX = [("cb", CB), ("ti", 4 * (n + 1))]
            if lev > 0:
                wX.append(("ts", 14 * n + 2 * lev))
                wX.append(("tb", 14 * n + 2 * lev))
            P.op("tensor", lambda e: e.matmul(psT1[:, :], Clb[:, :], Tb[:, :],
                 start=True, stop=True), inc=("t", 1), wait=wX)
            P.op("tensor", lambda e: e.matmul(psT2[:, :], Cub[:, :], Ttb[:, :],
                 start=True, stop=True), inc=("t", 1))
            TXZ = P.cnt("t")
            P.op("vector", lambda e: e.tensor_copy(Xb[:, :], psT1[:, :]),
                 inc=("xb", 1), wait=[("t", TXZ)])
            P.op("vector", lambda e: e.tensor_copy(Zb[:, :], psT2[:, :]), inc=("xb", 1))
            XB = P.cnt("xb")
            P.op("tensor", lambda e: e.matmul(psT1[:, :], Ttb[:, :], Xb[:, :],
                 start=True, stop=True), inc=("t", 1), wait=[("xb", XB)])
            P.op("tensor", lambda e: e.matmul(psT2[:, :], Tb[:, :], Zb[:, :],
                 start=True, stop=True), inc=("t", 1))
            TY = P.cnt("t")
            P.op("vector", lambda e: e.tensor_sub(Tf[:, :], Tf[:, :], psT1[:, :]),
                 inc=("ts", 1), wait=[("t", TY)])
            P.op("vector", lambda e: e.tensor_sub(Ttf[:, :], Ttf[:, :], psT2[:, :]),
                 inc=("ts", 1))
            TSS = P.cnt("ts")
            P.op("vector", lambda e: e.tensor_copy(Tb[:, :], Tf[:, :]),
                 inc=("tb", 1), wait=[("ts", TSS)])
            P.op("vector", lambda e: e.tensor_copy(Ttb[:, :], Ttf[:, :]), inc=("tb", 1))

        wG = [("tb", 14 * (n + 1)), ("eev", n + 1)]
        if n > 0:
            wG.append(("wf", 8 * n))
            wG.append(("bu", 2 * n))
        for half in range(2):
            P.op("tensor", lambda e, half=half: e.matmul(
                psBig2[:, half * 512:(half + 1) * 512], Tb[:, :],
                Ebf[:, half * 512:(half + 1) * 512], start=True, stop=True),
                inc=("t", 1), wait=wG if half == 0 else None)
        TG = P.cnt("t")
        P.op("vector", lambda e: e.tensor_copy(Gb[:, :], psBig2[:, :]),
             inc=("gb", 1), wait=[("t", TG)])

        for dc in range(NDC):
            P.op("tensor", lambda e, dc=dc, n=n: e.matmul(
                psS[:, :], kt(dc, n), qt(dc, n), start=(dc == 0), stop=(dc == NDC - 1)),
                inc=("t", 1) if dc == NDC - 1 else None,
                wait=[("bfv", P.cnt("bfv"))] if dc == 0 else None)
        TSQ = P.cnt("t")
        P.op("vector", lambda e: e.tensor_scalar(
            Pf[:, :], psS[:, :], -LR, -LR, ALU.mult, ALU.add),
            inc=("pe", 1), wait=[("t", TSQ)])
        P.op("vector", lambda e: e.tensor_mul(Pb[:, :], Pf[:, :], triu[:, :]),
             inc=("pe", 1), wait=[("pe", P.cnt("pe") - 1)])

        wO = [("gb", n + 1), ("pe", 2 * (n + 1)), ("eev", n + 1)]
        for half in range(2):
            for dc in range(NDC):
                P.op("tensor", lambda e, dc=dc, half=half, n=n: e.matmul(
                    psBig[:, half * 512:(half + 1) * 512], qt(dc, n),
                    Wtb[:, dc * D + half * 512: dc * D + (half + 1) * 512],
                    start=(dc == 0), stop=False),
                    wait=wO if (dc == 0 and half == 0) else None)
            P.op("tensor", lambda e, half=half: e.matmul(
                psBig[:, half * 512:(half + 1) * 512], ones1[:, :],
                bb_[0:1, half * 512:(half + 1) * 512], start=False, stop=False))
            P.op("tensor", lambda e, half=half: e.matmul(
                psBig[:, half * 512:(half + 1) * 512], Pb[:, :],
                Gb[:, half * 512:(half + 1) * 512], start=False, stop=True),
                inc=("t", 1))
        TO = P.cnt("t")
        wOe = [("t", TO)]
        if n >= 2:
            wOe.append(("sst", 16 * (n - 1)))
        P.op("vector", lambda e, n=n: e.tensor_copy(s_sb[n % 2][:, :], psBig[:, :]),
             inc=("oev", 1), wait=wOe)
        P.op("sync", lambda e, n=n: e.dma_start(
            out=s_tm[n * 128:(n + 1) * 128, :], in_=s_sb[n % 2][:, :]),
            inc=("sst", 16), wait=[("oev", n + 1)])

        for dc in range(NDC):
            wT = [("ktv", 8 * n + dc)]
            P.op("tensor", lambda e, dc=dc, n=n: e.transpose(
                psTP[:, :], kt(dc, n), idb[:, :]), inc=("t", 1), wait=wT)
            P.op("vector", lambda e, dc=dc: e.tensor_copy(
                Ktm[:, dc * 128:(dc + 1) * 128], psTP[:, :]),
                inc=("ktv", 1), wait=[("t", P.cnt("t"))])

        for dc in range(NDC):
            wW = [("ktv", 8 * n + dc + 1), ("gb", n + 1)]
            if dc > 0:
                wW.append(("wf", 8 * n + dc))
            for half in range(2):
                P.op("tensor", lambda e, dc=dc, half=half: e.matmul(
                    psBig2[:, half * 512:(half + 1) * 512],
                    Ktm[:, dc * 128:(dc + 1) * 128],
                    Gb[:, half * 512:(half + 1) * 512], start=True, stop=True),
                    inc=("t", 1) if half == 1 else None,
                    wait=wW if half == 0 else None)
            TW = P.cnt("t")
            P.op("vector", lambda e, dc=dc: e.scalar_tensor_tensor(
                Wtf[:, dc * D:(dc + 1) * D], psBig2[:, :], -LR,
                Wtf[:, dc * D:(dc + 1) * D], ALU.mult, ALU.add),
                inc=("wf", 1), wait=[("t", TW)])
            P.op("vector", lambda e, dc=dc: e.tensor_copy(
                Wtb[:, dc * D:(dc + 1) * D], Wtf[:, dc * D:(dc + 1) * D]),
                inc=("wb", 1), wait=[("wf", P.cnt("wf"))])

        for half in range(2):
            wB = [("wf", 8 * (n + 1))]
            if half == 1:
                wB = [("bu", 2 * n + 1)]
            P.op("tensor", lambda e, half=half: e.matmul(
                psBig2[0:1, half * 512:(half + 1) * 512], onesc[:, :],
                Gb[:, half * 512:(half + 1) * 512], start=True, stop=True),
                inc=("t", 1), wait=wB)
            P.op("vector", lambda e, half=half: e.scalar_tensor_tensor(
                bf_[0:1, half * 512:(half + 1) * 512],
                psBig2[0:1, half * 512:(half + 1) * 512], -LR,
                bf_[0:1, half * 512:(half + 1) * 512], ALU.mult, ALU.add),
                inc=("bu", 1), wait=[("t", P.cnt("t"))])
        P.op("vector", lambda e: e.tensor_copy(bb_[:, :], bf_[:, :]),
             inc=("bb", 1), wait=[("bu", P.cnt("bu"))])

    P.wait("gpsimd", "sst", 16 * NCH)


def build_l3():
    nc = bass.Bass("TRN2", target_bir_lowering=False, debug=False)
    dd = {}
    dd["k_fm"] = nc.dram_tensor("k_fm", [D, S], BF16, kind="ExternalInput").ap()
    dd["q_fm"] = nc.dram_tensor("q_fm", [D, S], BF16, kind="ExternalInput").ap()
    dd["ve_tm"] = nc.dram_tensor("ve_tm", [S, D], BF16, kind="ExternalInput").ap()
    dd["W0T"] = nc.dram_tensor("W0T", [D, D], F32, kind="ExternalInput").ap()
    dd["b0row"] = nc.dram_tensor("b0row", [1, D], F32, kind="ExternalInput").ap()
    dd["ident_f"] = nc.dram_tensor("ident_f", [128, 128], F32, kind="ExternalInput").ap()
    dd["ident_bf"] = nc.dram_tensor("ident_bf", [128, 128], BF16, kind="ExternalInput").ap()
    dd["dmask"] = nc.dram_tensor("dmask", [7, 128, 128], F32, kind="ExternalInput").ap()
    dd["dmaskT"] = nc.dram_tensor("dmaskT", [7, 128, 128], F32, kind="ExternalInput").ap()
    dd["triu_incl"] = nc.dram_tensor("triu_incl", [128, 128], F32, kind="ExternalInput").ap()
    dd["s_tm"] = nc.dram_tensor("s_tm", [S, D], F32, kind="ExternalOutput").ap()
    with ExitStack() as ctx:
        P = Prog(nc, ctx)
        phase_l3(P, dd)
        P.finalize()
    return nc



def emit_quantize(P, w_ap, wq_sb, pieces, pw, pcols, acc, part, wst, ones_cf,
                  ones_rf, thr_col, nthr_col, psG, psB, nelem, tag, prev=None):
    nc = P.nc
    sLD, sV, sQ = f"{tag}ld", f"{tag}v", f"{tag}q"
    for i, (r0, c0, _) in enumerate(pieces):
        p = i % 2
        w = [(sLD, 16 * i)]
        if i >= 2:
            w.append((sV, 2 * (i - 1)))
        elif prev is not None:
            w.append(prev)
        P.op("sync", lambda e, p=p, r0=r0, c0=c0: e.dma_start(
            out=wst[p][:, 0:pw], in_=w_ap[r0:r0 + 128, c0:c0 + pw]), inc=(sLD, 16), wait=w)
        P.op("vector", lambda e, p=p: e.reduce_sum(
            part[:, 0:1], wst[p][:, 0:pw], AX, apply_absolute_value=True),
            inc=(sV, 1), wait=[(sLD, 16 * (i + 1)), (sV, 2 * i)])
        P.op("vector", lambda e: e.tensor_add(acc[:, :], acc[:, :], part[:, 0:1]),
             inc=(sV, 1), wait=[(sV, 2 * i + 1)])
    NV = P.cnt(sV)
    P.op("tensor", lambda e: e.matmul(psG[:, :], acc[:, :], ones_cf[:, :],
         start=True, stop=True), inc=(f"{tag}t", 1), wait=[(sV, NV)])
    P.op("vector", lambda e: e.tensor_scalar(
        part[0:1, 0:1], psG[0:1, 0:1], 0.5 / nelem, 0.5 * Q_EPS, ALU.mult, ALU.add),
        inc=(sV, 1), wait=[(f"{tag}t", 1)])
    P.op("tensor", lambda e: e.matmul(psB[:, :], ones_rf[:, :], part[0:1, 0:1],
         start=True, stop=True), inc=(f"{tag}t", 1), wait=[(sV, NV + 1)])
    P.op("vector", lambda e: e.tensor_copy(thr_col[:, :], psB[:, :]),
         inc=(sV, 1), wait=[(f"{tag}t", 2)])
    P.op("vector", lambda e: e.tensor_scalar_mul(nthr_col[:, :], thr_col[:, :], -1.0),
         inc=(sV, 1), wait=[(sV, NV + 2)])
    NV2 = P.cnt(sV)
    for i, (r0, c0, off) in enumerate(pieces):
        p = i % 2
        w = [(sV, NV2 if i < 2 else NV2 + 2 * (i - 1))]
        P.op("sync", lambda e, p=p, r0=r0, c0=c0: e.dma_start(
            out=wst[p][:, 0:pw], in_=w_ap[r0:r0 + 128, c0:c0 + pw]),
            inc=(sLD, 16), wait=w)
        P.op("vector", lambda e, p=p: e.tensor_single_scalar(
            P.A[:, 0:pw], wst[p][:, 0:pw], nthr_col[:, 0:1], ALU.is_lt),
            inc=(sV, 1), wait=[(sLD, 16 * (len(pieces) + i + 1)), (sV, NV2)])
        P.op("vector", lambda e, p=p, off=off: e.scalar_tensor_tensor(
            wq_sb[:, off:off + pw], wst[p][:, 0:pw], thr_col[:, 0:1], P.A[:, 0:pw],
            ALU.is_gt, ALU.subtract),
            inc=(sV, 1), wait=[(sV, NV2 + 2 * i + 1)])
    return P.cnt(sV)


L4_DBG = False


def phase_l4(P, dd):
    nc, ctx, tg = P.nc, P.ctx, P.tag
    NFC = FF // 128
    TB = 256
    NTB = S // TB
    xnfm = dd["xn_fm"]
    W1T = dd["W1T"]
    W2T = dd["W2T"]
    b1r = dd["b1r"]
    b2row = dd["b2row"]
    g_tm = dd["g_tm"]
    sb = lambda name, shape, dt: ctx.enter_context(nc.sbuf_tensor(tg + name, shape, dt))
    ps = lambda name, shape, dt: ctx.enter_context(nc.psum_tensor(tg + name, shape, dt))
    xn = sb("xn", [128, NDC * S], BF16)
    W1q = sb("W1q", [128, NDC * FF], BF16)
    W2q = sb("W2q", [128, NFC * D], BF16)
    h = sb("h", [128, NFC * TB], BF16)
    wst = [sb(f"wst{i}", [128, 2048], F32) for i in range(2)]
    A = sb("A", [128, 2048], BF16)
    P.A = A[:, :]
    acc1 = sb("acc1", [128, 1], F32)
    acc2 = sb("acc2", [128, 1], F32)
    part = sb("part", [128, 2], F32)
    ones_cf = sb("ones_cf", [128, 1], F32)
    ones_rf = sb("ones_rf", [1, 128], F32)
    ones1b = sb("ones1b", [1, 128], BF16)
    thr1 = sb("thr1", [128, 1], F32)
    nthr1 = sb("nthr1", [128, 1], F32)
    thr2 = sb("thr2", [128, 1], F32)
    nthr2 = sb("nthr2", [128, 1], F32)
    b1c = sb("b1c", [128, NFC], F32)
    b2b = sb("b2b", [1, D], BF16)
    g_sb = [sb(f"g_sb{i}", [128, D], F32) for i in range(2)]
    psG = ps("psG", [1, 1], F32)
    psB = ps("psB", [128, 1], F32)
    psH = [ps(f"psH{i}", [128, TB], F32) for i in range(2)]
    psO = [ps(f"psO{i}", [128, 512], F32) for i in range(2)]

    P.op("vector", lambda e: e.memset(acc1[:, :], 0.0), inc=("init", 1))
    P.op("vector", lambda e: e.memset(acc2[:, :], 0.0), inc=("init", 1))
    P.op("vector", lambda e: e.memset(ones_cf[:, :], 1.0), inc=("init", 1))
    P.op("vector", lambda e: e.memset(ones_rf[:, :], 1.0), inc=("init", 1))
    P.op("vector", lambda e: e.memset(ones1b[:, :], 1.0), inc=("init", 1))
    for dc in range(NDC):
        P.op("sync", lambda e, dc=dc: e.dma_start(out=xn[:, dc * S:(dc + 1) * S],
             in_=xnfm[dc * 128:(dc + 1) * 128, :]), inc=("xld", 16))
    P.op("sync", lambda e: e.dma_start(out=b1c[:, :], in_=b1r[:, :]), inc=("xld", 16))
    P.op("gpsimd", lambda e: e.dma_start(out=b2b[:, :], in_=b2row[:, :]), inc=("xld", 16))
    P.wait("vector", "init", 5)

    pieces1 = [(dc * 128, hf * 2048, dc * FF + hf * 2048)
               for dc in range(NDC) for hf in range(2)]
    q1_done = emit_quantize(P, W1T, W1q, pieces1, 2048, FF, acc1, part, wst,
                            ones_cf, ones_rf, thr1, nthr1, psG, psB, D * FF, "q1")
    pieces2 = [(fc * 128, 0, fc * D) for fc in range(NFC)]
    q2_done = emit_quantize(P, W2T, W2q, pieces2, 1024, D, acc2, part, wst,
                            ones_cf, ones_rf, thr2, nthr2, psG, psB, FF * D, "q2",
                            prev=("q1v", q1_done))

    if L4_DBG:
        P.op("sync", lambda e: e.dma_start(out=thr_dbg[0], in_=thr1[:, :]),
             inc=("dbg", 16), wait=[("q1v", q1_done), ("q2v", q2_done)])
        P.op("sync", lambda e: e.dma_start(out=thr_dbg[1], in_=thr2[:, :]), inc=("dbg", 16))
        P.op("gpsimd", lambda e: e.dma_start(out=w1q_dbg[:, :], in_=W1q[:, 0:FF]), inc=("dbg", 16))
        P.wait("gpsimd", "dbg", 48)
    XLD = 16 * (NDC + 2)
    for tb in range(NTB):
        for fc in range(NFC):
            pq = fc % 2
            gidx = tb * NFC + fc
            if gidx == 0:
                w = [("q1v", q1_done), ("xld", XLD)]
            else:
                w = [("hev", gidx - 1)]
            for dc in range(NDC):
                P.op("tensor", lambda e, fc=fc, dc=dc, tb=tb, pq=pq: e.matmul(
                    psH[pq][:, :], W1q[:, dc * FF + fc * 128: dc * FF + (fc + 1) * 128],
                    xn[:, dc * S + tb * TB: dc * S + (tb + 1) * TB],
                    start=(dc == 0), stop=(dc == NDC - 1)),
                    inc=("t", 1) if dc == NDC - 1 else None,
                    wait=w if dc == 0 else None)
            P.op("scalar", lambda e, fc=fc, pq=pq: e.activation(
                h[:, fc * TB:(fc + 1) * TB], psH[pq][:, :], AF.Relu,
                bias=b1c[:, fc:fc + 1]),
                inc=("hev", 1), wait=[("t", P.cnt("t"))])
        HEV = P.cnt("hev")
        if L4_DBG and tb == 0:
            for dcq in range(NDC):
                P.op("gpsimd", lambda e, dcq=dcq: e.dma_start(
                    out=h_dbg[:, dcq * 512:(dcq + 1) * 512],
                    in_=h[:, dcq * TB: dcq * TB + 512]), inc=("dbg2", 16),
                    wait=[("hev", HEV)])
            P.wait("gpsimd", "dbg2", 16 * NDC)
        for ts_ in range(TB // 128):
            t128 = tb * (TB // 128) + ts_
            po = t128 % 2
            for half in range(2):
                w2 = [("hev", HEV)]
                if t128 == 0 and half == 0:
                    w2.append(("q2v", q2_done))
                if half == 1:
                    w2 = [("gevh", 2 * t128 + 1)]
                elif t128 >= 2:
                    w2.append(("gevh", 2 * (t128 - 1)))
                for fc in range(NFC):
                    P.op("tensor", lambda e, fc=fc, half=half, ts_=ts_, po=po: e.matmul(
                        psO[po][:, :],
                        h[:, fc * TB + ts_ * 128: fc * TB + ts_ * 128 + 128],
                        W2q[:, fc * D + half * 512: fc * D + (half + 1) * 512],
                        start=(fc == 0), stop=False),
                        wait=w2 if fc == 0 else None)
                P.op("tensor", lambda e, half=half, po=po: e.matmul(
                    psO[po][:, :], ones1b[:, :],
                    b2b[0:1, half * 512:(half + 1) * 512], start=False, stop=True),
                    inc=("t", 1))
                TO2 = P.cnt("t")
                wg = [("t", TO2)]
                if t128 >= 2 and half == 0:
                    wg.append(("gst", 16 * (t128 - 1)))
                P.op("scalar", lambda e, half=half, po=po: e.activation(
                    g_sb[po][:, half * 512:(half + 1) * 512], psO[po][:, :], AF.Silu),
                    inc=("gevh", 1), wait=wg)
            P.op("sync", lambda e, t128=t128, po=po: e.dma_start(
                out=g_tm[t128 * 128:(t128 + 1) * 128, :], in_=g_sb[po][:, :]),
                inc=("gst", 16), wait=[("gevh", P.cnt("gevh"))])
    P.wait("gpsimd", "gst", 16 * NCH)


def build_l4():
    nc = bass.Bass("TRN2", target_bir_lowering=False, debug=False)
    dd = {}
    dd["xn_fm"] = nc.dram_tensor("xn_fm", [D, S], BF16, kind="ExternalInput").ap()
    dd["W1T"] = nc.dram_tensor("W1T", [D, FF], F32, kind="ExternalInput").ap()
    dd["W2T"] = nc.dram_tensor("W2T", [FF, D], F32, kind="ExternalInput").ap()
    dd["b1r"] = nc.dram_tensor("b1r", [128, FF // 128], F32, kind="ExternalInput").ap()
    dd["b2row"] = nc.dram_tensor("b2row", [1, D], F32, kind="ExternalInput").ap()
    dd["g_tm"] = nc.dram_tensor("g_tm", [S, D], F32, kind="ExternalOutput").ap()
    with ExitStack() as ctx:
        P = Prog(nc, ctx)
        phase_l4(P, dd)
        P.finalize()
    return nc



def phase_l5(P, dd):
    nc, ctx, tg = P.nc, P.ctx, P.tag
    x = dd["x"]
    s_tm = dd["s_tm"]
    g_tm = dd["g_tm"]
    identb = dd["ident_bf"]
    xp_tm = dd["xp_tm"]
    xn1_fm = dd["xn1_fm"]
    sbf = lambda name, shape, dt: ctx.enter_context(nc.sbuf_tensor(tg + name, shape, dt))
    ident_sb = sbf("ident_sb", [128, 128], BF16)
    xb = [sbf(f"xb{i}", [128, D], F32) for i in range(2)]
    sb_ = [sbf(f"sb_{i}", [128, D], F32) for i in range(2)]
    gb = [sbf(f"gb{i}", [128, D], F32) for i in range(2)]
    xp = [sbf(f"xp{i}", [128, D], F32) for i in range(2)]
    P.op("sync", lambda e: e.dma_start(out=ident_sb[:, :], in_=identb[:, :]), inc=("ild", 16))
    P.wait("tensor", "ild", 16)
    for tc in range(NCH):
        p = tc % 2
        w = [("cmb", 2 * (tc - 1))] if tc >= 2 else None
        P.op("sync", lambda e, p=p, tc=tc: e.dma_start(
            out=xb[p][:, :], in_=x[tc * 128:(tc + 1) * 128, :]), inc=("cld", 16), wait=w)
        P.op("sync", lambda e, p=p, tc=tc: e.dma_start(
            out=sb_[p][:, :], in_=s_tm[tc * 128:(tc + 1) * 128, :]), inc=("cld", 16))
        P.op("sync", lambda e, p=p, tc=tc: e.dma_start(
            out=gb[p][:, :], in_=g_tm[tc * 128:(tc + 1) * 128, :]), inc=("cld", 16))
        w = [("cld", 48 * (tc + 1))]
        if tc >= 2:
            w.append(("cst", 16 * (tc - 1)))
        P.op("vector", lambda e, p=p: e.tensor_mul(sb_[p][:, :], sb_[p][:, :], gb[p][:, :]),
             inc=("cmb", 1), wait=w)
        P.op("vector", lambda e, p=p: e.tensor_add(xp[p][:, :], xb[p][:, :], sb_[p][:, :]),
             inc=("cmb", 1), wait=[("cmb", 2 * tc + 1)])
        P.op("sync", lambda e, p=p, tc=tc: e.dma_start(
            out=xp_tm[tc * 128:(tc + 1) * 128, :], in_=xp[p][:, :]),
            inc=("cst", 16), wait=[("cmb", 2 * tc + 2)])
    _, sST = emit_ln_to_fm(P, xp_tm, xn1_fm, ident_sb, "ln1",
                           extra_load_wait=lambda tc: [("cst", 16 * (tc + 1))],
                           tm_out=dd.get("xn1_tm"))
    P.wait("gpsimd", sST, 16 * NDC)
    if dd.get("xn1_tm") is not None:
        P.wait("gpsimd", "ln1tmo", 16 * NCH)
    P.wait("gpsimd", "cst", 16 * NCH)


def build_l5():
    nc = bass.Bass("TRN2", target_bir_lowering=False, debug=False)
    dd = {}
    dd["x"] = nc.dram_tensor("x", [S, D], F32, kind="ExternalInput").ap()
    dd["s_tm"] = nc.dram_tensor("s_tm", [S, D], F32, kind="ExternalInput").ap()
    dd["g_tm"] = nc.dram_tensor("g_tm", [S, D], F32, kind="ExternalInput").ap()
    dd["ident_bf"] = nc.dram_tensor("ident_bf", [128, 128], BF16, kind="ExternalInput").ap()
    dd["xp_tm"] = nc.dram_tensor("xp_tm", [S, D], F32, kind="ExternalOutput").ap()
    dd["xn1_fm"] = nc.dram_tensor("xn1_fm", [D, S], BF16, kind="ExternalOutput").ap()
    dd["xn1_tm"] = nc.dram_tensor("xn1_tm", [S, D], BF16, kind="ExternalOutput").ap()
    with ExitStack() as ctx:
        P = Prog(nc, ctx)
        phase_l5(P, dd)
        P.finalize()
    return nc



def phase_l6(P, dd):
    nc, ctx, tg = P.nc, P.ctx, P.tag
    xnfm = dd["xn1_fm"]
    qw = dd["qw"]
    kw = dd["kw"]
    vw = dd["vw"]
    k_fm = dd["k1_fm"]
    q_fm = dd["q1_fm"]
    v_tm = dd["v1_tm"]
    sbf = lambda name, shape, dt: ctx.enter_context(nc.sbuf_tensor(tg + name, shape, dt))
    xn = sbf("xn", [128, NDC * S], BF16)
    wK = sbf("wK", [128, NDC * D], BF16)
    wQ = sbf("wQ", [128, NDC * D], BF16)
    wV = sbf("wV", [128, NDC * D], BF16)
    outK = sbf("outK", [128, NDC * S], BF16)
    outQ = sbf("outQ", [128, NDC * S], BF16)
    outV = sbf("outV", [128, NCH * D], BF16)
    pk = ctx.enter_context(nc.psum_tensor(tg + "pk", [128, 512], F32))
    pq = ctx.enter_context(nc.psum_tensor(tg + "pq", [128, 512], F32))
    pv = ctx.enter_context(nc.psum_tensor(tg + "pv", [128, 1024], F32))
    emit_proj_ttt(P, xnfm, kw, qw, vw, xn, wK, wQ, wV, outK, outQ, outV,
                  pk, pq, pv, "p6", vscale=1.0)
    nst = 0
    for dc in range(NDC):
        P.op("sync", lambda e, dc=dc: e.dma_start(
            out=k_fm[dc * 128:(dc + 1) * 128, :], in_=outK[:, dc * S:(dc + 1) * S]),
            inc=("p6st", 16), wait=[("p6evK", NDC * (S // 512))])
        P.op("sync", lambda e, dc=dc: e.dma_start(
            out=q_fm[dc * 128:(dc + 1) * 128, :], in_=outQ[:, dc * S:(dc + 1) * S]),
            inc=("p6st", 16), wait=[("p6evQ", NDC * (S // 512))])
        nst += 2
    for tc in range(NCH):
        P.op("sync", lambda e, tc=tc: e.dma_start(
            out=v_tm[tc * 128:(tc + 1) * 128, :], in_=outV[:, tc * D:(tc + 1) * D]),
            inc=("p6st", 16), wait=[("p6evV", NCH)])
        nst += 1
    P.wait("gpsimd", "p6st", 16 * nst)


def build_l6():
    nc = bass.Bass("TRN2", target_bir_lowering=False, debug=False)
    dd = {}
    dd["xn1_fm"] = nc.dram_tensor("xn1_fm", [D, S], BF16, kind="ExternalInput").ap()
    dd["qw"] = nc.dram_tensor("qw", [D, D], F32, kind="ExternalInput").ap()
    dd["kw"] = nc.dram_tensor("kw", [D, D], F32, kind="ExternalInput").ap()
    dd["vw"] = nc.dram_tensor("vw", [D, D], F32, kind="ExternalInput").ap()
    dd["k1_fm"] = nc.dram_tensor("k1_fm", [D, S], BF16, kind="ExternalOutput").ap()
    dd["q1_fm"] = nc.dram_tensor("q1_fm", [D, S], BF16, kind="ExternalOutput").ap()
    dd["v1_tm"] = nc.dram_tensor("v1_tm", [S, D], BF16, kind="ExternalOutput").ap()
    with ExitStack() as ctx:
        P = Prog(nc, ctx)
        phase_l6(P, dd)
        P.finalize()
    return nc



DBG_CROW = False
DBG_ATTN = False


def phase_l7(P, dd, nh_a=NH, nh_b=NH, do_c=True, nqb_b=4, bparts=7):
    nc, ctx, tg = P.nc, P.ctx, P.tag
    ratio = 2.0 ** (-8.0 / NH)
    slopes = [ratio ** (i + 1) for i in range(NH)]
    NQB = S // 512
    q_fm = dd["q1_fm"]
    k_fm = dd["k1_fm"]
    v_tm = dd["v1_tm"]
    ow = dd["ow"]
    iota_row = dd["iota_row"]
    biascols = dd["biascols"]
    madd = dd["madd_bf"]
    identb = dd["ident_bf"]
    crow_dram = dd["crow_scr"]
    s1_tm = dd["s1_tm"]
    sbf = lambda name, shape, dt: ctx.enter_context(nc.sbuf_tensor(tg + name, shape, dt))
    psf = lambda name, shape, dt: ctx.enter_context(nc.psum_tensor(tg + name, shape, dt))
    ka = [sbf(f"ka{i}", [64, S], BF16) for i in range(2)]
    qa = [sbf(f"qa{i}", [64, S], BF16) for i in range(2)]
    kaug = [sbf(f"kaug{i}", [65, S], BF16) for i in range(2)]
    qaug = [sbf(f"qaug{i}", [65, S], BF16) for i in range(2)]
    vt = sbf("vt", [128, NCH * D], BF16)
    owb = sbf("owb", [128, NDC * D], BF16)
    attn = sbf("attn", [128, NDC * S], BF16)
    iot = sbf("iot", [1, S], F32)
    bcols = sbf("bcols", [128, NH * NCH], F32)
    madd_sb = sbf("madd_sb", [128, 128], BF16)
    idb = sbf("idb", [128, 128], BF16)
    Q2 = sbf("Q2", [64, S], BF16)
    onesc64 = sbf("onesc64", [64, 1], BF16)
    onescol = sbf("onescol", [128, 1], BF16)
    ones64f = sbf("ones64f", [1, 64], F32)
    mx = sbf("mx", [1, 4], F32)
    qm2 = sbf("qm2", [1, 1], F32)
    km2 = sbf("km2", [1, 1], F32)
    c8 = sbf("c8", [1, 1], F32)
    crow = sbf("crow", [1, S], F32)
    lrow = sbf("lrow", [1, 512], F32)
    rlrow = sbf("rlrow", [1, 512], F32)
    pt = [sbf(f"pt{i}", [128, 512], BF16) for i in range(4)]
    Osb = sbf("Osb", [64, 512], BF16)
    s1sb = [sbf(f"s1sb{i}", [128, D], F32) for i in range(2)]
    psST = [psf(f"psST{i}", [128, 512], F32) for i in range(4)]
    psO = psf("psO", [64, 512], F32)
    psL = psf("psL", [1, 512], F32)
    psRL = psf("psRL", [64, 512], F32)

    nld = 0
    for tc in range(NCH):
        P.op("sync", lambda e, tc=tc: e.dma_start(
            out=vt[:, tc * D:(tc + 1) * D], in_=v_tm[tc * 128:(tc + 1) * 128, :]),
            inc=("ld", 16)); nld += 1
    for dc in range(NDC):
        P.op("gpsimd", lambda e, dc=dc: e.dma_start(
            out=owb[:, dc * D:(dc + 1) * D], in_=ow[dc * 128:(dc + 1) * 128, :]),
            inc=("ld", 16)); nld += 1
    for name, dst, src in (("iot", iot, iota_row), ("bcols", bcols, biascols),
                           ("madd", madd_sb, madd), ("idb", idb, identb)):
        P.op("sync", lambda e, d=dst, s_=src: e.dma_start(out=d[:, :], in_=s_[:, :]),
             inc=("ld", 16)); nld += 1
    LD = 16 * nld
    P.op("vector", lambda e: e.memset(kaug[0][64:65, :], 1.0), inc=("ms", 1))
    P.op("vector", lambda e: e.memset(kaug[1][64:65, :], 1.0), inc=("ms", 1))
    P.op("vector", lambda e: e.memset(onesc64[:, :], 1.0), inc=("ms", 1))
    P.op("vector", lambda e: e.memset(onescol[:, :], 1.0), inc=("ms", 1))
    P.op("vector", lambda e: e.memset(ones64f[:, :], 1.0), inc=("ms", 1))
    P.wait("vector", "ld", LD)

    ntA = []
    for h in range(nh_a):
        p = h % 2
        wl = [("nt", ntA[h - 2])] if h >= 2 else None
        P.op("sync", lambda e, h=h, p=p: e.dma_start(
            out=qa[p][:, :], in_=q_fm[h * 64:(h + 1) * 64, :]), inc=("lda", 16), wait=wl)
        P.op("sync", lambda e, h=h, p=p: e.dma_start(
            out=ka[p][:, :], in_=k_fm[h * 64:(h + 1) * 64, :]), inc=("lda", 16))
        for side, src_sb, dstm in ((0, qa[p], qm2), (1, ka[p], km2)):
            NT0 = P.cnt("nt")
            P.op("scalar", lambda e, s_=src_sb: e.activation(
                Q2[:, :], s_[:, :], AF.Square),
                inc=("ns", 1), wait=[("nt", NT0), ("lda", 32 * (h + 1)), ("ld", LD)])
            for blk in range(4):
                wmm = [("ns", P.cnt("ns")), ("ms", 5), ("nv", P.cnt("nv"))]
                P.op("tensor", lambda e, blk=blk: e.matmul(
                    psL[:, :], onesc64[:, :], Q2[:, blk * 512:(blk + 1) * 512],
                    start=True, stop=True), inc=("nt", 1), wait=wmm)
                P.op("vector", lambda e, blk=blk: e.reduce_max(
                    mx[0:1, blk:blk + 1], psL[:, :], AX),
                    inc=("nv", 1), wait=[("nt", P.cnt("nt"))])
            P.op("vector", lambda e, d=dstm: e.reduce_max(d[:, :], mx[0:1, 0:4], AX),
                 inc=("nv", 1), wait=[("nv", P.cnt("nv"))])
        ntA.append(P.cnt("nt"))
        P.op("vector", lambda e: e.tensor_mul(c8[:, :], qm2[:, :], km2[:, :]),
             inc=("nv", 1), wait=[("nv", P.cnt("nv"))])
        P.op("scalar", lambda e: e.activation(c8[:, :], c8[:, :], AF.Sqrt),
             inc=("ns", 1), wait=[("nv", P.cnt("nv"))])
        wc = [("ns", P.cnt("ns"))]
        if h >= 1:
            wc.append(("crst", 16 * h))
        P.op("vector", lambda e, h=h: e.tensor_scalar(
            crow[:, :], iot[:, :], -8.0 * slopes[h], c8[0:1, 0:1],
            ALU.mult, ALU.subtract),
            inc=("nv", 1), wait=wc)
        P.op("sync", lambda e, h=h: e.dma_start(
            out=crow_dram[h:h + 1, :], in_=crow[:, :]),
            inc=("crst", 16), wait=[("nv", P.cnt("nv"))])

    stmH = {}
    tile = 0
    for h in range(nh_b):
        p = h % 2
        wl = [("stm", stmH[h - 2])] if h >= 2 else None
        P.op("sync", lambda e, h=h, p=p: e.dma_start(
            out=qaug[p][0:64, :], in_=q_fm[h * 64:(h + 1) * 64, :]),
            inc=("ldb", 16), wait=wl)
        wcr = [("crst", 16 * (h + 1))]
        if h >= 2:
            wcr.append(("stm", stmH[h - 2]))
        P.op("gpsimd", lambda e, h=h, p=p: e.dma_start(
            out=qaug[p][64:65, :], in_=crow_dram[h:h + 1, :]),
            inc=("ldb", 16), wait=wcr)
        P.op("sync", lambda e, h=h, p=p: e.dma_start(
            out=kaug[p][0:64, :], in_=k_fm[h * 64:(h + 1) * 64, :]),
            inc=("ldb", 16))
        LDB = P.cnt("ldb")
        for qb in range(nqb_b):
            nkt = 4 * qb + 4
            for idx in range(nkt):
                kt = idx
                diag = kt >= 4 * qb
                off = 128 * (kt - 4 * qb) if diag else 0
                w = 512 - off
                pq = tile % 2
                ptb = tile % 2
                wst_ = [("ldb", LDB), ("ms", 5)]
                if tile == 0:
                    wst_ += [("ld", LD), ("lda", P.cnt("lda")), ("crst", P.cnt("crst")),
                             ("nv", P.cnt("nv")), ("ns", P.cnt("ns")), ("nt", P.cnt("nt"))]
                if tile >= 2:
                    wst_.append(("expv", tile - 1))
                P.op("tensor", lambda e, h=h, qb=qb, kt=kt, off=off, w=w, pq=pq, p=p:
                     e.matmul(psST[pq][:, 0:w],
                              kaug[p][:, kt * 128:(kt + 1) * 128],
                              qaug[p][:, qb * 512 + off:(qb + 1) * 512],
                              start=True, stop=not diag),
                     inc=("stm", 1) if not diag else None, wait=wst_)
                if diag:
                    if bparts & 1:
                        P.op("tensor", lambda e, pq=pq: e.matmul(
                            psST[pq][:, 0:128], idb[:, :], madd_sb[:, :],
                            start=False, stop=True), inc=("stm", 1))
                    else:
                        P.op("tensor", lambda e, pq=pq: e.matmul(
                            psST[pq][:, 0:128], idb[:, :], idb[:, :],
                            start=False, stop=True), inc=("stm", 1))
                wexp = [("stm", P.cnt("stm"))]
                if tile >= 2:
                    wexp.append(("ol", tile - 1))
                P.op("scalar", lambda e, h=h, kt=kt, w=w, pq=pq, ptb=ptb:
                     e.activation(pt[ptb][:, 0:w], psST[pq][:, 0:w], AF.Exp,
                                  bias=bcols[:, h * NCH + kt: h * NCH + kt + 1],
                                  scale=0.125),
                     inc=("expv", 1), wait=wexp)
                wo = [("expv", P.cnt("expv"))]
                if idx == 0:
                    wo.append(("osb", P.cnt("osb")))
                    wo.append(("lns", P.cnt("lns")))
                if bparts & 2:
                    P.op("tensor", lambda e, h=h, kt=kt, off=off, w=w, ptb=ptb, idx=idx,
                         nkt=nkt: e.matmul(
                             psO[0:64, off:512], vt[:, kt * D + h * 64: kt * D + h * 64 + 64],
                             pt[ptb][:, 0:w], start=(idx == 0), stop=(idx == nkt - 1)),
                         wait=wo)
                    P.op("tensor", lambda e, off=off, w=w, ptb=ptb, idx=idx, nkt=nkt:
                         e.matmul(psL[0:1, off:512], onescol[:, :], pt[ptb][:, 0:w],
                                  start=(idx == 0), stop=(idx == nkt - 1)),
                         inc=("ol", 1))
                else:
                    P.op("vector", lambda e, ptb=ptb: e.tensor_copy(Osb[:, 0:4], pt[ptb][:, 0:4][0:64, :]) if False else e.engine_nop(), inc=("ol", 1), wait=wo)
                tile += 1
            if not (bparts & 4):
                OL = P.cnt("ol")
                P.op("vector", lambda e: e.engine_nop(), inc=("atn", 1), wait=[("ol", OL)])
                P.op("vector", lambda e: e.engine_nop(), inc=("osb", 1))
                P.op("vector", lambda e: e.engine_nop(), inc=("lns", 1))
                P.op("vector", lambda e: e.engine_nop(), inc=("lns", 1))
                continue
            OL = P.cnt("ol")
            P.op("scalar", lambda e: e.activation(lrow[:, :], psL[0:1, :], AF.Ln),
                 inc=("lns", 1), wait=[("ol", OL)])
            P.op("scalar", lambda e: e.activation(rlrow[:, :], lrow[:, :], AF.Exp,
                 scale=-1.0), inc=("lns", 1))
            wb_ = [("lns", P.cnt("lns")), ("atn", P.cnt("atn"))]
            P.op("tensor", lambda e: e.matmul(psRL[:, :], ones64f[:, :], rlrow[:, :],
                 start=True, stop=True), inc=("ntb", 1), wait=wb_)
            P.op("vector", lambda e: e.tensor_copy(Osb[:, :], psO[0:64, :]),
                 inc=("osb", 1), wait=[("ol", OL)])
            p0 = 64 * (h % 2)
            fo = (h // 2) * S + qb * 512
            if bparts & 8:
                P.op("vector", lambda e, p0=p0, fo=fo: e.tensor_copy(
                    attn[p0:p0 + 64, fo:fo + 512], Osb[:, :]),
                    inc=("atn", 1), wait=[("osb", P.cnt("osb")), ("ntb", P.cnt("ntb"))])
            else:
                P.op("vector", lambda e, p0=p0, fo=fo: e.tensor_mul(
                    attn[p0:p0 + 64, fo:fo + 512], Osb[:, :], psRL[:, :]),
                    inc=("atn", 1), wait=[("osb", P.cnt("osb")), ("ntb", P.cnt("ntb"))])
        stmH[h] = P.cnt("stm")

    if DBG_ATTN:
        for dc in range(NDC):
            P.op("sync", lambda e, dc=dc: e.dma_start(
                out=attn_dbg[dc * 128:(dc + 1) * 128, :], in_=attn[:, dc * S:(dc + 1) * S]),
                inc=("adb", 16), wait=[("atn", P.cnt("atn"))])
        P.wait("gpsimd", "adb", 16 * NDC)
    ATN = P.cnt("atn")
    EXPALL = P.cnt("expv")
    for t128 in range(NCH if do_c else 0):
        po = t128 % 2
        for half in range(2):
            wc = [("atn", ATN), ("expv", EXPALL)]
            if t128 >= 2:
                wc.append(("s1e", 2 * (t128 - 1)))
            for dc in range(NDC):
                P.op("tensor", lambda e, dc=dc, t128=t128, half=half: e.matmul(
                    psST[half][:, :], attn[:, dc * S + t128 * 128: dc * S + (t128 + 1) * 128],
                    owb[:, dc * D + half * 512: dc * D + (half + 1) * 512],
                    start=(dc == 0), stop=(dc == NDC - 1)),
                    inc=("ct", 1) if dc == NDC - 1 else None,
                    wait=wc if dc == 0 else None)
            we = [("ct", P.cnt("ct"))]
            if t128 >= 2 and half == 0:
                we.append(("s1st", 16 * (t128 - 1)))
            P.op("vector", lambda e, po=po, half=half: e.tensor_copy(
                s1sb[po][:, half * 512:(half + 1) * 512], psST[half][:, :]),
                inc=("s1e", 1), wait=we)
        P.op("sync", lambda e, t128=t128, po=po: e.dma_start(
            out=s1_tm[t128 * 128:(t128 + 1) * 128, :], in_=s1sb[po][:, :]),
            inc=("s1st", 16), wait=[("s1e", P.cnt("s1e"))])
    if do_c:
        P.wait("gpsimd", "s1st", 16 * NCH)
    elif nh_b > 0:
        P.wait("gpsimd", "atn", P.cnt("atn"))
    else:
        P.wait("gpsimd", "crst", 16 * nh_a)


def build_l7(nh_a=NH, nh_b=NH, do_c=True, nqb_b=4, bparts=7):
    nc = bass.Bass("TRN2", target_bir_lowering=False, debug=False)
    dd = {}
    dd["q1_fm"] = nc.dram_tensor("q1_fm", [D, S], BF16, kind="ExternalInput").ap()
    dd["k1_fm"] = nc.dram_tensor("k1_fm", [D, S], BF16, kind="ExternalInput").ap()
    dd["v1_tm"] = nc.dram_tensor("v1_tm", [S, D], BF16, kind="ExternalInput").ap()
    dd["ow"] = nc.dram_tensor("ow", [D, D], F32, kind="ExternalInput").ap()
    dd["iota_row"] = nc.dram_tensor("iota_row", [1, S], F32, kind="ExternalInput").ap()
    dd["biascols"] = nc.dram_tensor("biascols", [128, NH * NCH], F32, kind="ExternalInput").ap()
    dd["madd_bf"] = nc.dram_tensor("madd_bf", [128, 128], BF16, kind="ExternalInput").ap()
    dd["ident_bf"] = nc.dram_tensor("ident_bf", [128, 128], BF16, kind="ExternalInput").ap()
    dd["crow_scr"] = nc.dram_tensor("crow_scr", [NH, S], F32, kind="Internal" if not DBG_CROW else "ExternalOutput").ap()
    dd["s1_tm"] = nc.dram_tensor("s1_tm", [S, D], F32, kind="ExternalOutput").ap()
    with ExitStack() as ctx:
        P = Prog(nc, ctx)
        phase_l7(P, dd, nh_a, nh_b, do_c, nqb_b, bparts)
        P.finalize()
    return nc



L8_SERIAL = False


def phase_l8(P, dd, nexp=NE):
    nc, ctx, tg = P.nc, P.ctx, P.tag
    IH = FF // 2
    NIC = IH // 128
    xnfm = dd["xn1_fm"]
    WgT = dd["WgT"]
    W1 = dd["W1m"]
    b1r = dd["mb1r"]
    W2 = dd["W2m"]
    b2 = dd["b2m"]
    revI = dd["revI"]
    identb = dd["ident_bf"]
    y_tm = dd["y_tm"]
    sbf = lambda name, shape, dt: ctx.enter_context(nc.sbuf_tensor(tg + name, shape, dt))
    psf = lambda name, shape, dt: ctx.enter_context(nc.psum_tensor(tg + name, shape, dt))
    xn = sbf("xn", [128, NDC * S], BF16)
    y = sbf("y", [128, NCH * D], F32)
    W1h = sbf("W1h", [128, NDC * IH], BF16)
    W2h = sbf("W2h", [128, NIC * D], BF16)
    hsb = sbf("hsb", [128, NIC * 512], BF16)
    wg = sbf("wg", [128, NDC * NE], BF16)
    b2b = sbf("b2b", [NE, D], BF16)
    b1all = sbf("b1all", [128, NE * (FF // 128)], F32)
    revI_sb = sbf("revI_sb", [128, NE], F32)
    idb = sbf("idb", [128, 128], BF16)
    gates = sbf("gates", [128, NCH * NE], F32)
    gatesb = sbf("gatesb", [128, NE], BF16)
    gatesT = sbf("gatesT", [NE, S], BF16)
    lg = sbf("lg", [128, NE], F32)
    m1 = sbf("m1", [128, 1], F32)
    eq = sbf("eq", [128, NE], F32)
    s1_ = sbf("s1_", [128, NE], F32)
    rm = sbf("rm", [128, 1], F32)
    is1 = sbf("is1", [128, NE], F32)
    is2 = sbf("is2", [128, NE], F32)
    msk = sbf("msk", [128, NE], F32)
    m2 = sbf("m2", [128, 1], F32)
    tt = sbf("tt", [128, 1], F32)
    den = sbf("den", [128, 1], F32)
    w1c = sbf("w1c", [128, 1], F32)
    w2c = sbf("w2c", [128, 1], F32)
    psGt = psf("psGt", [128, NE], F32)
    psTr = psf("psTr", [NE, 128], BF16)
    psH = [psf(f"psH{i}", [128, 512], F32) for i in range(2)]
    psO2 = [psf(f"psO2{i}", [128, 1024], F32) for i in range(2)]

    nld = 0
    for dc in range(NDC):
        P.op("sync", lambda e, dc=dc: e.dma_start(out=xn[:, dc * S:(dc + 1) * S],
             in_=xnfm[dc * 128:(dc + 1) * 128, :]), inc=("ld", 16)); nld += 1
        P.op("gpsimd", lambda e, dc=dc: e.dma_start(out=wg[:, dc * NE:(dc + 1) * NE],
             in_=WgT[dc * 128:(dc + 1) * 128, :]), inc=("ld", 16)); nld += 1
    P.op("gpsimd", lambda e: e.dma_start(out=b2b[:, :], in_=b2[:, :]), inc=("ld", 16)); nld += 1
    for e_ in range(NE):
        P.op("sync", lambda e, e_=e_: e.dma_start(
            out=b1all[:, e_ * 32:(e_ + 1) * 32], in_=b1r[e_]), inc=("ld", 16)); nld += 1
    P.op("sync", lambda e: e.dma_start(out=revI_sb[:, :], in_=revI[:, :]), inc=("ld", 16)); nld += 1
    P.op("sync", lambda e: e.dma_start(out=idb[:, :], in_=identb[:, :]), inc=("ld", 16)); nld += 1
    LD = 16 * nld

    for tc in range(NCH):
        wg0 = [("ld", LD)]
        if tc > 0:
            wg0.append(("gv", P.cnt("gv")))
        for dc in range(NDC):
            P.op("tensor", lambda e, dc=dc, tc=tc: e.matmul(
                psGt[:, :], xn[:, dc * S + tc * 128: dc * S + (tc + 1) * 128],
                wg[:, dc * NE:(dc + 1) * NE], start=(dc == 0), stop=(dc == NDC - 1)),
                inc=("gt", 1) if dc == NDC - 1 else None, wait=wg0 if dc == 0 else None)
        def chain(fn, eng="vector"):
            P.op(eng, fn, inc=("gv", 1),
                 wait=[("gv", P.cnt("gv")), ("gs", P.cnt("gs")), ("gt", P.cnt("gt"))])
        chain(lambda e: e.tensor_copy(lg[:, :], psGt[:, :]))
        chain(lambda e: e.reduce_max(m1[:, :], lg[:, :], AX))
        chain(lambda e: e.tensor_single_scalar(eq[:, :], lg[:, :], m1[:, 0:1], ALU.is_ge))
        chain(lambda e: e.tensor_mul(s1_[:, :], eq[:, :], revI_sb[:, :]))
        chain(lambda e: e.reduce_max(rm[:, :], s1_[:, :], AX))
        chain(lambda e: e.tensor_single_scalar(is1[:, :], s1_[:, :], rm[:, 0:1], ALU.is_ge))
        chain(lambda e: e.scalar_tensor_tensor(msk[:, :], is1[:, :], NEG, lg[:, :],
                                               ALU.mult, ALU.add))
        chain(lambda e: e.reduce_max(m2[:, :], msk[:, :], AX))
        chain(lambda e: e.tensor_single_scalar(eq[:, :], msk[:, :], m2[:, 0:1], ALU.is_ge))
        chain(lambda e: e.tensor_mul(s1_[:, :], eq[:, :], revI_sb[:, :]))
        chain(lambda e: e.reduce_max(rm[:, :], s1_[:, :], AX))
        chain(lambda e: e.tensor_single_scalar(is2[:, :], s1_[:, :], rm[:, 0:1], ALU.is_ge))
        chain(lambda e: e.tensor_sub(m2[:, :], m2[:, :], m1[:, :]))
        chain(lambda e: e.activation(tt[:, :], m2[:, :], AF.Exp), eng="scalar")
        chain(lambda e: e.tensor_scalar_add(den[:, :], tt[:, :], 1.0))
        chain(lambda e: e.activation(den[:, :], den[:, :], AF.Ln), eng="scalar")
        chain(lambda e: e.activation(w1c[:, :], den[:, :], AF.Exp, scale=-1.0), eng="scalar")
        chain(lambda e: e.tensor_mul(w2c[:, :], tt[:, :], w1c[:, :]))
        chain(lambda e, tc=tc: e.tensor_scalar(
            gates[:, tc * NE:(tc + 1) * NE], is1[:, :], w1c[:, 0:1], None, ALU.mult))
        chain(lambda e, tc=tc: e.scalar_tensor_tensor(
            gates[:, tc * NE:(tc + 1) * NE], is2[:, :], w2c[:, 0:1],
            gates[:, tc * NE:(tc + 1) * NE], ALU.mult, ALU.add))
        chain(lambda e, tc=tc: e.tensor_copy(gatesb[:, :], gates[:, tc * NE:(tc + 1) * NE]))
        P.op("tensor", lambda e, tc=tc: e.transpose(psTr[:, :], gatesb[:, :], idb[:, :]),
             inc=("gt", 1), wait=[("gv", P.cnt("gv")), ("gtr", P.cnt("gtr"))])
        P.op("vector", lambda e, tc=tc: e.tensor_copy(
            gatesT[:, tc * 128:(tc + 1) * 128], psTr[0:NE, 0:128]),
            inc=("gtr", 1), wait=[("gt", P.cnt("gt"))])
    GTR = P.cnt("gtr")
    GV = P.cnt("gv")

    for tc in range(NCH):
        wy = [("gtr", GTR)]
        if tc >= 2:
            wy.append(("yi", tc - 1))
        for half in range(2):
            P.op("tensor", lambda e, tc=tc, half=half: e.matmul(
                psO2[tc % 2][:, half * 512:(half + 1) * 512],
                gatesT[:, tc * 128:(tc + 1) * 128],
                b2b[:, half * 512:(half + 1) * 512], start=True, stop=True),
                inc=("gt", 1) if half == 1 else None, wait=wy if half == 0 else None)
        P.op("vector", lambda e, tc=tc: e.tensor_copy(
            y[:, tc * D:(tc + 1) * D], psO2[tc % 2][:, :]),
            inc=("yi", 1), wait=[("gt", P.cnt("gt"))])
    YI = P.cnt("yi")

    rnd = 0
    for e_ in range(nexp):
        for ih in range(2):
            wld = []
            if rnd > 0:
                wld = [("t8", P.cnt("t8"))]
            for dc in range(NDC):
                P.op("gpsimd", lambda e, e_=e_, ih=ih, dc=dc: e.dma_start(
                    out=W1h[:, dc * IH:(dc + 1) * IH],
                    in_=W1[e_, dc * 128:(dc + 1) * 128, ih * IH:(ih + 1) * IH]),
                    inc=("wld", 16), wait=wld if dc == 0 else None)
            for ic in range(NIC):
                P.op("gpsimd", lambda e, e_=e_, ih=ih, ic=ic: e.dma_start(
                    out=W2h[:, ic * D:(ic + 1) * D],
                    in_=W2[e_, ih * IH + ic * 128: ih * IH + (ic + 1) * 128, :]),
                    inc=("wld", 16))
            WLD = P.cnt("wld")
            for tb in range(4):
                if L8_SERIAL:
                    allw = [(s, P.cnt(s)) for s in ("t8", "hev", "yst", "yi", "wld")]
                    for s, v in allw:
                        P.wait("tensor", s, v); P.wait("vector", s, v); P.wait("scalar", s, v)
                for ic in range(NIC):
                    pq = ic % 2
                    gidx = rnd * 64 + tb * NIC + ic
                    w1_ = [("wld", WLD), ("ld", LD)]
                    if P.cnt("hev") >= 1:
                        w1_.append(("hev", P.cnt("hev") - 1))
                    for dc in range(NDC):
                        P.op("tensor", lambda e, dc=dc, ic=ic, tb=tb, pq=pq: e.matmul(
                            psH[pq][:, :],
                            W1h[:, dc * IH + ic * 128: dc * IH + (ic + 1) * 128],
                            xn[:, dc * S + tb * 512: dc * S + (tb + 1) * 512],
                            start=(dc == 0), stop=(dc == NDC - 1)),
                            inc=("t8", 1) if dc == NDC - 1 else None,
                            wait=w1_ if dc == 0 else None)
                    P.op("scalar", lambda e, ic=ic, pq=pq, e_=e_, ih=ih: e.activation(
                        hsb[:, ic * 512:(ic + 1) * 512], psH[pq][:, :], AF.Gelu,
                        bias=b1all[:, e_ * 32 + ih * 16 + ic: e_ * 32 + ih * 16 + ic + 1]),
                        inc=("hev", 1), wait=[("t8", P.cnt("t8"))])
                HEV = P.cnt("hev")
                for ts_ in range(4):
                    if L8_SERIAL:
                        for s in ("t8", "hev", "yst", "yi"):
                            v = P.cnt(s)
                            P.wait("tensor", s, v); P.wait("vector", s, v)
                    t128 = tb * 4 + ts_
                    po = t128 % 2
                    w2_ = [("hev", HEV), ("yi", YI)]
                    if P.cnt("yst") >= 1:
                        w2_.append(("yst", P.cnt("yst") - 1))
                    for half in range(2):
                        for ic in range(NIC):
                            P.op("tensor", lambda e, ic=ic, half=half, ts_=ts_, po=po,
                                 tb=tb: e.matmul(
                                    psO2[po][:, half * 512:(half + 1) * 512],
                                    hsb[:, ic * 512 + ts_ * 128: ic * 512 + ts_ * 128 + 128],
                                    W2h[:, ic * D + half * 512: ic * D + (half + 1) * 512],
                                    start=(ic == 0), stop=(ic == NIC - 1)),
                                inc=("t8", 1) if ic == NIC - 1 else None,
                                wait=w2_ if (half == 0 and ic == 0) else None)
                    prev_same = (rnd - 1) * 16 + (tb * 4 + ts_) + 1
                    wy2 = [("t8", P.cnt("t8")), ("yi", YI)]
                    if rnd > 0:
                        wy2.append(("yst", prev_same))
                    P.op("vector", lambda e, t128=t128, po=po, e_=e_: e.scalar_tensor_tensor(
                        y[:, t128 * D:(t128 + 1) * D], psO2[po][:, :],
                        gates[:, t128 * NE + e_: t128 * NE + e_ + 1],
                        y[:, t128 * D:(t128 + 1) * D], ALU.mult, ALU.add),
                        inc=("yst", 1), wait=wy2)
            rnd += 1
    YST = P.cnt("yst") if nexp > 0 else 0
    for tc in range(NCH):
        P.op("sync", lambda e, tc=tc: e.dma_start(
            out=y_tm[tc * 128:(tc + 1) * 128, :], in_=y[:, tc * D:(tc + 1) * D]),
            inc=("yout", 16), wait=[("yst", YST), ("yi", YI)])
    P.wait("gpsimd", "yout", 16 * NCH)


def build_l8(nexp=NE):
    nc = bass.Bass("TRN2", target_bir_lowering=False, debug=False)
    dd = {}
    dd["xn1_fm"] = nc.dram_tensor("xn1_fm", [D, S], BF16, kind="ExternalInput").ap()
    dd["WgT"] = nc.dram_tensor("WgT", [D, NE], F32, kind="ExternalInput").ap()
    dd["W1m"] = nc.dram_tensor("W1m", [NE, D, FF], F32, kind="ExternalInput").ap()
    dd["mb1r"] = nc.dram_tensor("mb1r", [NE, 128, FF // 128], F32, kind="ExternalInput").ap()
    dd["W2m"] = nc.dram_tensor("W2m", [NE, FF, D], F32, kind="ExternalInput").ap()
    dd["b2m"] = nc.dram_tensor("b2m", [NE, D], F32, kind="ExternalInput").ap()
    dd["revI"] = nc.dram_tensor("revI", [128, NE], F32, kind="ExternalInput").ap()
    dd["ident_bf"] = nc.dram_tensor("ident_bf", [128, 128], BF16, kind="ExternalInput").ap()
    dd["y_tm"] = nc.dram_tensor("y_tm", [S, D], F32, kind="ExternalOutput").ap()
    with ExitStack() as ctx:
        P = Prog(nc, ctx)
        phase_l8(P, dd, nexp)
        P.finalize()
    return nc



def phase_l8s(P, dd):
    from concourse.bass_core_rust import IndirectOffsetOnAxis as IOA
    I32 = mybir.dt.int32
    nc, ctx, tg = P.nc, P.ctx, P.tag
    IH = FF // 2
    NIC = IH // 128
    CAP = 1024
    NSC = CAP // 128
    BIG = 1.0e6
    xnfm = dd["xn1_fm"]
    xn_tm_d = dd["xn1_tm"]
    WgT = dd["WgT"]
    W1 = dd["W1m"]
    b1r = dd["mb1r"]
    W2 = dd["W2m"]
    b2 = dd["b2m"]
    revI = dd["revI"]
    identb = dd["ident_bf"]
    identf = dd["ident_f"]
    tokid_d = dd["tokid"]
    eidx_d = dd["eidx"]
    y_tm = dd["y_tm"]
    idxscr = dd["idxscr"]
    gates_d = dd["gates_scr"]
    sbf = lambda name, shape, dt: ctx.enter_context(nc.sbuf_tensor(tg + name, shape, dt))
    psf = lambda name, shape, dt: ctx.enter_context(nc.psum_tensor(tg + name, shape, dt))
    xnt = [sbf(f"xnt{i}", [128, NDC * 128], BF16) for i in range(2)]
    W1h = sbf("W1h", [128, NDC * IH], BF16)
    W2h = sbf("W2h", [128, NIC * D], BF16)
    hsb = sbf("hsb", [128, NIC * CAP], BF16)
    xe_tm = sbf("xe_tm", [128, NSC * D], BF16)
    xe_fm = sbf("xe_fm", [128, NDC * CAP], BF16)
    oacc = sbf("oacc", [128, NSC * D], F32)
    oscat = sbf("oscat", [128, D], F32)
    wg = sbf("wg", [128, NDC * NE], BF16)
    b2b = sbf("b2b", [NE, D], BF16)
    b1all = sbf("b1all", [128, NE * (FF // 128)], F32)
    revI_sb = sbf("revI_sb", [128, NE], F32)
    idb = sbf("idb", [128, 128], BF16)
    idf8 = sbf("idf8", [8, 8], F32)
    tokid = sbf("tokid_s", [128, NCH], F32)
    eidx = sbf("eidx_s", [128, NE], F32)
    gates = sbf("gates", [128, NCH * NE], F32)
    gatesb = sbf("gatesb", [128, NE], BF16)
    gatesT = sbf("gatesT", [NE, S], BF16)
    is12 = sbf("is12", [128, NCH * 2 * NE], F32)
    iseb = sbf("iseb", [128, NE], BF16)
    iseT = sbf("iseT", [NE, S], F32)
    posTa = sbf("posTa", [NE, S], F32)
    pos_tm = sbf("pos_tm", [128, NCH * NE], F32)
    offc = sbf("offc", [128, 4], F32)
    offi = sbf("offi", [128, 2], I32)
    fill = sbf("fill", [128, NE * CAP // 128], F32)
    idxf = sbf("idxf", [128, NSC], F32)
    idxi = sbf("idxi", [128, NSC], I32)
    idxs_ = sbf("idxs_", [128, NSC], I32)
    gsl = sbf("gsl", [128, NSC * NE], F32)
    lg = sbf("lg", [128, NE], F32)
    m1 = sbf("m1", [128, 1], F32)
    eq = sbf("eq", [128, NE], F32)
    s1_ = sbf("s1_", [128, NE], F32)
    rm = sbf("rm", [128, 1], F32)
    is1 = sbf("is1", [128, NE], F32)
    is2 = sbf("is2", [128, NE], F32)
    msk = sbf("msk", [128, NE], F32)
    m2 = sbf("m2", [128, 1], F32)
    tt = sbf("tt", [128, 1], F32)
    den = sbf("den", [128, 1], F32)
    w1c = sbf("w1c", [128, 1], F32)
    w2c = sbf("w2c", [128, 1], F32)
    psGt = psf("psGt", [128, NE], F32)
    psTr = psf("psTr", [NE, 128], BF16)
    psTP = psf("psTP", [128, 128], BF16)
    psPM = psf("psPM", [128, NE], F32)
    psH = [psf(f"psH{i}", [128, 512], F32) for i in range(2)]
    psO2 = psf("psO2", [128, 1024], F32)

    nld = 0
    for dc in range(NDC):
        P.op("gpsimd", lambda e, dc=dc: e.dma_start(out=wg[:, dc * NE:(dc + 1) * NE],
             in_=WgT[dc * 128:(dc + 1) * 128, :]), inc=("ld", 16)); nld += 1
    P.op("gpsimd", lambda e: e.dma_start(out=b2b[:, :], in_=b2[:, :]), inc=("ld", 16)); nld += 1
    for e_ in range(NE):
        P.op("sync", lambda e, e_=e_: e.dma_start(
            out=b1all[:, e_ * 32:(e_ + 1) * 32], in_=b1r[e_]), inc=("ld", 16)); nld += 1
    for nm, dst, srcd in (("revI", revI_sb, revI[:, :]), ("idb", idb, identb[:, :]),
                          ("tokid", tokid, tokid_d[:, :]), ("eidx", eidx, eidx_d[:, :]),
                          ("idf8", idf8, identf[0:8, 0:8])):
        P.op("sync", lambda e, d=dst, s_=srcd: e.dma_start(out=d[:, :], in_=s_),
             inc=("ld", 16)); nld += 1
    LD = 16 * nld
    P.op("vector", lambda e: e.memset(fill[:, :], BIG), inc=("gv", 1))
    P.op("sync", lambda e: e.dma_start(
        out=idxscr.rearrange("(p c) one -> p (c one)", p=128), in_=fill[:, :]),
        inc=("pf", 16), wait=[("gv", 1)])
    FST = P.cnt("pf")

    for tc in range(NCH):
        p_ = tc % 2
        wxl = [("t8", P.cnt("t8"))] if tc >= 2 else None
        for dc in range(NDC):
            P.op("sync", lambda e, dc=dc, tc=tc, p_=p_: e.dma_start(
                out=xnt[p_][:, dc * 128:(dc + 1) * 128],
                in_=xnfm[dc * 128:(dc + 1) * 128, tc * 128:(tc + 1) * 128]),
                inc=("d8", 16), wait=wxl if dc == 0 else None)
        wg0 = [("ld", LD), ("d8", P.cnt("d8"))]
        if tc > 0:
            wg0.append(("gv", P.cnt("gv")))
        for dc in range(NDC):
            P.op("tensor", lambda e, dc=dc, tc=tc, p_=p_: e.matmul(
                psGt[:, :], xnt[p_][:, dc * 128:(dc + 1) * 128],
                wg[:, dc * NE:(dc + 1) * NE], start=(dc == 0), stop=(dc == NDC - 1)),
                inc=("t8", 1) if dc == NDC - 1 else None, wait=wg0 if dc == 0 else None)
        def chain(fn, eng="vector"):
            P.op(eng, fn, inc=("gv", 1),
                 wait=[("gv", P.cnt("gv")), ("s8", P.cnt("s8")), ("t8", P.cnt("t8"))])
        chain(lambda e: e.tensor_copy(lg[:, :], psGt[:, :]))
        chain(lambda e: e.reduce_max(m1[:, :], lg[:, :], AX))
        chain(lambda e: e.tensor_single_scalar(eq[:, :], lg[:, :], m1[:, 0:1], ALU.is_ge))
        chain(lambda e: e.tensor_mul(s1_[:, :], eq[:, :], revI_sb[:, :]))
        chain(lambda e: e.reduce_max(rm[:, :], s1_[:, :], AX))
        chain(lambda e: e.tensor_single_scalar(is1[:, :], s1_[:, :], rm[:, 0:1], ALU.is_ge))
        chain(lambda e: e.scalar_tensor_tensor(msk[:, :], is1[:, :], NEG, lg[:, :],
                                               ALU.mult, ALU.add))
        chain(lambda e: e.reduce_max(m2[:, :], msk[:, :], AX))
        chain(lambda e: e.tensor_single_scalar(eq[:, :], msk[:, :], m2[:, 0:1], ALU.is_ge))
        chain(lambda e: e.tensor_mul(s1_[:, :], eq[:, :], revI_sb[:, :]))
        chain(lambda e: e.reduce_max(rm[:, :], s1_[:, :], AX))
        chain(lambda e: e.tensor_single_scalar(is2[:, :], s1_[:, :], rm[:, 0:1], ALU.is_ge))
        chain(lambda e: e.tensor_sub(m2[:, :], m2[:, :], m1[:, :]))
        chain(lambda e: e.activation(tt[:, :], m2[:, :], AF.Exp), eng="scalar")
        chain(lambda e: e.tensor_scalar_add(den[:, :], tt[:, :], 1.0))
        chain(lambda e: e.activation(den[:, :], den[:, :], AF.Ln), eng="scalar")
        chain(lambda e: e.activation(w1c[:, :], den[:, :], AF.Exp, scale=-1.0), eng="scalar")
        chain(lambda e: e.tensor_mul(w2c[:, :], tt[:, :], w1c[:, :]))
        chain(lambda e, tc=tc: e.tensor_scalar(
            gates[:, tc * NE:(tc + 1) * NE], is1[:, :], w1c[:, 0:1], None, ALU.mult))
        chain(lambda e, tc=tc: e.scalar_tensor_tensor(
            gates[:, tc * NE:(tc + 1) * NE], is2[:, :], w2c[:, 0:1],
            gates[:, tc * NE:(tc + 1) * NE], ALU.mult, ALU.add))
        chain(lambda e, tc=tc: e.tensor_copy(is12[:, tc * 16: tc * 16 + 8], is1[:, :]))
        chain(lambda e, tc=tc: e.tensor_copy(is12[:, tc * 16 + 8: tc * 16 + 16], is2[:, :]))
        chain(lambda e: e.tensor_add(iseb[:, :], is1[:, :], is2[:, :]))
        chain(lambda e, tc=tc: e.tensor_copy(gatesb[:, :], gates[:, tc * NE:(tc + 1) * NE]))
        P.op("sync", lambda e, tc=tc: e.dma_start(
            out=gates_d[tc * 128:(tc + 1) * 128, :], in_=gates[:, tc * NE:(tc + 1) * NE]),
            inc=("d8", 16), wait=[("gv", P.cnt("gv"))])
        P.op("tensor", lambda e: e.transpose(psTr[:, :], gatesb[:, :], idb[:, :]),
             inc=("t8", 1), wait=[("gv", P.cnt("gv")), ("v8", P.cnt("v8"))])
        P.op("vector", lambda e, tc=tc: e.tensor_copy(
            gatesT[:, tc * 128:(tc + 1) * 128], psTr[0:NE, 0:128]),
            inc=("v8", 1), wait=[("t8", P.cnt("t8"))])
        P.op("tensor", lambda e: e.transpose(psTr[:, :], iseb[:, :], idb[:, :]),
             inc=("t8", 1), wait=[("v8", P.cnt("v8"))])
        P.op("vector", lambda e, tc=tc: e.tensor_copy(
            iseT[:, tc * 128:(tc + 1) * 128], psTr[0:NE, 0:128]),
            inc=("v8", 1), wait=[("t8", P.cnt("t8"))])
    GTR = P.cnt("v8")
    GST = P.cnt("d8")

    for tc in range(NCH):
        wy = [("v8", GTR)]
        if tc > 0:
            wy.append(("v8", P.cnt("v8")))
        if tc >= NSC:
            wy.append(("yb2", 16 * (tc - NSC + 1)))
        for half in range(2):
            P.op("tensor", lambda e, tc=tc, half=half: e.matmul(
                psO2[:, half * 512:(half + 1) * 512],
                gatesT[:, tc * 128:(tc + 1) * 128],
                b2b[:, half * 512:(half + 1) * 512], start=True, stop=True),
                inc=("t8", 1) if half == 1 else None, wait=wy if half == 0 else None)
        P.op("vector", lambda e, tc=tc: e.tensor_copy(
            oacc[:, (tc % NSC) * D:(tc % NSC + 1) * D], psO2[:, :]),
            inc=("v8", 1), wait=[("t8", P.cnt("t8"))])
        P.op("sync", lambda e, tc=tc: e.dma_start(
            out=y_tm[tc * 128:(tc + 1) * 128, :],
            in_=oacc[:, (tc % NSC) * D:(tc % NSC + 1) * D]),
            inc=("yb2", 16), wait=[("v8", P.cnt("v8"))])
    YB2 = 16 * NCH

    P.op("vector", lambda e: e.engine_nop(), inc=("v8", 1), wait=[("v8", GTR)])
    cur, nxt = iseT, posTa
    k = 1
    while k < S:
        P.op("vector", lambda e, k=k, cur=cur, nxt=nxt: e.tensor_add(
            nxt[:, k:S], cur[:, k:S], cur[:, 0:S - k]),
            inc=("v8", 1), wait=[("v8", P.cnt("v8"))])
        P.op("vector", lambda e, k=k, cur=cur, nxt=nxt: e.tensor_copy(
            nxt[:, 0:k], cur[:, 0:k]),
            inc=("v8", 1), wait=[("v8", P.cnt("v8"))])
        cur, nxt = nxt, cur
        k *= 2
    posT = cur
    CS = P.cnt("v8")
    for tc in range(NCH):
        wpm = [("v8", CS), ("ld", LD)]
        if tc > 0:
            wpm.append(("v8", P.cnt("v8")))
        P.op("tensor", lambda e, tc=tc, posT=posT: e.matmul(
            psPM[:, :], posT[:, tc * 128:(tc + 1) * 128], idf8[:, :],
            start=True, stop=True), inc=("t8", 1), wait=wpm)
        P.op("vector", lambda e, tc=tc: e.tensor_copy(
            pos_tm[:, tc * NE:(tc + 1) * NE], psPM[:, :]),
            inc=("v8", 1), wait=[("t8", P.cnt("t8"))])
    PMV = P.cnt("v8")

    for tc in range(NCH):
        def ch(fn):
            P.op("vector", fn, inc=("v8", 1),
                 wait=[("v8", P.cnt("v8")), ("v8", PMV), ("d8", P.cnt("d8"))])
        ch(lambda e, tc=tc: e.tensor_mul(eq[:, :], is12[:, tc * 16: tc * 16 + 8],
                                         pos_tm[:, tc * NE:(tc + 1) * NE]))
        ch(lambda e: e.reduce_sum(offc[:, 0:1], eq[:, :], AX))
        ch(lambda e, tc=tc: e.tensor_mul(eq[:, :], is12[:, tc * 16: tc * 16 + 8],
                                         eidx[:, :]))
        ch(lambda e: e.reduce_sum(offc[:, 1:2], eq[:, :], AX))
        ch(lambda e, tc=tc: e.tensor_mul(eq[:, :], is12[:, tc * 16 + 8: tc * 16 + 16],
                                         pos_tm[:, tc * NE:(tc + 1) * NE]))
        ch(lambda e: e.reduce_sum(offc[:, 2:3], eq[:, :], AX))
        ch(lambda e, tc=tc: e.tensor_mul(eq[:, :], is12[:, tc * 16 + 8: tc * 16 + 16],
                                         eidx[:, :]))
        ch(lambda e: e.reduce_sum(offc[:, 3:4], eq[:, :], AX))
        ch(lambda e: e.scalar_tensor_tensor(offc[:, 1:2], offc[:, 1:2], float(CAP),
                                            offc[:, 0:1], ALU.mult, ALU.add))
        ch(lambda e: e.scalar_tensor_tensor(offc[:, 3:4], offc[:, 3:4], float(CAP),
                                            offc[:, 2:3], ALU.mult, ALU.add))
        ch(lambda e: e.tensor_scalar_add(offc[:, 1:2], offc[:, 1:2], -1.0))
        ch(lambda e: e.tensor_scalar_add(offc[:, 3:4], offc[:, 3:4], -1.0))
        ch(lambda e: e.tensor_copy(offi[:, 0:1], offc[:, 1:2]))
        ch(lambda e: e.tensor_copy(offi[:, 1:2], offc[:, 3:4]))
        OV = P.cnt("v8")
        for lane in range(2):
            P.op("gpsimd", lambda e, tc=tc, lane=lane: e.indirect_dma_start(
                idxscr[:, :], IOA(offi[:, lane:lane + 1], 0),
                tokid[:, tc:tc + 1], None),
                inc=("d8", 16), wait=[("v8", OV), ("pf", FST)])
    SCI = P.cnt("d8")

    scat_done = 0
    for e_ in range(NE):
        wix = [("d8", SCI), ("v8", P.cnt("v8"))]
        for c in range(NSC):
            P.op("sync", lambda e, e_=e_, c=c: e.dma_start(
                out=idxf[:, c:c + 1],
                in_=idxscr[e_ * CAP + c * 128: e_ * CAP + (c + 1) * 128, :]),
                inc=("d8", 16), wait=wix if c == 0 else None)
        wmn = [("d8", P.cnt("d8")), ("v8", P.cnt("v8")), ("sct", scat_done)]
        P.op("vector", lambda e: e.tensor_copy(idxs_[:, :], idxf[:, :]),
             inc=("v8", 1), wait=wmn)
        P.op("vector", lambda e: e.tensor_scalar_min(idxf[:, :], idxf[:, :], float(S - 1)),
             inc=("v8", 1), wait=[("v8", P.cnt("v8"))])
        P.op("vector", lambda e: e.tensor_scalar_max(idxf[:, :], idxf[:, :], 0.0),
             inc=("v8", 1), wait=[("v8", P.cnt("v8"))])
        P.op("vector", lambda e: e.tensor_copy(idxi[:, :], idxf[:, :]),
             inc=("v8", 1), wait=[("v8", P.cnt("v8"))])
        IXV = P.cnt("v8")
        wg_ = [("v8", IXV), ("t8", P.cnt("t8")), ("v8", P.cnt("v8"))]
        for c in range(NSC):
            P.op("gpsimd", lambda e, c=c: e.indirect_dma_start(
                xe_tm[:, c * D:(c + 1) * D], None, xn_tm_d[:, :],
                IOA(idxi[:, c:c + 1], 0)),
                inc=("d8", 16), wait=wg_ if c == 0 else None)
            P.op("gpsimd", lambda e, c=c: e.indirect_dma_start(
                gsl[:, c * NE:(c + 1) * NE], None, gates_d[:, :],
                IOA(idxi[:, c:c + 1], 0)),
                inc=("d8", 16),
                wait=[("d8", GST)] if (e_ == 0 and c == 0) else None)
        GXE = P.cnt("d8")
        for c in range(NSC):
            for dc in range(NDC):
                w = [("d8", GXE), ("v8", P.cnt("v8"))]
                if e_ > 0 and c == 0 and dc == 0:
                    w.append(("t8", P.cnt("t8")))
                P.op("tensor", lambda e, c=c, dc=dc: e.transpose(
                    psTP[:, :], xe_tm[:, c * D + dc * 128: c * D + (dc + 1) * 128],
                    idb[:, :]), inc=("t8", 1), wait=w)
                P.op("vector", lambda e, c=c, dc=dc: e.tensor_copy(
                    xe_fm[:, dc * CAP + c * 128: dc * CAP + (c + 1) * 128], psTP[:, :]),
                    inc=("v8", 1), wait=[("t8", P.cnt("t8"))])
        TPV = P.cnt("v8")
        for ih in range(2):
            wld = [("t8", P.cnt("t8"))] if (e_ + ih) > 0 else []
            for dc in range(NDC):
                P.op("gpsimd", lambda e, e_=e_, ih=ih, dc=dc: e.dma_start(
                    out=W1h[:, dc * IH:(dc + 1) * IH],
                    in_=W1[e_, dc * 128:(dc + 1) * 128, ih * IH:(ih + 1) * IH]),
                    inc=("d8", 16), wait=wld if dc == 0 else None)
            for ic in range(NIC):
                P.op("gpsimd", lambda e, e_=e_, ih=ih, ic=ic: e.dma_start(
                    out=W2h[:, ic * D:(ic + 1) * D],
                    in_=W2[e_, ih * IH + ic * 128: ih * IH + (ic + 1) * 128, :]),
                    inc=("d8", 16))
            WLD = P.cnt("d8")
            for ic in range(NIC):
                for sh in range(2):
                    pq = (ic * 2 + sh) % 2
                    w1_ = [("d8", WLD), ("v8", TPV)]
                    if P.cnt("s8") >= 1:
                        w1_.append(("s8", P.cnt("s8") - 1))
                    for dc in range(NDC):
                        P.op("tensor", lambda e, dc=dc, ic=ic, sh=sh, pq=pq: e.matmul(
                            psH[pq][:, :],
                            W1h[:, dc * IH + ic * 128: dc * IH + (ic + 1) * 128],
                            xe_fm[:, dc * CAP + sh * 512: dc * CAP + (sh + 1) * 512],
                            start=(dc == 0), stop=(dc == NDC - 1)),
                            inc=("t8", 1) if dc == NDC - 1 else None,
                            wait=w1_ if dc == 0 else None)
                    P.op("scalar", lambda e, ic=ic, pq=pq, e_=e_, ih=ih, sh=sh: e.activation(
                        hsb[:, ic * CAP + sh * 512: ic * CAP + (sh + 1) * 512],
                        psH[pq][:, :], AF.Gelu,
                        bias=b1all[:, e_ * 32 + ih * 16 + ic: e_ * 32 + ih * 16 + ic + 1]),
                        inc=("s8", 1), wait=[("t8", P.cnt("t8"))])
            HEV = P.cnt("s8")
            for sc in range(NSC):
                w2_ = [("s8", HEV), ("v8", P.cnt("v8"))]
                if e_ == 0 and ih == 0:
                    w2_.append(("yb2", YB2))
                for half in range(2):
                    for ic in range(NIC):
                        P.op("tensor", lambda e, ic=ic, half=half, sc=sc: e.matmul(
                            psO2[:, half * 512:(half + 1) * 512],
                            hsb[:, ic * CAP + sc * 128: ic * CAP + sc * 128 + 128],
                            W2h[:, ic * D + half * 512: ic * D + (half + 1) * 512],
                            start=(ic == 0), stop=(ic == NIC - 1)),
                            inc=("t8", 1) if ic == NIC - 1 else None,
                            wait=w2_ if (half == 0 and ic == 0) else None)
                if ih == 0:
                    P.op("vector", lambda e, sc=sc: e.tensor_copy(
                        oacc[:, sc * D:(sc + 1) * D], psO2[:, :]),
                        inc=("v8", 1), wait=[("t8", P.cnt("t8"))])
                else:
                    P.op("vector", lambda e, sc=sc: e.tensor_add(
                        oacc[:, sc * D:(sc + 1) * D], oacc[:, sc * D:(sc + 1) * D],
                        psO2[:, :]),
                        inc=("v8", 1), wait=[("t8", P.cnt("t8"))])
        OEV = P.cnt("v8")
        for sc in range(NSC):
            P.op("vector", lambda e, sc=sc, e_=e_: e.tensor_scalar(
                oscat[:, :], oacc[:, sc * D:(sc + 1) * D],
                gsl[:, sc * NE + e_: sc * NE + e_ + 1], None, ALU.mult),
                inc=("v8", 1),
                wait=[("v8", OEV), ("sct", P.cnt("sct")), ("v8", P.cnt("v8")),
                      ("d8", GXE)])
            wsc = [("v8", P.cnt("v8")), ("sct", scat_done)]
            if e_ == 0 and sc == 0:
                wsc.append(("yb2", YB2))
            P.op("gpsimd", lambda e, sc=sc: e.indirect_dma_start(
                y_tm[:, :], IOA(idxs_[:, sc:sc + 1], 0), oscat[:, :], None,
                compute_op=ALU.add),
                inc=("sct", 16), wait=wsc)
        scat_done = P.cnt("sct")

    P.wait("gpsimd", "sct", P.cnt("sct"))
    P.wait("sync", "sct", P.cnt("sct"))


L8S_DBG = False


def build_l8s():
    nc = bass.Bass("TRN2", target_bir_lowering=False, debug=False)
    dd = {}
    dd["xn1_fm"] = nc.dram_tensor("xn1_fm", [D, S], BF16, kind="ExternalInput").ap()
    dd["xn1_tm"] = nc.dram_tensor("xn1_tm", [S, D], BF16, kind="ExternalInput").ap()
    dd["WgT"] = nc.dram_tensor("WgT", [D, NE], F32, kind="ExternalInput").ap()
    dd["W1m"] = nc.dram_tensor("W1m", [NE, D, FF], F32, kind="ExternalInput").ap()
    dd["mb1r"] = nc.dram_tensor("mb1r", [NE, 128, FF // 128], F32, kind="ExternalInput").ap()
    dd["W2m"] = nc.dram_tensor("W2m", [NE, FF, D], F32, kind="ExternalInput").ap()
    dd["b2m"] = nc.dram_tensor("b2m", [NE, D], F32, kind="ExternalInput").ap()
    dd["revI"] = nc.dram_tensor("revI", [128, NE], F32, kind="ExternalInput").ap()
    dd["ident_bf"] = nc.dram_tensor("ident_bf", [128, 128], BF16, kind="ExternalInput").ap()
    dd["ident_f"] = nc.dram_tensor("ident_f", [128, 128], F32, kind="ExternalInput").ap()
    dd["tokid"] = nc.dram_tensor("tokid", [128, NCH], F32, kind="ExternalInput").ap()
    dd["eidx"] = nc.dram_tensor("eidx", [128, NE], F32, kind="ExternalInput").ap()
    dd["idxscr"] = nc.dram_tensor("idxscr", [NE * 1024, 1], F32, kind="Internal" if not L8S_DBG else "ExternalOutput").ap()
    dd["gates_scr"] = nc.dram_tensor("gates_scr", [S, NE], F32, kind="Internal" if not L8S_DBG else "ExternalOutput").ap()
    dd["y_tm"] = nc.dram_tensor("y_tm", [S, D], F32, kind="ExternalOutput").ap()
    with ExitStack() as ctx:
        P = Prog(nc, ctx)
        phase_l8s(P, dd)
        P.finalize()
    return nc



def phase_l9(P, dd):
    nc, ctx, tg = P.nc, P.ctx, P.tag
    xp = dd["xp_tm"]
    s1 = dd["s1_tm"]
    yy = dd["y_tm"]
    out = dd["out_tm"]
    sbf = lambda name, shape, dt: ctx.enter_context(nc.sbuf_tensor(tg + name, shape, dt))
    xb = [sbf(f"xb{i}", [128, D], F32) for i in range(2)]
    sb_ = [sbf(f"sb_{i}", [128, D], F32) for i in range(2)]
    yb = [sbf(f"yb{i}", [128, D], F32) for i in range(2)]
    ob = [sbf(f"ob{i}", [128, D], F32) for i in range(2)]
    for tc in range(NCH):
        p = tc % 2
        w = [("cmb", 2 * (tc - 1)), ("gy", tc - 1)] if tc >= 2 else None
        P.op("sync", lambda e, p=p, tc=tc: e.dma_start(
            out=xb[p][:, :], in_=xp[tc * 128:(tc + 1) * 128, :]), inc=("cld", 16), wait=w)
        P.op("sync", lambda e, p=p, tc=tc: e.dma_start(
            out=sb_[p][:, :], in_=s1[tc * 128:(tc + 1) * 128, :]), inc=("cld", 16))
        P.op("sync", lambda e, p=p, tc=tc: e.dma_start(
            out=yb[p][:, :], in_=yy[tc * 128:(tc + 1) * 128, :]), inc=("cld", 16))
        P.op("scalar", lambda e, p=p: e.activation(yb[p][:, :], yb[p][:, :], AF.Silu),
             inc=("gy", 1), wait=[("cld", 48 * (tc + 1))])
        P.op("vector", lambda e, p=p: e.tensor_mul(sb_[p][:, :], sb_[p][:, :], yb[p][:, :]),
             inc=("cmb", 1), wait=[("gy", P.cnt("gy"))])
        wadd = [("cmb", 2 * tc + 1)]
        if tc >= 2:
            wadd.append(("cst", 16 * (tc - 1)))
        P.op("vector", lambda e, p=p: e.tensor_add(ob[p][:, :], xb[p][:, :], sb_[p][:, :]),
             inc=("cmb", 1), wait=wadd)
        P.op("sync", lambda e, p=p, tc=tc: e.dma_start(
            out=out[tc * 128:(tc + 1) * 128, :], in_=ob[p][:, :]),
            inc=("cst", 16), wait=[("cmb", 2 * tc + 2)])
    P.wait("gpsimd", "cst", 16 * NCH)


def build_l9():
    nc = bass.Bass("TRN2", target_bir_lowering=False, debug=False)
    dd = {}
    dd["xp_tm"] = nc.dram_tensor("xp_tm", [S, D], F32, kind="ExternalInput").ap()
    dd["s1_tm"] = nc.dram_tensor("s1_tm", [S, D], F32, kind="ExternalInput").ap()
    dd["y_tm"] = nc.dram_tensor("y_tm", [S, D], F32, kind="ExternalInput").ap()
    dd["out_tm"] = nc.dram_tensor("out_tm", [S, D], F32, kind="ExternalOutput").ap()
    with ExitStack() as ctx:
        P = Prog(nc, ctx)
        phase_l9(P, dd)
        P.finalize()
    return nc



FUSE_SUBSET = None
FUSE_DUMMY_SEMS = 0
FUSE_DBG = False


def build_fused(subset=None, kinds=None):
    nc = bass.Bass("TRN2", target_bir_lowering=False, debug=False)
    dd = {}
    EI, EO, IN = "ExternalInput", "ExternalOutput", "Internal"
    decls = [
        ("x", [S, D], F32, EI), ("ident_bf", [128, 128], BF16, EI),
        ("ident_f", [128, 128], F32, EI), ("thKT", [D, D], F32, EI),
        ("thQT", [D, D], F32, EI), ("thVT", [D, D], F32, EI),
        ("W0T", [D, D], F32, EI), ("b0row", [1, D], F32, EI),
        ("dmask", [7, 128, 128], F32, EI), ("dmaskT", [7, 128, 128], F32, EI),
        ("triu_incl", [128, 128], F32, EI), ("W1T", [D, FF], F32, EI),
        ("W2T", [FF, D], F32, EI), ("b1r", [128, FF // 128], F32, EI),
        ("b2row", [1, D], F32, EI), ("qw", [D, D], F32, EI),
        ("kw", [D, D], F32, EI), ("vw", [D, D], F32, EI), ("ow", [D, D], F32, EI),
        ("iota_row", [1, S], F32, EI), ("biascols", [128, NH * NCH], F32, EI),
        ("madd_bf", [128, 128], BF16, EI), ("WgT", [D, NE], F32, EI),
        ("W1m", [NE, D, FF], F32, EI), ("mb1r", [NE, 128, FF // 128], F32, EI),
        ("W2m", [NE, FF, D], F32, EI), ("b2m", [NE, D], F32, EI),
        ("revI", [128, NE], F32, EI),
        ("xn_fm", [D, S], BF16, IN), ("k_fm", [D, S], BF16, IN),
        ("q_fm", [D, S], BF16, IN), ("ve_tm", [S, D], BF16, IN),
        ("s_tm", [S, D], F32, IN), ("g_tm", [S, D], F32, IN),
        ("xp_tm", [S, D], F32, IN), ("xn1_fm", [D, S], BF16, IN),
        ("k1_fm", [D, S], BF16, IN), ("q1_fm", [D, S], BF16, IN),
        ("v1_tm", [S, D], BF16, IN), ("crow_scr", [NH, S], F32, IN),
        ("s1_tm", [S, D], F32, "ExternalOutput" if FUSE_DBG else IN),
        ("y_tm", [S, D], F32, "ExternalOutput" if FUSE_DBG else IN),
        ("xn1_tm", [S, D], BF16, "ExternalOutput" if FUSE_DBG else IN),
        ("idxscr", [NE * 1024, 1], F32, IN),
        ("gates_scr", [S, NE], F32, IN),
        ("tokid", [128, NCH], F32, EI), ("eidx", [128, NE], F32, EI),
        ("out_tm", [S, D], F32, EO),
    ]
    for nm, shp, dt_, kind in decls:
        if kinds and nm in kinds:
            kind = kinds[nm]
        dd[nm] = nc.dram_tensor(nm, shp, dt_, kind=kind).ap()
    phases = [("p1:", phase_l1), ("p2:", phase_l2), ("p3:", phase_l3),
              ("p4:", phase_l4), ("p5:", phase_l5), ("p6:", phase_l6),
              ("p7:", phase_l7), ("p8:", phase_l8s), ("p9:", phase_l9)]
    sub = subset if subset is not None else FUSE_SUBSET
    if sub is not None:
        phases = [p for p in phases if p[0] in sub]
    with ExitStack() as topctx:
        P = Prog(nc, topctx)
        for _d in range(FUSE_DUMMY_SEMS):
            P.sem(f"dummy{_d}")
        for tag, fn in phases:
            with ExitStack() as pctx:
                fn(NSProg(P, tag, pctx), dd)
            P.barrier()
        P.finalize()
    return nc



def _run(nc, in_maps):
    res = run_bass_kernel_spmd(nc, in_maps, core_ids=list(range(8)))
    return res.results


def _prep(inp):
    w = {}
    w["x"] = [np.ascontiguousarray(np.asarray(inp["L0_attn_in_seq"][i], dtype=np.float32))
              for i in range(8)]
    w["thKT"] = np.ascontiguousarray(np.asarray(inp["L0_attn_theta_K"]).T)
    w["thQT"] = np.ascontiguousarray(np.asarray(inp["L0_attn_theta_Q"]).T)
    w["thVT"] = np.ascontiguousarray(np.asarray(inp["L0_attn_theta_V"]).T)
    w["W0T"] = np.ascontiguousarray(np.asarray(inp["L0_attn_W0"]).T)
    w["b0row"] = np.ascontiguousarray(np.asarray(inp["L0_attn_b0"]).reshape(1, D))
    w["W1T"] = np.ascontiguousarray(np.asarray(inp["L0_ffn_W1"]).T)
    w["W2T"] = np.ascontiguousarray(np.asarray(inp["L0_ffn_W2"]).T)
    w["b1r"] = np.ascontiguousarray(np.asarray(inp["L0_ffn_b1"]).reshape(FF // 128, 128).T)
    w["b2row"] = np.ascontiguousarray(np.asarray(inp["L0_ffn_b2"]).reshape(1, D))
    w["qw"] = np.ascontiguousarray(np.asarray(inp["L1_attn_q_w"]))
    w["kw"] = np.ascontiguousarray(np.asarray(inp["L1_attn_k_w"]))
    w["vw"] = np.ascontiguousarray(np.asarray(inp["L1_attn_v_w"]))
    w["ow"] = np.ascontiguousarray(np.asarray(inp["L1_attn_o_w"]))
    w["WgT"] = np.ascontiguousarray(np.asarray(inp["L1_ffn_Wg"]).T)
    w["mW1"] = np.ascontiguousarray(np.asarray(inp["L1_ffn_W1"]))
    w["mb1r"] = np.ascontiguousarray(
        np.asarray(inp["L1_ffn_b1"]).reshape(NE, FF // 128, 128).transpose(0, 2, 1))
    w["mW2"] = np.ascontiguousarray(np.asarray(inp["L1_ffn_W2"]))
    w["mb2"] = np.ascontiguousarray(np.asarray(inp["L1_ffn_b2"]))
    return w


KERNEL_SPLIT = True


def kernel(**inp):
    if KERNEL_SPLIT:
        return kernel_split(**inp)
    w = _prep(inp)
    c = host_consts()
    nc = build_fused()
    in_maps = []
    for i in range(8):
        in_maps.append({
            "x": w["x"][i], "ident_bf": c["ident_bf"], "ident_f": c["ident_f"],
            "thKT": w["thKT"], "thQT": w["thQT"], "thVT": w["thVT"],
            "W0T": w["W0T"], "b0row": w["b0row"], "dmask": c["dmask"],
            "dmaskT": c["dmaskT"], "triu_incl": c["triu_incl"], "W1T": w["W1T"],
            "W2T": w["W2T"], "b1r": w["b1r"], "b2row": w["b2row"], "qw": w["qw"],
            "kw": w["kw"], "vw": w["vw"], "ow": w["ow"], "iota_row": c["iota_row"],
            "biascols": c["biascols"], "madd_bf": c["madd_bf"], "WgT": w["WgT"],
            "W1m": w["mW1"], "mb1r": w["mb1r"], "W2m": w["mW2"], "b2m": w["mb2"],
            "revI": c["revI"], "tokid": c["tokid"], "eidx": c["eidx"],
        })
    r = _run(nc, in_maps)
    return np.stack([r[i]["out_tm"] for i in range(8)], axis=0)


def _fused_in_maps(w, c):
    in_maps = []
    for i in range(8):
        in_maps.append({
            "x": w["x"][i], "ident_bf": c["ident_bf"], "ident_f": c["ident_f"],
            "thKT": w["thKT"], "thQT": w["thQT"], "thVT": w["thVT"],
            "W0T": w["W0T"], "b0row": w["b0row"], "dmask": c["dmask"],
            "dmaskT": c["dmaskT"], "triu_incl": c["triu_incl"], "W1T": w["W1T"],
            "W2T": w["W2T"], "b1r": w["b1r"], "b2row": w["b2row"], "qw": w["qw"],
            "kw": w["kw"], "vw": w["vw"], "ow": w["ow"], "iota_row": c["iota_row"],
            "biascols": c["biascols"], "madd_bf": c["madd_bf"], "WgT": w["WgT"],
            "W1m": w["mW1"], "mb1r": w["mb1r"], "W2m": w["mW2"], "b2m": w["mb2"],
            "revI": c["revI"], "tokid": c["tokid"], "eidx": c["eidx"],
        })
    return in_maps


def kernel_split(**inp):
    w = _prep(inp)
    c = host_consts()
    cross = ("xp_tm", "s1_tm", "xn1_fm", "xn1_tm")
    ncA = build_fused(subset={"p1:", "p2:", "p3:", "p4:", "p5:", "p6:", "p7:"},
                      kinds={t: "ExternalOutput" for t in cross})
    base = _fused_in_maps(w, c)
    rA = _run(ncA, base)
    ncB = build_fused(subset={"p8:", "p9:"},
                      kinds={t: "ExternalInput" for t in cross})
    mapsB = []
    for i in range(8):
        m = dict(base[i])
        for t in cross:
            m[t] = rA[i][t]
        mapsB.append(m)
    rB = _run(ncB, mapsB)
    return np.stack([rB[i]["out_tm"] for i in range(8)], axis=0)
